# version 20
# baseline (speedup 1.0000x reference)
import contextlib

import numpy as np
import ml_dtypes

import concourse.bass as bass
import concourse.mybir as mybir
from concourse.bass_utils import run_bass_kernel_spmd

F32 = mybir.dt.float32
BF16 = mybir.dt.bfloat16
I32 = mybir.dt.int32
AF = mybir.ActivationFunctionType
ALU = mybir.AluOpType
AX = mybir.AxisListType
BF16NP = ml_dtypes.bfloat16

NCORES = 8
T = 8192
D = 2048
TPC = T // NCORES
NTT = TPC // 128
DI = 4096
DIPC = DI // NCORES
NDB = DIPC // 128
NST = 16
RK = 128
EPS = 1e-5
SCALE = float(1.0 / np.sqrt(128.0))
QKV_SC = 1024
IP_SC = 1024
NSC = T // QKV_SC
SCAN_TC = 256
N_TC = T // SCAN_TC

ENGS = ("sy", "pe", "v", "a", "g")
BKNAME = {"sy": "sync", "pe": "tensor", "v": "vector", "a": "scalar",
          "g": "gpsimd"}
CORE_IDS = list(range(NCORES))


class Op:
    __slots__ = ("eng", "fn", "deps", "kind", "ndma", "needed", "ms", "waits",
                 "label", "dma_sem", "dma_val")

    def __init__(self, eng, fn, deps, kind, label, ndma=1):
        self.eng = eng
        self.fn = fn
        self.deps = deps
        self.kind = kind
        self.ndma = ndma
        self.needed = False
        self.ms = None
        self.waits = None
        self.label = label


N_DMASEM = 64


class G:

    def __init__(self):
        self.ops = []
        self.reg = {}
        self.last = {e: None for e in ENGS}
        self.dma_idx = 0
        self.dma_prev = [None] * N_DMASEM
        self.dma_cnt = [0] * N_DMASEM
        self.cur_dma_sem = None

    def op(self, eng, fn, reads=(), writes=(), kind="c", label="", ndma=1):
        deps = set()
        for k in reads:
            st = self.reg.get(k)
            if st is not None and st[0] is not None:
                deps.add(st[0])
        for k in writes:
            st = self.reg.get(k)
            if st is not None:
                deps.update(st[1])
                if st[0] is not None:
                    deps.add(st[0])
        o = Op(eng, fn, deps, kind, label, ndma)
        if kind == "d":
            si = self.dma_idx % N_DMASEM
            self.dma_idx += 1
            o.dma_sem = si
            if self.dma_prev[si] is not None:
                deps.add(self.dma_prev[si])
            self.dma_prev[si] = o
            self.dma_cnt[si] += 16 * ndma
            o.dma_val = self.dma_cnt[si]
        self.ops.append(o)
        for k in reads:
            self.reg.setdefault(k, [None, []])[1].append(o)
        for k in writes:
            self.reg[k] = [o, []]
        self.last[eng] = o
        return o

    def dma(self, eng, fn, reads=(), writes=(), label=""):
        return self.op(eng, fn, reads, writes, kind="d", label=label)

    def dmam(self, eng, fn, n, reads=(), writes=(), label=""):
        return self.op(eng, fn, reads, writes, kind="d", label=label, ndma=n)

    def cc(self, fn, reads=(), writes=(), label=""):
        return self.op("g", fn, reads, writes, kind="cc", label=label)

    def fence(self):
        lasts = set(o for o in self.last.values() if o is not None)
        for e in ENGS:
            o = Op(e, (lambda eng: eng.nop()), set(lasts), "c", "fence")
            self.ops.append(o)
            self.last[e] = o
        self.reg.clear()

    def plan(self):
        for o in self.ops:
            for d in o.deps:
                d.needed = True
        cnt = {}
        for o in self.ops:
            if o.kind == "d":
                o.ms = (("ds", o.dma_sem), o.dma_val)
            elif o.kind == "cc":
                cnt["cc"] = cnt.get("cc", 0) + 1
                o.ms = ("cc", cnt["cc"])
            elif o.needed:
                key = ("c", o.eng)
                cnt[key] = cnt.get(key, 0) + 1
                o.ms = (key, cnt[key])
        waited = {e: {} for e in ENGS}
        for o in self.ops:
            w = {}
            for d in o.deps:
                if d.ms is None:
                    continue
                k, v = d.ms
                if w.get(k, 0) < v:
                    w[k] = v
            W = waited[o.eng]
            o.waits = []
            for k, v in sorted(w.items(), key=lambda kv: str(kv[0])):
                if W.get(k, 0) < v:
                    o.waits.append((k, v))
                    W[k] = v

    def emit(self, nc, semmap):
        with nc.Block() as bk:
            def mk(ename):
                def body(eng):
                    for o in self.ops:
                        if o.eng != ename:
                            continue
                        for k, v in o.waits:
                            eng.wait_ge(semmap[k], v)
                        if o.kind == "d":
                            self.cur_dma_sem = semmap[("ds", o.dma_sem)]
                        inst = o.fn(eng)
                        if o.kind == "d":
                            inst.then_inc(semmap[("ds", o.dma_sem)], 16)
                        elif o.kind == "cc":
                            inst.then_inc(semmap["cc"], 1)
                        elif o.needed:
                            inst.then_inc(semmap[("c", ename)], 1)
                return body
            for ename in ENGS:
                getattr(bk, BKNAME[ename])(mk(ename))


def build_program(debug=False):
    nc = bass.Bass("TRN2", target_bir_lowering=False, num_devices=NCORES)
    g = G()

    def inp(name, shape, dt=F32):
        return nc.dram_tensor(name, shape, dt, kind="ExternalInput")

    x_rows = inp("x_rows", [TPC, D])
    cid_in = inp("cid_in", [1, 1], I32)
    w_rot = inp("w_rot", [D, 128])
    b_rot = inp("b_rot", [128, 1])
    w_pass = inp("w_pass", [D, 384])
    b_pass = inp("b_pass", [96, 4])
    w_v = inp("w_v", [D, 256])
    b_v = inp("b_v", [128, 2])
    perm_all = inp("perm_all", [128, 128], BF16)
    cosg = inp("cosg", [128, T])
    sing = inp("sing", [128, T])
    wbT = inp("wbT", [D, D])
    bbase_rep = inp("bbase_rep", [128, D])
    routerT = inp("routerT", [D, 8])
    aT = inp("aT", [D, 128])
    bwT = inp("bwT", [128, D])
    inwT_sl = inp("inwT_sl", [D, 1024])
    inb_sl = inp("inb_sl", [128, 8])
    convw_sl = inp("convw_sl", [128, 16])
    convb_sl = inp("convb_sl", [128, 4])
    xprojT_sl = inp("xprojT_sl", [DIPC, 160])
    dtwT_sl = inp("dtwT_sl", [RK, DIPC])
    dtb_sl = inp("dtb_sl", [128, 4])
    alog_sl = inp("alog_sl", [128, 4 * NST])
    d_sl = inp("d_sl", [128, 4])
    outwT_sl = inp("outwT_sl", [DIPC, D])
    outb_rep = inp("outb_rep", [128, D])
    idb = inp("idb", [128, 128], BF16)
    idf = inp("idf", [128, 128])
    trm = inp("trm", [128, 128], BF16)
    ones1 = inp("ones1", [1, 128])

    out_rows = nc.dram_tensor("out_rows", [TPC, D], F32, kind="ExternalOutput")
    dmp = {}
    if debug:
        for nm, shape, dt in (("d_xnf", [D, TPC], BF16),
                              ("d_ao", [T, 256], BF16),
                              ("d_x3", [TPC, D], F32),
                              ("d_sp", [160, T], F32),
                              ("d_hs", [DIPC, T], BF16),
                              ("d_sg", [DIPC, T], BF16),
                              ("d_rs", [TPC, D], F32),
                              ("d_x2", [TPC, D], F32),
                              ("d_fb", [128, D], F32),
                              ("d_lg", [128, 8], F32),
                              ("d_hm", [128, 128], F32),
                              ("d_xn2", [128, D], BF16),
                              ("d_xnT", [128, 16 * 128], BF16)):
            dmp[nm] = nc.dram_tensor(nm, shape, dt, kind="ExternalOutput")

    def dram(name, shape, dt=F32):
        return nc.dram_tensor(name, shape, dt, kind="Internal")

    xn_c = dram("xn_c", [D, TPC], BF16)
    xn_full = dram("xn_full", [NCORES * D, TPC], BF16)
    ao_c = dram("ao_c", [T, 256], BF16)
    ao_full = dram("ao_full", [NCORES * T, 256], BF16)
    sp_c = dram("sp_c", [160, T])
    sp_full = dram("sp_full", [160, T])
    hsb_d = dram("hsb_d", [DIPC, T], BF16)
    sgb_d = dram("sgb_d", [DIPC, T], BF16)
    x3_d = dram("x3_d", [TPC, D])
    rs_in = dram("rs_in", [T, D])
    rs_out = dram("rs_out", [TPC, D])

    stk = contextlib.ExitStack()
    with stk:
        def sb(name, shape, dt=F32, st=None):
            return (st or stk).enter_context(nc.sbuf_tensor(name, shape, dt))

        psA = [stk.enter_context(nc.psum_tensor(f"psA{i}", [128, 512], F32))
               for i in range(4)]
        psB = [stk.enter_context(nc.psum_tensor(f"psB{i}", [128, 512], F32))
               for i in range(2)]
        psM = stk.enter_context(nc.psum_tensor("psM", [128, 512], F32))
        psT = stk.enter_context(nc.psum_tensor("psT", [128, 1024], BF16))

        c_idb = sb("c_idb", [128, 128], BF16)
        c_idf = sb("c_idf", [128, 128])
        c_trm = sb("c_trm", [128, 128], BF16)
        c_ones = sb("c_ones", [1, 128])
        c_onesb = sb("c_onesb", [1, 128], BF16)
        c_cid = sb("c_cid", [1, 1], I32)
        lnsq = sb("lnsq", [128, D])
        lnst = sb("lnst", [128, 16])

        semmap = {}
        for e in ENGS:
            semmap[("c", e)] = stk.enter_context(nc.semaphore(f"sC{e}"))
        for i in range(N_DMASEM):
            semmap[("ds", i)] = stk.enter_context(nc.semaphore(f"sDp{i}"))
        semmap["cc"] = stk.enter_context(nc.semaphore("sCC"))

        g.dma("g", lambda e: e.dma_start(out=c_onesb[:, :], in_=ones1.ap()),
              writes=[("const", "onesb")], label="ldc_onesb")
        for t_, s_, n_ in ((idb, c_idb, "idb"), (idf, c_idf, "idf"),
                           (trm, c_trm, "trm"),
                           (ones1, c_ones, "ones"), (cid_in, c_cid, "cid")):
            g.dma("sy", lambda e, t=t_, s=s_: e.dma_start(out=s[:, :],
                                                          in_=t.ap()),
                  writes=[("const", n_)], label=f"ldc_{n_}")

        def emit_ln(xap, rkey, out_ap, okey, tag):
            g.op("a", lambda e: e.activation(lnsq[:, :], xap, AF.Square),
                 reads=[rkey], writes=[("lnsq",)], label=f"lnsq{tag}")
            g.op("v", lambda e: e.tensor_reduce(
                lnst[:, 0:1], xap, AX.X, ALU.add, negate=True),
                reads=[rkey], writes=[("lt0",)], label=f"lns0{tag}")
            g.op("v", lambda e: e.tensor_reduce(
                lnst[:, 1:2], lnsq[:, :], AX.X, ALU.add),
                reads=[("lnsq",)], writes=[("lt1",)], label=f"lns1{tag}")
            g.op("v", lambda e: e.tensor_scalar(
                lnst[:, 2:3], lnst[:, 0:1], 1.0 / D, None, ALU.mult),
                reads=[("lt0",)], writes=[("lt2",)], label=f"lnm{tag}")
            g.op("v", lambda e: e.tensor_scalar(
                lnst[:, 3:4], lnst[:, 1:2], 1.0 / D, EPS, ALU.mult, ALU.add),
                reads=[("lt1",)], writes=[("lt3",)], label=f"lne{tag}")
            g.op("v", lambda e: e.tensor_tensor(
                lnst[:, 4:5], lnst[:, 2:3], lnst[:, 2:3], ALU.mult),
                reads=[("lt2",)], writes=[("lt4",)], label=f"lnm2{tag}")
            g.op("v", lambda e: e.tensor_tensor(
                lnst[:, 5:6], lnst[:, 3:4], lnst[:, 4:5], ALU.subtract),
                reads=[("lt3",), ("lt4",)], writes=[("lt5",)],
                label=f"lnv{tag}")
            g.op("a", lambda e: e.activation(lnst[:, 6:7], lnst[:, 5:6],
                                             AF.Ln),
                 reads=[("lt5",)], writes=[("lt6",)], label=f"lnl{tag}")
            g.op("a", lambda e: e.activation(lnst[:, 7:8], lnst[:, 6:7],
                                             AF.Exp, scale=-0.5),
                 reads=[("lt6",)], writes=[("lt7",)], label=f"lnr{tag}")
            g.op("v", lambda e: e.tensor_tensor(
                lnst[:, 8:9], lnst[:, 2:3], lnst[:, 7:8], ALU.mult),
                reads=[("lt2",), ("lt7",)], writes=[("lt8",)],
                label=f"lnb{tag}")
            return g.op("a", lambda e: e.activation(
                out_ap, xap, AF.Identity, bias=lnst[:, 8:9], scale=lnst[:, 7:8]),
                reads=[rkey, ("lt7",), ("lt8",)], writes=[okey],
                label=f"lnx{tag}")

        def emit_xnT_phase(rowbuf, rowtag, tag, xn_t, xnt_t):
            for tt in range(NTT):
                rb = rowbuf[:, tt * D:(tt + 1) * D]
                emit_ln(rb, ("row", rowtag, tt), xn_t[:, :], ("xnt_x", tag),
                        f"{tag}_{tt}")
                for half in range(2):
                    def tp(e, half=half):
                        last = None
                        for i in range(8):
                            et = half * 8 + i
                            last = e.transpose(
                                psT[:, i * 128:(i + 1) * 128],
                                xn_t[:, et * 128:(et + 1) * 128], c_idb[:, :])
                        return last
                    g.op("pe", tp, reads=[("xnt_x", tag), ("const", "idb")],
                         writes=[("psT",)], label=f"xtp{tag}{tt}{half}")

                    def ev(e, tt=tt, half=half):
                        return e.tensor_copy(
                            xnt_t[:, :].rearrange("p (et t) -> p et t", et=16)
                            [:, half * 8:(half + 1) * 8,
                             tt * 128:(tt + 1) * 128],
                            psT[:, :].rearrange("p (i t) -> p i t", i=8))
                    g.op("v", ev, reads=[("psT",)],
                         writes=[("xnt_o", tag, tt, half)],
                         label=f"xev{tag}{tt}{half}")
            g.dma("sy", lambda e: e.dma_start(
                out=xn_c.ap().rearrange("(et p) t -> p et t", p=128),
                in_=xnt_t[:, :].rearrange("p (et t) -> p et t", et=16)),
                reads=[("xnt_o", tag, tt, h) for tt in range(NTT)
                       for h in range(2)],
                writes=[("dr", "xn_c")], label=f"xst{tag}")
            g.cc(lambda e: e.collective_compute(
                "AllGather", ALU.bypass, replica_groups=[CORE_IDS],
                ins=[xn_c.ap()], outs=[xn_full.ap()]),
                reads=[("dr", "xn_c")], writes=[("dr", "xn_full")],
                label=f"ag{tag}")

        st0 = contextlib.ExitStack()
        xrow0 = sb("xrow0", [128, NTT * D], st=st0)
        xnsc0 = sb("xnsc0", [128, D], BF16, st=st0)
        xnts0 = sb("xnts0", [128, NTT * D], BF16, st=st0)
        g.dma("sy", lambda e: e.dma_start(
            out=xrow0[:, :].rearrange("p (t d) -> p t d", t=NTT),
            in_=x_rows.ap().rearrange("(t p) d -> p t d", p=128)),
            writes=[("row", "x1", tt) for tt in range(NTT)], label="ldx")
        emit_xnT_phase(xrow0, "x1", "x1", xnsc0, xnts0)
        if debug:
            g.dma("sy", lambda e: e.dma_start(
                out=dmp["d_xnf"].ap(), in_=xn_full.ap()[0:D, :]),
                reads=[("dr", "xn_full")], label="dump_xnf")
        st0.close()
        g.fence()

        stA = contextlib.ExitStack()
        units = [sb(f"unit{u}", [128, T], BF16, st=stA) for u in range(4)]
        vtm = sb("vtm", [128, 64 * 256], BF16, st=stA)
        vtmv = vtm[:, :].rearrange("p (tt h d) -> p tt h d", tt=64, h=2)

        stP1 = contextlib.ExitStack()
        wrot_sb = sb("wrot_sb", [128, 16 * 128], BF16, st=stP1)
        wpass_sb = sb("wpass_sb", [128, 16 * 384], BF16, st=stP1)
        wv_sb = sb("wv_sb", [128, 16 * 256], BF16, st=stP1)
        brot_sb = sb("brot_sb", [128, 1], st=stP1)
        bpass_sb = sb("bpass_sb", [96, 4], st=stP1)
        bv_sb = sb("bv_sb", [128, 2], st=stP1)
        perm_sb = sb("perm_sb", [128, 128], BF16, st=stP1)
        xnq = sb("xnq", [128, 16 * QKV_SC], BF16, st=stP1)
        rotg = sb("rotg", [128, QKV_SC], st=stP1)
        r12b = sb("r12b", [128, QKV_SC], BF16, st=stP1)
        rs1 = sb("rs1", [64, QKV_SC], st=stP1)
        rs2 = sb("rs2", [64, QKV_SC], st=stP1)
        csc = sb("csc", [128, QKV_SC], st=stP1)
        css = sb("css", [128, QKV_SC], st=stP1)
        vfm = sb("vfm", [128, QKV_SC], BF16, st=stP1)

        for t_, s_, n_ in ((w_rot, wrot_sb, 128), (w_pass, wpass_sb, 384),
                           (w_v, wv_sb, 256)):
            g.dma("g", lambda e, t=t_, s=s_, n=n_: e.dma_start(
                out=s[:, :].rearrange("p (et o) -> p et o", et=16),
                in_=t.ap().rearrange("(et p) o -> p et o", p=128)),
                writes=[("w", t_.name)], label=f"ldw_{t_.name}")
        g.dma("g", lambda e: e.dma_start(out=perm_sb[:, :],
                                         in_=perm_all.ap()),
              writes=[("w", "perm")], label="ldperm")
        for t_, s_ in ((b_rot, brot_sb), (b_pass, bpass_sb), (b_v, bv_sb)):
            g.dma("sy", lambda e, t=t_, s=s_: e.dma_start(out=s[:, :],
                                                          in_=t.ap()),
                  writes=[("w", t_.name)], label=f"ldb_{t_.name}")

        for sc in range(NSC):
            w0 = sc * QKV_SC
            g.dma("sy", lambda e, sc=sc: e.dma_start(
                out=xnq[:, :].rearrange("p (et t) -> p et t", et=16),
                in_=xn_full.ap().rearrange("(r et p) t -> r p et t",
                                           p=128, et=16)[sc]),
                reads=[("dr", "xn_full")], writes=[("xnq",)],
                label=f"ldxnq{sc}")
            g.dma("sy", lambda e, w0=w0: e.dma_start(
                out=csc[:, :], in_=cosg.ap()[:, w0:w0 + QKV_SC]),
                writes=[("csc",)], label=f"ldcos{sc}")
            g.dma("sy", lambda e, w0=w0: e.dma_start(
                out=css[:, :], in_=sing.ap()[:, w0:w0 + QKV_SC]),
                writes=[("css",)], label=f"ldsin{sc}")
            for fw in range(2):
                def mm(e, fw=fw):
                    last = None
                    for et in range(16):
                        last = e.matmul(
                            psA[fw][:, :],
                            wrot_sb[:, et * 128:(et + 1) * 128],
                            xnq[:, et * QKV_SC + fw * 512:
                                et * QKV_SC + fw * 512 + 512],
                            start=(et == 0), stop=(et == 15))
                    return last
                g.op("pe", mm, reads=[("xnq",), ("w", "w_rot")],
                     writes=[("psA", fw)], label=f"rmm{sc}{fw}")
                g.op("a", lambda e, fw=fw: e.activation(
                    rotg[:, fw * 512:fw * 512 + 512], psA[fw][:, :],
                    AF.Identity, bias=brot_sb[:, 0:1]),
                    reads=[("psA", fw), ("w", "b_rot")], writes=[("rotg",)],
                    label=f"rev{sc}{fw}")
            g.op("v", lambda e: e.tensor_tensor(
                rs1[:, :], rotg[0:64, :], csc[0:64, :], ALU.mult),
                reads=[("rotg",), ("csc",)], writes=[("rs1",)],
                label=f"rp1_{sc}")
            g.op("v", lambda e: e.tensor_tensor(
                rs2[:, :], rotg[64:128, :], css[64:128, :], ALU.mult),
                reads=[("rotg",), ("css",)], writes=[("rs2",)],
                label=f"rp2_{sc}")
            g.op("v", lambda e: e.tensor_tensor(
                r12b[0:64, :], rs1[:, :], rs2[:, :], ALU.subtract),
                reads=[("rs1",), ("rs2",)], writes=[("r12a",)],
                label=f"rp3_{sc}")
            g.op("v", lambda e: e.tensor_tensor(
                rs1[:, :], rotg[0:64, :], css[0:64, :], ALU.mult),
                reads=[("rotg",), ("css",)], writes=[("rs1",)],
                label=f"rp4_{sc}")
            g.op("v", lambda e: e.tensor_tensor(
                rs2[:, :], rotg[64:128, :], csc[64:128, :], ALU.mult),
                reads=[("rotg",), ("csc",)], writes=[("rs2",)],
                label=f"rp5_{sc}")
            g.op("v", lambda e: e.tensor_tensor(
                r12b[64:128, :], rs1[:, :], rs2[:, :], ALU.add),
                reads=[("rs1",), ("rs2",)], writes=[("r12b",)],
                label=f"rp6_{sc}")
            for u in range(4):
                for fw in range(2):
                    def pmm(e, u=u, fw=fw):
                        return e.matmul(
                            psM[0:32, :],
                            perm_sb[:, u * 32:(u + 1) * 32],
                            r12b[:, fw * 512:fw * 512 + 512],
                            start=True, stop=True)
                    g.op("pe", pmm,
                         reads=[("r12a",), ("r12b",), ("w", "perm")],
                         writes=[("psM",)], label=f"pmm{sc}{u}{fw}")

                    def pev(e, u=u, fw=fw, w0=w0):
                        return e.tensor_copy(
                            units[u][96:128,
                                     w0 + fw * 512:w0 + fw * 512 + 512],
                            psM[0:32, :])
                    g.op("v", pev, reads=[("psM",)], writes=[("un", u, sc)],
                         label=f"pev{sc}{u}{fw}")
            for u in range(4):
                for fw in range(2):
                    def mm(e, u=u, fw=fw):
                        last = None
                        for et in range(16):
                            last = e.matmul(
                                psA[2 + fw][0:96, :],
                                wpass_sb[:, et * 384 + u * 96:
                                         et * 384 + (u + 1) * 96],
                                xnq[:, et * QKV_SC + fw * 512:
                                    et * QKV_SC + fw * 512 + 512],
                                start=(et == 0), stop=(et == 15))
                        return last
                    g.op("pe", mm, reads=[("xnq",), ("w", "w_pass")],
                         writes=[("psA", 2 + fw)], label=f"qmm{sc}{u}{fw}")

                    def qev(e, u=u, fw=fw, w0=w0):
                        return e.activation(
                            units[u][0:96, w0 + fw * 512:w0 + fw * 512 + 512],
                            psA[2 + fw][0:96, :], AF.Identity,
                            bias=bpass_sb[:, u:u + 1],
                            scale=SCALE if u in (0, 2) else 1.0)
                    g.op("a", qev, reads=[("psA", 2 + fw), ("w", "b_pass")],
                         writes=[("un", u, sc)], label=f"qev{sc}{u}{fw}")
            for vh in range(2):
                for fw in range(2):
                    def mm(e, vh=vh, fw=fw):
                        last = None
                        for et in range(16):
                            last = e.matmul(
                                psB[fw][:, :],
                                wv_sb[:, et * 256 + vh * 128:
                                      et * 256 + (vh + 1) * 128],
                                xnq[:, et * QKV_SC + fw * 512:
                                    et * QKV_SC + fw * 512 + 512],
                                start=(et == 0), stop=(et == 15))
                        return last
                    g.op("pe", mm, reads=[("xnq",), ("w", "w_v")],
                         writes=[("psB", fw)], label=f"vmm{sc}{vh}{fw}")

                    def vev(e, vh=vh, fw=fw):
                        return e.activation(
                            vfm[:, fw * 512:fw * 512 + 512], psB[fw][:, :],
                            AF.Identity, bias=bv_sb[:, vh:vh + 1])
                    g.op("a", vev, reads=[("psB", fw), ("w", "b_v")],
                         writes=[("vfm",)], label=f"vev{sc}{vh}{fw}")

                def vtp(e):
                    last = None
                    for i in range(8):
                        last = e.transpose(
                            psT[:, i * 128:(i + 1) * 128],
                            vfm[:, i * 128:(i + 1) * 128], c_idb[:, :])
                    return last
                g.op("pe", vtp, reads=[("vfm",), ("const", "idb")],
                     writes=[("psT",)], label=f"vtp{sc}{vh}")

                def vte(e, vh=vh, sc=sc):
                    return e.tensor_copy(
                        vtmv[:, sc * 8:(sc + 1) * 8, vh, :],
                        psT[:, :].rearrange("p (i d) -> p i d", i=8))
                g.op("v", vte, reads=[("psT",)], writes=[("vt", vh, sc)],
                     label=f"vte{sc}{vh}")
        stP1.close()

        stP2 = contextlib.ExitStack()
        p2rs = [sb(f"p2r{i}", [128, 2 * T], BF16, st=stP2) for i in range(2)]
        aosg = [sb(f"aosg{i}", [128, 256], BF16, st=stP2)
                for i in range(2)]
        ptb = [sb(f"ptb{i}", [128, 1024], BF16, st=stP2) for i in range(2)]
        avb = sb("avb", [128, 256], st=stP2)
        nmx = sb("nmx", [128, 64], st=stP2)
        ssum = sb("ssum", [128, 64], st=stP2)
        fbuf = sb("fbuf", [128, 64], st=stP2)
        wtm = sb("wtm", [128, 64], st=stP2)
        nmn = sb("nmn", [128, 4], st=stP2)
        rsum = sb("rsum", [128, 4], st=stP2)
        rrec = sb("rrec", [128, 4], st=stP2)

        un_reads = [("un", u, sc) for u in range(4) for sc in range(NSC)]
        vt_reads = [("vt", vh, sc) for vh in range(2) for sc in range(NSC)]

        SBK = [("psB", 0), ("psB", 1), ("psA", 0), ("psA", 1)]
        SBT = [psB[0], psB[1], psA[0], psA[1]]

        def S_thunks(hh, g2, qt, pp):
            uq, uk = units[2 * hh], units[2 * hh + 1]
            j = 2 * g2 + qt
            kvlen = 128 * j + 128
            kvspan = 256 * g2 + 256
            nw = (kvlen + 511) // 512
            p2 = p2rs[pp]
            base = pp * 32 + qt * 16
            th = []
            if qt == 0 and kvlen < kvspan:
                def t_ms():
                    g.op("v", lambda e: e.memset(p2[:, kvlen:kvspan], 0),
                         reads=[], writes=[("p2", pp, 0)],
                         label=f"mst{hh}_{j}")
                th.append(t_ms)
            for w0_ in range(0, nw, 2):
                ws = [w_ for w_ in (w0_, w0_ + 1) if w_ < nw]

                def t_batch(ws=ws, j=j, kvlen=kvlen, nw=nw, base=base,
                            uq=uq, uk=uk, p2=p2, pp=pp, qt=qt, hh=hh):
                    bkeys = [SBK[w_ % 4] for w_ in ws]

                    def smm(e):
                        last = None
                        for w_ in ws:
                            fwid = min(512, kvlen - 512 * w_)
                            isl = w_ == nw - 1
                            last = e.matmul(
                                SBT[w_ % 4][:, :fwid],
                                uq[:, j * 128:(j + 1) * 128],
                                uk[:, w_ * 512:w_ * 512 + fwid],
                                start=True, stop=not isl)
                            if isl:
                                last = e.matmul(
                                    SBT[w_ % 4][:, fwid - 128:fwid],
                                    c_idb[:, :], c_trm[:, :],
                                    start=False, stop=True)
                        return last
                    g.op("pe", smm,
                         reads=un_reads + [("const", "idb"),
                                           ("const", "trm")],
                         writes=bkeys, label=f"smm{hh}_{j}_{ws[0]}")

                    def cmx(e):
                        last = None
                        for w_ in ws:
                            fwid = min(512, kvlen - 512 * w_)
                            last = e.tensor_reduce(
                                nmx[:, base + w_:base + w_ + 1],
                                SBT[w_ % 4][:, :fwid], AX.X, ALU.max,
                                negate=True)
                        return last
                    g.op("v", cmx, reads=bkeys,
                         writes=[("nmx", pp, qt, ws[0])],
                         label=f"cmx{hh}_{j}_{ws[0]}")

                    def exc(e):
                        last = None
                        for w_ in ws:
                            fwid = min(512, kvlen - 512 * w_)
                            last = e.activation(
                                p2[:, qt * T + w_ * 512:
                                   qt * T + w_ * 512 + fwid],
                                SBT[w_ % 4][:, :fwid], AF.Exp,
                                bias=nmx[:, base + w_:base + w_ + 1],
                                scale=1.0,
                                accum_out=ssum[:, base + w_:base + w_ + 1])
                        return last
                    g.op("a", exc, reads=bkeys + [("nmx", pp, qt, ws[0])],
                         writes=[("p2", pp, qt), ("ssum", pp, qt, ws[0])],
                         label=f"exc{hh}_{j}_{ws[0]}")
                th.append(t_batch)

            def t_fin(j=j, nw=nw, base=base, pp=pp, qt=qt, hh=hh, p2=p2,
                      kvlen=kvlen):
                rds = [("nmx", pp, qt, w0_) for w0_ in range(0, nw, 2)]
                srds = [("ssum", pp, qt, w0_) for w0_ in range(0, nw, 2)]
                g.op("v", lambda e: e.tensor_reduce(
                    nmn[:, 2 * pp + qt:2 * pp + qt + 1],
                    nmx[:, base:base + nw], AX.X, ALU.min),
                    reads=rds, writes=[("nmn", pp, qt)], label=f"nmn{hh}_{j}")
                g.op("a", lambda e: e.activation(
                    fbuf[:, base:base + nw], nmx[:, base:base + nw],
                    AF.Exp, bias=nmn[:, 2 * pp + qt:2 * pp + qt + 1],
                    scale=-1.0),
                    reads=rds + [("nmn", pp, qt)], writes=[("fct", pp, qt)],
                    label=f"fct{hh}_{j}")
                g.op("g", lambda e: e.tensor_tensor(
                    wtm[:, base:base + nw], fbuf[:, base:base + nw],
                    ssum[:, base:base + nw], ALU.mult),
                    reads=srds + [("fct", pp, qt)], writes=[("wtm", pp, qt)],
                    label=f"wtm{hh}_{j}")
                g.op("v", lambda e: e.tensor_reduce(
                    rsum[:, 2 * pp + qt:2 * pp + qt + 1],
                    wtm[:, base:base + nw], AX.X, ALU.add),
                    reads=[("wtm", pp, qt)], writes=[("rsm", pp, qt)],
                    label=f"wsm{hh}_{j}")
                for w0_ in range(0, nw, 2):
                    ws = [w_ for w_ in (w0_, w0_ + 1) if w_ < nw]

                    def rsc(e, ws=ws):
                        last = None
                        for w_ in ws:
                            fwid = min(512, kvlen - 512 * w_)
                            sl = p2[:, qt * T + w_ * 512:
                                    qt * T + w_ * 512 + fwid]
                            last = e.tensor_scalar(
                                sl, sl, fbuf[:, base + w_:base + w_ + 1],
                                None, ALU.mult)
                        return last
                    g.op("v", rsc, reads=[("fct", pp, qt), ("p2", pp, qt)],
                         writes=[("p2", pp, qt)],
                         label=f"rsc{hh}_{j}_{ws[0]}")
            th.append(t_fin)
            return th

        def B_thunks(hh, g2, pp):
            kvspan = 256 * g2 + 256
            nkc = kvspan // 128
            p2 = p2rs[pp]
            th = []

            def t_rec():
                g.op("v", lambda e: e.reciprocal(
                    rrec[:, 2 * pp:2 * pp + 2], rsum[:, 2 * pp:2 * pp + 2]),
                    reads=[("rsm", pp, 0), ("rsm", pp, 1)],
                    writes=[("rrec", pp)], label=f"rec{hh}_{g2}")
            th.append(t_rec)
            batches = [list(range(kb, min(kb + 2, nkc)))
                       for kb in range(0, nkc, 2)]
            nb = len(batches)

            psA2b = psA[2][:, :].bitcast(BF16)

            def mk_ptp(bi):
                kcs = batches[bi]
                ch = bi % 2
                tgt = psT if ch == 0 else psA2b
                tkey = ("psT",) if ch == 0 else ("psA", 2)

                def ptp(e, kcs=kcs, tgt=tgt):
                    last = None
                    for ii, kc in enumerate(kcs):
                        for qt in range(2):
                            last = e.transpose(
                                tgt[:, ii * 256 + qt * 128:
                                    ii * 256 + (qt + 1) * 128],
                                p2[:, qt * T + kc * 128:
                                   qt * T + kc * 128 + 128],
                                c_idb[:, :])
                    return last
                g.op("pe", ptp,
                     reads=[("p2", pp, 0), ("p2", pp, 1), ("const", "idb")],
                     writes=[tkey], label=f"ptp{hh}_{g2}_{bi}")

                peng = "v" if ch == 0 else "a"

                def pte(e, kcs=kcs, tgt=tgt, ch=ch, peng=peng):
                    if peng == "v":
                        return e.tensor_copy(
                            ptb[ch][:, :len(kcs) * 256],
                            tgt[:, :len(kcs) * 256])
                    return e.activation(
                        ptb[ch][:, :len(kcs) * 256],
                        tgt[:, :len(kcs) * 256], AF.Copy)
                g.op(peng, pte, reads=[tkey], writes=[("ptb", ch)],
                     label=f"pte{hh}_{g2}_{bi}")

            def mk_avm(bi):
                kcs = batches[bi]
                ch = bi % 2

                def avm(e, kcs=kcs, ch=ch, hh=hh, bi=bi, nb=nb):
                    last = None
                    for ii, kc in enumerate(kcs):
                        last = e.matmul(
                            psM[:, :256], vtmv[:, kc, hh, :],
                            ptb[ch][:, ii * 256:(ii + 1) * 256],
                            start=(bi == 0 and ii == 0),
                            stop=(bi == nb - 1 and ii == len(kcs) - 1))
                    return last
                g.op("pe", avm, reads=vt_reads + [("ptb", ch), ("psM",)],
                     writes=[("psM",)], label=f"avm{hh}_{g2}_{bi}")

            th.append(lambda: mk_ptp(0))
            for bi in range(nb):
                def t_step(bi=bi):
                    if bi + 1 < nb:
                        mk_ptp(bi + 1)
                    mk_avm(bi)
                th.append(t_step)

            def t_tail():
                g.op("v", lambda e: e.tensor_copy(avb[:, :], psM[:, :256]),
                     reads=[("psM",)], writes=[("avb",), ("psM",)],
                     label=f"ave{hh}_{g2}")
                for qt in range(2):
                    j = 2 * g2 + qt

                    def avt(e, qt=qt):
                        return e.transpose(
                            psM[:, :128], avb[:, qt * 128:(qt + 1) * 128],
                            c_idf[:, :])
                    g.op("pe", avt, reads=[("avb",), ("const", "idf")],
                         writes=[("psM",)], label=f"avt{hh}_{g2}_{qt}")

                    def aoe(e, qt=qt):
                        return e.activation(
                            aosg[pp][:, qt * 128:(qt + 1) * 128],
                            psM[:, :128], AF.Identity,
                            scale=rrec[:, 2 * pp + qt:2 * pp + qt + 1])
                    g.op("a", aoe, reads=[("psM",), ("rrec", pp)],
                         writes=[("aosg", pp, qt)], label=f"aoe{hh}_{g2}_{qt}")
                g.dma("sy", lambda e: e.dma_start(
                    out=ao_c.ap().rearrange("(jt p) hd -> p jt hd", p=128)
                    [:, 2 * g2:2 * g2 + 2, hh * 128:(hh + 1) * 128],
                    in_=aosg[pp][:, :].rearrange("p (jt d) -> p jt d", jt=2)),
                    reads=[("aosg", pp, 0), ("aosg", pp, 1)],
                    writes=[("dr", "aoc", hh, g2)], label=f"aost{hh}_{g2}")
            th.append(t_tail)
            return th

        units_list = [(hh, g2) for hh in range(2) for g2 in range(32)]
        prev_bt = []
        for i, (hh, g2) in enumerate(units_list):
            pp = i % 2
            th = S_thunks(hh, g2, 0, pp) + S_thunks(hh, g2, 1, pp)
            nS, nB = len(th), len(prev_bt)
            bi = 0
            for k, t in enumerate(th):
                t()
                want = ((k + 1) * nB) // nS
                while bi < want:
                    prev_bt[bi]()
                    bi += 1
            while bi < nB:
                prev_bt[bi]()
                bi += 1
            prev_bt = B_thunks(hh, g2, pp)
        for t in prev_bt:
            t()
        g.cc(lambda e: e.collective_compute(
            "AllGather", ALU.bypass, replica_groups=[CORE_IDS],
            ins=[ao_c.ap()], outs=[ao_full.ap()]),
            reads=[("dr", "aoc", hh, g2)
                   for hh in range(2) for g2 in range(32)],
            writes=[("dr", "ao_full")], label="agao")
        if debug:
            g.dma("sy", lambda e: e.dma_start(
                out=dmp["d_ao"].ap(), in_=ao_full.ap()[0:T, :]),
                reads=[("dr", "ao_full")], label="dump_ao")
        stP2.close()
        stA.close()
        g.fence()

        stx2 = contextlib.ExitStack()
        x2buf = sb("x2buf", [128, NTT * D], st=stx2)
        stl = contextlib.ExitStack()
        aoin = sb("aoin", [128, NTT * 8 * 256], BF16, st=stl)
        xin2 = sb("xin2", [128, NTT * D], st=stl)
        g.dma("sy", lambda e: e.dma_start(
            out=xin2[:, :].rearrange("p (t d) -> p t d", t=NTT),
            in_=x_rows.ap().rearrange("(t p) d -> p t d", p=128)),
            writes=[("xin2", tt) for tt in range(NTT)], label="ldx2")

        def ld_ao(e):
            dsem = g.cur_dma_sem
            with e.register("r_ao") as r0:
                e.reg_load(r0, c_cid[0:1, 0:1])
                cid = e.snap(r0, min_val=0, max_val=7)
                base = cid * TPC
                last = None
                for r in range(NCORES):
                    last = e.dma_start(
                        out=aoin[:, :].rearrange(
                            "p (t r hd) -> p t r hd", t=NTT, r=8)[:, :, r, :],
                        in_=ao_full.ap()[
                            bass.ds(base + r * T, TPC), :].rearrange(
                            "(t p) hd -> p t hd", p=128))
                    if r < NCORES - 1:
                        last.then_inc(dsem, 16)
                return last
        g.dmam("g", ld_ao, NCORES,
               reads=[("dr", "ao_full"), ("const", "cid")],
               writes=[("aoin", tt) for tt in range(NTT)], label="ldao")
        for tt in range(NTT):
            def xadd(e, tt=tt):
                return e.tensor_tensor(
                    x2buf[:, tt * D:(tt + 1) * D],
                    xin2[:, tt * D:(tt + 1) * D],
                    aoin[:, tt * 2048:(tt + 1) * 2048], ALU.add)
            g.op("v", xadd, reads=[("xin2", tt), ("aoin", tt)],
                 writes=[("row", "x2", tt)], label=f"x2_{tt}")
        if debug:
            g.dma("sy", lambda e: e.dma_start(
                out=dmp["d_x2"].ap().rearrange("(t p) d -> p t d", p=128),
                in_=x2buf[:, :].rearrange("p (t d) -> p t d", t=NTT)),
                reads=[("row", "x2", tt) for tt in range(NTT)],
                label="dump_x2")
        stl.close()
        g.fence()

        def build_ffn(srcbuf, rtag_in, rtag_out, store_out):
            stf = contextlib.ExitStack()
            tg = rtag_out
            wb_sb = sb(f"wb_sb{tg}", [128, 16 * D], BF16, st=stf)
            rt_sb = sb(f"rt_sb{tg}", [128, 16 * 8], BF16, st=stf)
            at_sb = sb(f"at_sb{tg}", [128, 16 * 128], BF16, st=stf)
            bw_sb = sb(f"bw_sb{tg}", [128, D], BF16, st=stf)
            bb_sb = sb(f"bb_sb{tg}", [128, D], st=stf)
            xn2 = sb(f"xn2{tg}", [128, D], BF16, st=stf)
            xn2T = sb(f"xn2T{tg}", [128, 16 * 128], BF16, st=stf)
            lgt = sb(f"lgt{tg}", [128, 8], st=stf)
            lmx = sb(f"lmx{tg}", [128, 1], st=stf)
            msk = sb(f"msk{tg}", [128, 8], st=stf)
            hmb = sb(f"hmb{tg}", [128, 128], BF16, st=stf)
            hmT = sb(f"hmT{tg}", [128, 128], BF16, st=stf)
            fsc = sb(f"fsc{tg}", [128, D], st=stf)
            fob = sb(f"fob{tg}", [128, D], st=stf)

            for t_, s_, et_ in ((wbT, wb_sb, 16), (routerT, rt_sb, 16),
                                (aT, at_sb, 16)):
                g.dma("g", lambda e, t=t_, s=s_, et=et_: e.dma_start(
                    out=s[:, :].rearrange("p (et o) -> p et o", et=et),
                    in_=t.ap().rearrange("(et p) o -> p et o", p=128)),
                    writes=[("fw", t_.name, tg)], label=f"ldf_{t_.name}{tg}")
            g.dma("g", lambda e: e.dma_start(out=bw_sb[:, :], in_=bwT.ap()),
                  writes=[("fw", "bwT", tg)], label=f"ldf_bw{tg}")
            g.dma("sy", lambda e: e.dma_start(out=bb_sb[:, :],
                                              in_=bbase_rep.ap()),
                  writes=[("fw", "bb", tg)], label=f"ldf_bb{tg}")

            for tt in range(NTT):
                xap = srcbuf[:, tt * D:(tt + 1) * D]
                emit_ln(xap, ("row", rtag_in, tt), xn2[:, :], ("fxn", tg),
                        f"{tg}_{tt}")
                for half in range(2):
                    def tp(e, half=half):
                        last = None
                        for i in range(8):
                            et = half * 8 + i
                            last = e.transpose(
                                psT[:, i * 128:(i + 1) * 128],
                                xn2[:, et * 128:(et + 1) * 128], c_idb[:, :])
                        return last
                    g.op("pe", tp, reads=[("fxn", tg), ("const", "idb")],
                         writes=[("psT",)], label=f"ftp{tg}{tt}{half}")

                    def ev(e, half=half):
                        return e.tensor_copy(
                            xn2T[:, :].rearrange("p (et t) -> p et t", et=16)
                            [:, half * 8:(half + 1) * 8, :],
                            psT[:, :].rearrange("p (i t) -> p i t", i=8))
                    g.op("v", ev, reads=[("psT",)], writes=[("fxT", tg)],
                         label=f"fev{tg}{tt}{half}")

                def bigmm(e):
                    last = None
                    for et in range(16):
                        lhs = xn2T[:, et * 128:(et + 1) * 128]
                        for oc in range(4):
                            last = e.matmul(
                                psA[oc][:, :], lhs,
                                wb_sb[:, et * D + oc * 512:
                                      et * D + oc * 512 + 512],
                                start=(et == 0), stop=False)
                        last = e.matmul(
                            psB[0][:, :8], lhs,
                            rt_sb[:, et * 8:(et + 1) * 8],
                            start=(et == 0), stop=(et == 15))
                        last = e.matmul(
                            psB[1][:, :128], lhs,
                            at_sb[:, et * 128:(et + 1) * 128],
                            start=(et == 0), stop=(et == 15))
                    return last
                g.op("pe", bigmm,
                     reads=[("fxT", tg), ("fw", "wbT", tg),
                            ("fw", "routerT", tg), ("fw", "aT", tg)],
                     writes=[("psA", 0), ("psA", 1), ("psA", 2), ("psA", 3),
                             ("psB", 0), ("psB", 1)],
                     label=f"fmm{tg}{tt}")
                g.op("v", lambda e: e.tensor_copy(lgt[:, :], psB[0][:, :8]),
                     reads=[("psB", 0)], writes=[("flg", tg)],
                     label=f"flg{tg}{tt}")
                g.op("v", lambda e: e.tensor_reduce(
                    lmx[:, :], lgt[:, :], AX.X, ALU.max),
                    reads=[("flg", tg)], writes=[("fmx", tg)],
                    label=f"fmx{tg}{tt}")
                g.op("v", lambda e: e.tensor_tensor(
                    msk[:, :], lgt[:, :], lmx[:, 0:1].broadcast_to([128, 8]),
                    ALU.is_equal),
                    reads=[("flg", tg), ("fmx", tg)], writes=[("fms", tg)],
                    label=f"feq{tg}{tt}")
                g.op("v", lambda e: e.tensor_scalar(
                    msk[:, :], msk[:, :], 2.0, None, ALU.mult),
                    reads=[("fms", tg)], writes=[("fms", tg)],
                    label=f"fm2{tg}{tt}")
                def fhm(e):
                    return e.tensor_tensor(
                        hmb[:, :].rearrange("p (ee r) -> p ee r", ee=8),
                        psB[1][:, :128].rearrange("p (ee r) -> p ee r", ee=8),
                        msk[:, :].rearrange("p (ee o) -> p ee o", o=1)
                        .broadcast_to([128, 8, 16]),
                        ALU.mult)
                g.op("v", fhm, reads=[("psB", 1), ("fms", tg)],
                     writes=[("fhm", tg)], label=f"fhm{tg}{tt}")
                g.op("pe", lambda e: e.transpose(
                    psT[:, :128], hmb[:, :], c_idb[:, :]),
                    reads=[("fhm", tg), ("const", "idb")], writes=[("psT",)],
                    label=f"fht{tg}{tt}")
                g.op("v", lambda e: e.tensor_copy(hmT[:, :], psT[:, :128]),
                     reads=[("psT",)], writes=[("fhT", tg)],
                     label=f"fhe{tg}{tt}")

                def dmm(e):
                    last = None
                    for oc in range(4):
                        last = e.matmul(
                            psA[oc][:, :], hmT[:, :],
                            bw_sb[:, oc * 512:oc * 512 + 512],
                            start=False, stop=True)
                    return last
                g.op("pe", dmm, reads=[("fhT", tg), ("fw", "bwT", tg),
                                       ("psA", 0), ("psA", 1), ("psA", 2),
                                       ("psA", 3)],
                     writes=[("psA", 0), ("psA", 1), ("psA", 2), ("psA", 3)],
                     label=f"fdm{tg}{tt}")
                def e1(e):
                    last = None
                    for oc in range(4):
                        last = e.tensor_tensor(
                            fsc[:, oc * 512:(oc + 1) * 512], psA[oc][:, :],
                            bb_sb[:, oc * 512:oc * 512 + 512], ALU.add)
                    return last
                g.op("v", e1,
                     reads=[("psA", 0), ("psA", 1), ("psA", 2), ("psA", 3),
                            ("fw", "bb", tg)],
                     writes=[("fsc", tg)], label=f"fe1{tg}{tt}")
                if store_out is None:
                    def e2(e, tt=tt):
                        return e.tensor_tensor(
                            srcbuf[:, tt * D:(tt + 1) * D], fsc[:, :],
                            srcbuf[:, tt * D:(tt + 1) * D], ALU.add)
                    g.op("v", e2,
                         reads=[("fsc", tg), ("row", rtag_in, tt)],
                         writes=[("row", rtag_out, tt),
                                 ("row", rtag_in, tt)],
                         label=f"fe2{tg}{tt}")
                else:
                    def e2(e, tt=tt):
                        return e.tensor_tensor(
                            fob[:, :], fsc[:, :],
                            srcbuf[:, tt * D:(tt + 1) * D], ALU.add)
                    g.op("v", e2,
                         reads=[("fsc", tg), ("row", rtag_in, tt)],
                         writes=[("fob", tg)], label=f"fe2{tg}{tt}")
                    g.dma("sy", lambda e, tt=tt: e.dma_start(
                        out=store_out.ap()[tt * 128:(tt + 1) * 128, :],
                        in_=fob[:, :]),
                        reads=[("fob", tg)],
                        writes=[("outst", tt)], label=f"fst{tg}{tt}")
            stf.close()

        build_ffn(x2buf, "x2", "x3", None)
        g.fence()

        st4 = contextlib.ExitStack()
        xnsc4 = sb("xnsc4", [128, D], BF16, st=st4)
        xnts4 = sb("xnts4", [128, NTT * D], BF16, st=st4)
        emit_xnT_phase(x2buf, "x3", "x3", xnsc4, xnts4)
        g.dma("sy", lambda e: e.dma_start(
            out=x3_d.ap().rearrange("(t p) d -> p t d", p=128),
            in_=x2buf[:, :].rearrange("p (t d) -> p t d", t=NTT)),
            reads=[("row", "x3", tt) for tt in range(NTT)],
            writes=[("dr", "x3_d")], label="spx3")
        if debug:
            g.dma("sy", lambda e: e.dma_start(
                out=dmp["d_x3"].ap(), in_=x3_d.ap()),
                reads=[("dr", "x3_d")], label="dump_x3")
        st4.close()
        stx2.close()
        g.fence()

        st5 = contextlib.ExitStack()
        inw_sb = sb("inw_sb", [128, 16 * 1024], BF16, st=st5)
        inb_sb = sb("inb_sb", [128, 8], st=st5)
        cw_sb = sb("cw_sb", [128, 16], st=st5)
        cb_sb = sb("cb_sb", [128, 4], st=st5)
        xpj_sb = sb("xpj_sb", [128, 4 * 160], BF16, st=st5)
        xnq5 = sb("xnq5", [128, 16 * IP_SC], BF16, st=st5)
        hbuf = sb("hbuf", [128, 4 * (IP_SC + 3)], st=st5)
        hbv = hbuf[:, :].rearrange("p (db t) -> p db t", db=4)
        halo = sb("halo", [128, 12], st=st5)
        acc1 = sb("acc1", [128, IP_SC], st=st5)
        acc2 = sb("acc2", [128, IP_SC], st=st5)
        hsf = sb("hsf", [128, IP_SC], st=st5)
        hsb = sb("hsb", [128, 4 * IP_SC], BF16, st=st5)
        sgb = sb("sgb", [128, 4 * IP_SC], BF16, st=st5)
        tsacc = sb("tsacc", [128, T], st=st5)
        bcacc = sb("bcacc", [32, T], st=st5)

        g.dma("g", lambda e: e.dma_start(
            out=inw_sb[:, :].rearrange("p (et o) -> p et o", et=16),
            in_=inwT_sl.ap().rearrange("(et p) o -> p et o", p=128)),
            writes=[("w5", "inw")], label="ld_inw")
        g.dma("sy", lambda e: e.dma_start(
            out=inb_sb[:, :], in_=inb_sl.ap()),
            writes=[("w5", "inb")], label="ld_inb")
        g.dma("sy", lambda e: e.dma_start(
            out=cw_sb[:, :], in_=convw_sl.ap()),
            writes=[("w5", "cw")], label="ld_cw")
        g.dma("sy", lambda e: e.dma_start(
            out=cb_sb[:, :], in_=convb_sl.ap()),
            writes=[("w5", "cb")], label="ld_cb")
        g.dma("g", lambda e: e.dma_start(
            out=xpj_sb[:, :].rearrange("p (db o) -> p db o", db=4),
            in_=xprojT_sl.ap().rearrange("(db p) o -> p db o", p=128)),
            writes=[("w5", "xpj")], label="ld_xpj")
        g.op("v", lambda e: e.memset(halo[:, :], 0), writes=[("halo",)],
             label="halo0")

        for sc in range(NSC):
            w0 = sc * IP_SC
            g.dma("sy", lambda e, sc=sc: e.dma_start(
                out=xnq5[:, :].rearrange("p (et t) -> p et t", et=16),
                in_=xn_full.ap().rearrange("(r et p) t -> r p et t",
                                           p=128, et=16)[sc]),
                reads=[("dr", "xn_full")], writes=[("xnq5",)],
                label=f"ld5x{sc}")
            for db in range(4):
                def hcp(e, db=db):
                    return e.tensor_copy(hbv[:, db, 0:3],
                                         halo[:, db * 3:(db + 1) * 3])
                g.op("v", hcp, reads=[("halo",)], writes=[("hb", db)],
                     label=f"hcp{sc}{db}")
            for dt in range(8):
                for fw in range(2):
                    bank = 2 * (dt % 2) + fw
                    def mm(e, dt=dt, fw=fw, bank=bank):
                        last = None
                        for et in range(16):
                            last = e.matmul(
                                psA[bank][:, :],
                                inw_sb[:, et * 1024 + dt * 128:
                                       et * 1024 + (dt + 1) * 128],
                                xnq5[:, et * IP_SC + fw * 512:
                                     et * IP_SC + fw * 512 + 512],
                                start=(et == 0), stop=(et == 15))
                        return last
                    g.op("pe", mm, reads=[("xnq5",), ("w5", "inw")],
                         writes=[("psA", bank)], label=f"imm{sc}{dt}{fw}")
                    if dt < 4:
                        def hev(e, dt=dt, fw=fw, bank=bank):
                            return e.activation(
                                hbv[:, dt, 3 + fw * 512:3 + fw * 512 + 512],
                                psA[bank][:, :], AF.Identity,
                                bias=inb_sb[:, dt:dt + 1])
                        g.op("a", hev, reads=[("psA", bank), ("w5", "inb")],
                             writes=[("hb", dt)], label=f"hev{sc}{dt}{fw}")
                    else:
                        def gev(e, dt=dt, fw=fw, bank=bank):
                            return e.activation(
                                sgb[:, :].rearrange("p (db t) -> p db t",
                                                    db=4)
                                [:, dt - 4, fw * 512:fw * 512 + 512],
                                psA[bank][:, :], AF.Silu,
                                bias=inb_sb[:, dt:dt + 1])
                        g.op("a", gev, reads=[("psA", bank), ("w5", "inb")],
                             writes=[("sg", dt - 4)], label=f"gev{sc}{dt}{fw}")
            for db in range(4):
                g.op("v", lambda e, db=db: e.tensor_scalar(
                    acc1[:, :], hbv[:, db, 0:IP_SC],
                    cw_sb[:, db * 4:db * 4 + 1], None, ALU.mult),
                    reads=[("hb", db), ("w5", "cw")], writes=[("ac1",)],
                    label=f"cv0{sc}{db}")
                g.op("v", lambda e, db=db: e.scalar_tensor_tensor(
                    acc2[:, :], hbv[:, db, 1:1 + IP_SC],
                    cw_sb[:, db * 4 + 1:db * 4 + 2], acc1[:, :],
                    ALU.mult, ALU.add),
                    reads=[("hb", db), ("ac1",), ("w5", "cw")],
                    writes=[("ac2",)], label=f"cv1{sc}{db}")
                g.op("v", lambda e, db=db: e.scalar_tensor_tensor(
                    acc1[:, :], hbv[:, db, 2:2 + IP_SC],
                    cw_sb[:, db * 4 + 2:db * 4 + 3], acc2[:, :],
                    ALU.mult, ALU.add),
                    reads=[("hb", db), ("ac2",), ("w5", "cw")],
                    writes=[("ac1",)], label=f"cv2{sc}{db}")
                g.op("v", lambda e, db=db: e.scalar_tensor_tensor(
                    acc2[:, :], hbv[:, db, 3:3 + IP_SC],
                    cw_sb[:, db * 4 + 3:db * 4 + 4], acc1[:, :],
                    ALU.mult, ALU.add),
                    reads=[("hb", db), ("ac1",), ("w5", "cw")],
                    writes=[("ac2",)], label=f"cv3{sc}{db}")
                def hsv(e, db=db):
                    return e.tensor_copy(halo[:, db * 3:(db + 1) * 3],
                                         hbv[:, db, IP_SC:IP_SC + 3])
                g.op("v", hsv, reads=[("hb", db)], writes=[("halo",)],
                     label=f"hsv{sc}{db}")
                g.op("a", lambda e, db=db: e.activation(
                    hsf[:, :], acc2[:, :], AF.Silu,
                    bias=cb_sb[:, db:db + 1]),
                    reads=[("ac2",), ("w5", "cb")], writes=[("hsf",)],
                    label=f"cvs{sc}{db}")
                def hcast(e, db=db):
                    return e.tensor_copy(
                        hsb[:, db * IP_SC:(db + 1) * IP_SC], hsf[:, :])
                g.op("v", hcast, reads=[("hsf",)], writes=[("hsb", db)],
                     label=f"hct{sc}{db}")
            g.dma("sy", lambda e, w0=w0: e.dma_start(
                out=hsb_d.ap().rearrange("(db p) t -> p db t", p=128)
                [:, :, w0:w0 + IP_SC],
                in_=hsb[:, :].rearrange("p (db t) -> p db t", db=4)),
                reads=[("hsb", db) for db in range(4)],
                writes=[("dr", "hsb", sc)], label=f"sth{sc}")
            g.dma("sy", lambda e, w0=w0: e.dma_start(
                out=sgb_d.ap().rearrange("(db p) t -> p db t", p=128)
                [:, :, w0:w0 + IP_SC],
                in_=sgb[:, :].rearrange("p (db t) -> p db t", db=4)),
                reads=[("sg", db) for db in range(4)],
                writes=[("dr", "sgb", sc)], label=f"stg{sc}")
            for fw in range(2):
                def xmm(e, fw=fw):
                    last = None
                    for db in range(4):
                        last = e.matmul(
                            psB[fw][:, :],
                            xpj_sb[:, db * 160:db * 160 + 128],
                            hsb[:, db * IP_SC + fw * 512:
                                db * IP_SC + fw * 512 + 512],
                            start=(db == 0), stop=(db == 3))
                    return last
                g.op("pe", xmm,
                     reads=[("hsb", db) for db in range(4)] + [("w5", "xpj")],
                     writes=[("psB", fw)], label=f"xts{sc}{fw}")

                def xev(e, fw=fw, w0=w0):
                    return e.tensor_copy(
                        tsacc[:, w0 + fw * 512:w0 + fw * 512 + 512],
                        psB[fw][:, :])
                g.op("v", xev, reads=[("psB", fw)], writes=[("tsa", sc, fw)],
                     label=f"xte{sc}{fw}")

                def bmm(e, fw=fw):
                    last = None
                    for db in range(4):
                        last = e.matmul(
                            psA[fw][0:32, :],
                            xpj_sb[:, db * 160 + 128:db * 160 + 160],
                            hsb[:, db * IP_SC + fw * 512:
                                db * IP_SC + fw * 512 + 512],
                            start=(db == 0), stop=(db == 3))
                    return last
                g.op("pe", bmm,
                     reads=[("hsb", db) for db in range(4)] + [("w5", "xpj")],
                     writes=[("psA", fw)], label=f"xbc{sc}{fw}")

                def bev(e, fw=fw, w0=w0):
                    return e.tensor_copy(
                        bcacc[:, w0 + fw * 512:w0 + fw * 512 + 512],
                        psA[fw][0:32, :])
                g.op("v", bev, reads=[("psA", fw)], writes=[("bca", sc, fw)],
                     label=f"xbe{sc}{fw}")
        g.dma("sy", lambda e: e.dma_start(
            out=sp_c.ap()[0:128, :], in_=tsacc[:, :]),
            reads=[("tsa", sc, fw) for sc in range(NSC) for fw in range(2)],
            writes=[("dr", "spc1")], label="stts")
        g.dma("sy", lambda e: e.dma_start(
            out=sp_c.ap()[128:160, :], in_=bcacc[:, :]),
            reads=[("bca", sc, fw) for sc in range(NSC) for fw in range(2)],
            writes=[("dr", "spc2")], label="stbc")
        g.cc(lambda e: e.collective_compute(
            "AllReduce", ALU.add, replica_groups=[CORE_IDS],
            ins=[sp_c.ap()], outs=[sp_full.ap()]),
            reads=[("dr", "spc1"), ("dr", "spc2")],
            writes=[("dr", "sp_full")], label="arsp")
        if debug:
            g.dma("sy", lambda e: e.dma_start(
                out=dmp["d_sp"].ap(), in_=sp_full.ap()),
                reads=[("dr", "sp_full")], label="dump_sp")
            g.dma("sy", lambda e: e.dma_start(
                out=dmp["d_hs"].ap(), in_=hsb_d.ap()),
                reads=[("dr", "hsb", sc) for sc in range(NSC)],
                label="dump_hs")
            g.dma("sy", lambda e: e.dma_start(
                out=dmp["d_sg"].ap(), in_=sgb_d.ap()),
                reads=[("dr", "sgb", sc) for sc in range(NSC)],
                label="dump_sg")
        st5.close()
        g.fence()

        st6 = contextlib.ExitStack()
        dtw_sb = sb("dtw_sb", [128, DIPC], BF16, st=st6)
        dtb_sb = sb("dtb_sb", [128, 4], st=st6)
        al_sb = sb("al_sb", [128, 4 * NST], st=st6)
        a_sb = sb("a_sb", [128, 4 * NST], st=st6)
        dd_sb = sb("dd_sb", [128, 4], st=st6)
        ow_sb = sb("ow_sb", [128, 4 * D], BF16, st=st6)
        tsch = sb("tsch", [128, 2 * SCAN_TC], BF16, st=st6)
        bcrep = sb("bcrep", [128, 2 * 2 * NST * SCAN_TC], BF16, st=st6)
        dtch = sb("dtch", [128, 4 * SCAN_TC], st=st6)
        dtu = sb("dtu", [128, 2 * SCAN_TC], st=st6)
        spa = sb("spa", [128, 2 * SCAN_TC], st=st6)
        spb = sb("spb", [128, 2 * SCAN_TC], st=st6)
        ach = sb("ach", [128, 2 * NST * SCAN_TC], st=st6)
        bch = sb("bch", [128, 2 * NST * SCAN_TC], st=st6)
        sch = sb("sch", [128, 2 * NST * SCAN_TC], st=st6)
        cmch = sb("cmch", [128, NST * SCAN_TC], st=st6)
        state = sb("state", [128, 4 * NST], st=st6)
        hsch = sb("hsch", [128, 2 * 4 * SCAN_TC], BF16, st=st6)
        sgch = sb("sgch", [128, 2 * 4 * SCAN_TC], BF16, st=st6)
        ybuf = sb("ybuf", [128, 2 * SCAN_TC], st=st6)
        zsc = sb("zsc", [128, SCAN_TC], st=st6)
        zsc2 = sb("zsc2", [128, SCAN_TC], st=st6)
        zb = sb("zb", [128, 2 * 4 * SCAN_TC], BF16, st=st6)
        obuf = sb("obuf", [128, D], st=st6)

        g.dma("g", lambda e: e.dma_start(out=dtw_sb[:, :],
                                         in_=dtwT_sl.ap()),
              writes=[("w6", "dtw")], label="ld_dtw")
        g.dma("sy", lambda e: e.dma_start(
            out=dtb_sb[:, :], in_=dtb_sl.ap()),
            writes=[("w6", "dtb")], label="ld_dtb")
        g.dma("sy", lambda e: e.dma_start(
            out=al_sb[:, :], in_=alog_sl.ap()),
            writes=[("w6", "al")], label="ld_al")
        g.dma("sy", lambda e: e.dma_start(
            out=dd_sb[:, :], in_=d_sl.ap()),
            writes=[("w6", "dd")], label="ld_dd")
        g.dma("g", lambda e: e.dma_start(
            out=ow_sb[:, :].rearrange("p (db o) -> p db o", db=4),
            in_=outwT_sl.ap().rearrange("(db p) o -> p db o", p=128)),
            writes=[("w6", "ow")], label="ld_ow")
        g.op("a", lambda e: e.activation(a_sb[:, :], al_sb[:, :], AF.Exp),
             reads=[("w6", "al")], writes=[("w6", "ae")], label="aexp")
        g.op("v", lambda e: e.tensor_scalar(
            a_sb[:, :], a_sb[:, :], -1.0, None, ALU.mult),
            reads=[("w6", "ae")], writes=[("w6", "a")], label="aneg")
        g.op("v", lambda e: e.memset(state[:, :], 0), writes=[("st", 0),
             ("st", 1), ("st", 2), ("st", 3)], label="st0")

        def emit_ldts(tc):
            w0 = tc * SCAN_TC
            pp_tc = tc % 2
            TSO = pp_tc * SCAN_TC
            g.dma("g", lambda e, w0=w0, TSO=TSO: e.dma_start(
                out=tsch[:, TSO:TSO + SCAN_TC],
                in_=sp_full.ap()[0:128, w0:w0 + SCAN_TC]),
                reads=[("dr", "sp_full")], writes=[("tsch", pp_tc)],
                label=f"ldts{tc}")

        def emit_ldbig(tc):
            w0 = tc * SCAN_TC
            pp_tc = tc % 2
            HSO = pp_tc * 4 * SCAN_TC
            RPO = pp_tc * NST * SCAN_TC

            def ldbc(e, w0=w0, RPO=RPO):
                return e.dma_start(
                    out=bcrep[:, 2 * RPO:2 * RPO + 2 * NST * SCAN_TC]
                    .rearrange("p (n t) -> p n t", n=2 * NST),
                    in_=sp_full.ap().rearrange("(o r) t -> o r t", o=1)
                    [0:1, 128:160, w0:w0 + SCAN_TC]
                    .broadcast_to([128, 2 * NST, SCAN_TC]))
            g.dma("g", ldbc, reads=[("dr", "sp_full")],
                  writes=[("bcrep", pp_tc)], label=f"ldbc{tc}")
            g.dma("sy", lambda e, w0=w0, HSO=HSO: e.dma_start(
                out=hsch[:, HSO:HSO + 4 * SCAN_TC].rearrange(
                    "p (db t) -> p db t", db=4),
                in_=hsb_d.ap().rearrange("(db p) t -> p db t", p=128)
                [:, :, w0:w0 + SCAN_TC]),
                reads=[("dr", "hsb", w0 // IP_SC)], writes=[("hsch", pp_tc)],
                label=f"ldhs{tc}")
            g.dma("sy", lambda e, w0=w0, HSO=HSO: e.dma_start(
                out=sgch[:, HSO:HSO + 4 * SCAN_TC].rearrange(
                    "p (db t) -> p db t", db=4),
                in_=sgb_d.ap().rearrange("(db p) t -> p db t", p=128)
                [:, :, w0:w0 + SCAN_TC]),
                reads=[("dr", "sgb", w0 // IP_SC)], writes=[("sgch", pp_tc)],
                label=f"ldsg{tc}")

        def stA(u):
            tc, db = divmod(u, NDB)
            pp_tc = tc % 2
            TSO = pp_tc * SCAN_TC
            pp = db % 2
            dps = psB[1] if pp == 0 else psB[0]
            if db == 0 and tc + 1 < N_TC:
                emit_ldts(tc + 1)
            if db == 0 and tc >= 1:
                emit_ldbig(tc)

            def dmm(e, db=db, dps=dps, TSO=TSO):
                return e.matmul(
                    dps[:, :SCAN_TC], dtw_sb[:, db * 128:(db + 1) * 128],
                    tsch[:, TSO:TSO + SCAN_TC], start=True, stop=True)
            g.op("pe", dmm, reads=[("tsch", pp_tc), ("w6", "dtw")],
                 writes=[("dtps", pp)], label=f"dmm{tc}_{db}")
            g.op("a", lambda e, db=db, pp=pp, dps=dps: e.activation(
                spa[:, pp * SCAN_TC:(pp + 1) * SCAN_TC],
                dps[:, :SCAN_TC], AF.Abs, bias=dtb_sb[:, db:db + 1]),
                reads=[("dtps", pp), ("w6", "dtb")], writes=[("spa", pp)],
                label=f"sp1{tc}_{db}")
            g.op("a", lambda e, pp=pp: e.activation(
                spa[:, pp * SCAN_TC:(pp + 1) * SCAN_TC],
                spa[:, pp * SCAN_TC:(pp + 1) * SCAN_TC],
                AF.Exp, scale=-1.0),
                reads=[("spa", pp)], writes=[("spa", pp)],
                label=f"sp2{tc}_{db}")
            g.op("a", lambda e, pp=pp: e.activation(
                spa[:, pp * SCAN_TC:(pp + 1) * SCAN_TC],
                spa[:, pp * SCAN_TC:(pp + 1) * SCAN_TC],
                AF.Ln, bias=1.0),
                reads=[("spa", pp)], writes=[("spa", pp)],
                label=f"sp3{tc}_{db}")
            g.op("a", lambda e, db=db, pp=pp, dps=dps: e.activation(
                spb[:, pp * SCAN_TC:(pp + 1) * SCAN_TC],
                dps[:, :SCAN_TC], AF.Relu, bias=dtb_sb[:, db:db + 1]),
                reads=[("dtps", pp), ("w6", "dtb")], writes=[("spb", pp)],
                label=f"sp4{tc}_{db}")
        def stA2(u):
            tc, db = divmod(u, NDB)
            pp = db % 2
            g.op("v", lambda e, db=db, pp=pp: e.tensor_tensor(
                dtch[:, db * SCAN_TC:(db + 1) * SCAN_TC],
                spa[:, pp * SCAN_TC:(pp + 1) * SCAN_TC],
                spb[:, pp * SCAN_TC:(pp + 1) * SCAN_TC], ALU.add),
                reads=[("spa", pp), ("spb", pp)],
                writes=[("dtch", db), ("dtps", pp)],
                label=f"sp5{tc}_{db}")

        def stB(u):
            tc, db = divmod(u, NDB)
            pp_tc = tc % 2
            HSO = pp_tc * 4 * SCAN_TC
            RPO = pp_tc * NST * SCAN_TC
            pp = db % 2

            def aex(e, db=db, pp=pp):
                last = None
                for n in range(NST):
                    last = e.activation(
                        ach[:, (pp * NST + n) * SCAN_TC:
                            (pp * NST + n + 1) * SCAN_TC],
                        dtch[:, db * SCAN_TC:(db + 1) * SCAN_TC],
                        AF.Exp,
                        scale=a_sb[:, db * NST + n:db * NST + n + 1])
                return last
            g.op("a", aex, reads=[("dtch", db), ("w6", "a")],
                 writes=[("ach", pp)], label=f"aex{tc}_{db}")

            par = u % 2

            def du(e, db=db, HSO=HSO, par=par):
                return e.tensor_tensor(
                    dtu[:, par * SCAN_TC:(par + 1) * SCAN_TC],
                    dtch[:, db * SCAN_TC:(db + 1) * SCAN_TC],
                    hsch[:, HSO + db * SCAN_TC:
                         HSO + (db + 1) * SCAN_TC], ALU.mult)
            g.op("g", du, reads=[("dtch", db), ("hsch", pp_tc)],
                 writes=[("dtu", par)], label=f"dtu{tc}_{db}")

            def bop(e, db=db, pp=pp, RPO=RPO, par=par):
                return e.tensor_tensor(
                    bch[:, pp * NST * SCAN_TC:(pp + 1) * NST * SCAN_TC]
                    .rearrange("p (n t) -> p n t", n=NST),
                    dtu[:, par * SCAN_TC:(par + 1) * SCAN_TC]
                    .rearrange("p (o t) -> p o t", o=1)
                    .broadcast_to([128, NST, SCAN_TC]),
                    bcrep[:, 2 * RPO:2 * RPO + NST * SCAN_TC]
                    .rearrange("p (n t) -> p n t", n=NST),
                    ALU.mult)
            g.op("g", bop,
                 reads=[("dtu", par), ("bcrep", pp_tc),
                        ("sch", (u - 1) % 2)],
                 writes=[("bch", pp)], label=f"bop{tc}_{db}")

        def stC_scn(u):
            tc, db = divmod(u, NDB)
            pp = db % 2
            par = u % 2
            SCO = par * NST * SCAN_TC

            def scn(e, db=db, pp=pp, SCO=SCO):
                last = None
                for n in range(NST):
                    last = e.tensor_tensor_scan(
                        sch[:, SCO + n * SCAN_TC:SCO + (n + 1) * SCAN_TC],
                        ach[:, (pp * NST + n) * SCAN_TC:
                            (pp * NST + n + 1) * SCAN_TC],
                        bch[:, (pp * NST + n) * SCAN_TC:
                            (pp * NST + n + 1) * SCAN_TC],
                        state[:, db * NST + n:db * NST + n + 1],
                        ALU.mult, ALU.add)
                return last
            g.op("v", scn,
                 reads=[("ach", pp), ("bch", pp), ("st", db)],
                 writes=[("sch", par), ("ach", pp), ("bch", pp)],
                 label=f"scn{tc}_{db}")

        def stC_cmu(u):
            tc, db = divmod(u, NDB)
            par = u % 2
            SCO = par * NST * SCAN_TC
            pp_tc = tc % 2
            RPO = pp_tc * NST * SCAN_TC

            def cmu(e, RPO=RPO, SCO=SCO):
                return e.tensor_tensor(
                    cmch[:, :].rearrange("p (t n) -> p n t", n=NST),
                    sch[:, SCO:SCO + NST * SCAN_TC]
                    .rearrange("p (n t) -> p n t", n=NST),
                    bcrep[:, 2 * RPO + NST * SCAN_TC:
                          2 * RPO + 2 * NST * SCAN_TC]
                    .rearrange("p (n t) -> p n t", n=NST),
                    ALU.mult)
            g.op("v", cmu, reads=[("sch", par), ("bcrep", pp_tc)],
                 writes=[("cmch",)], label=f"cmu{tc}_{db}")

        def stC2(u):
            tc, db = divmod(u, NDB)
            par = u % 2
            SCO = par * NST * SCAN_TC

            def stu(e, db=db, SCO=SCO):
                return e.activation(
                    state[:, db * NST:(db + 1) * NST],
                    sch[:, SCO:SCO + NST * SCAN_TC]
                    .rearrange("p (n t) -> p n t", n=NST)
                    [:, :, SCAN_TC - 1:SCAN_TC].rearrange(
                        "p n o -> p (n o)"), AF.Copy)
            g.op("a", stu, reads=[("sch", par)],
                 writes=[("st", db)], label=f"stu{tc}_{db}")

        def stD1(u):
            tc, db = divmod(u, NDB)
            par = u % 2

            def yrd(e, par=par):
                return e.tensor_reduce(
                    ybuf[:, par * SCAN_TC:(par + 1) * SCAN_TC],
                    cmch[:, :].rearrange("p (t n) -> p t n", n=NST),
                    AX.X, ALU.add)
            g.op("v", yrd, reads=[("cmch",)], writes=[("yb", par)],
                 label=f"yrd{tc}_{db}")

        def stD2(u):
            tc, db = divmod(u, NDB)
            pp_tc = tc % 2
            HSO = pp_tc * 4 * SCAN_TC
            par = u % 2
            ZO = pp_tc * 4 * SCAN_TC

            def z1a(e, db=db, HSO=HSO):
                return e.tensor_tensor(
                    zsc[:, :],
                    hsch[:, HSO + db * SCAN_TC:HSO + (db + 1) * SCAN_TC],
                    dd_sb[:, db:db + 1].broadcast_to([128, SCAN_TC]),
                    ALU.mult)
            g.op("g", z1a, reads=[("hsch", pp_tc), ("w6", "dd")],
                 writes=[("zsc",)], label=f"z1a{tc}_{db}")

            def z1b(e, par=par):
                return e.tensor_tensor(
                    zsc2[:, :], zsc[:, :],
                    ybuf[:, par * SCAN_TC:(par + 1) * SCAN_TC], ALU.add)
            g.op("v", z1b, reads=[("zsc",), ("yb", par)],
                 writes=[("zsc2",)], label=f"z1b{tc}_{db}")

            def z2(e, db=db, HSO=HSO, ZO=ZO):
                return e.tensor_tensor(
                    zb[:, ZO + db * SCAN_TC:ZO + (db + 1) * SCAN_TC],
                    zsc2[:, :],
                    sgch[:, HSO + db * SCAN_TC:HSO + (db + 1) * SCAN_TC],
                    ALU.mult)
            g.op("v", z2, reads=[("zsc2",), ("sgch", pp_tc)],
                 writes=[("zb", pp_tc, db)], label=f"z2_{tc}_{db}")
            if db == NDB - 1:
                emit_oproj(tc)

        def emit_oproj(tc):
            pp_tc = tc % 2
            ZO = pp_tc * 4 * SCAN_TC
            for t1 in range(SCAN_TC // 128):
                def omm(e, t1=t1, ZO=ZO):
                    last = None
                    for db in range(4):
                        for oc in range(4):
                            last = e.matmul(
                                psA[oc][:, :],
                                zb[:, ZO + db * SCAN_TC + t1 * 128:
                                   ZO + db * SCAN_TC + (t1 + 1) * 128],
                                ow_sb[:, db * D + oc * 512:
                                      db * D + oc * 512 + 512],
                                start=(db == 0), stop=(db == 3))
                    return last
                g.op("pe", omm,
                     reads=[("zb", pp_tc, db) for db in range(4)]
                     + [("w6", "ow")],
                     writes=[("psA", 0), ("psA", 1), ("psA", 2), ("psA", 3)],
                     label=f"omm{tc}_{t1}")
                for oc in range(4):
                    def oev(e, oc=oc):
                        return e.activation(
                            obuf[:, oc * 512:(oc + 1) * 512], psA[oc][:, :],
                            AF.Copy)
                    g.op("a", oev, reads=[("psA", oc)], writes=[("obuf", oc)],
                         label=f"oev{tc}_{t1}_{oc}")
                g.dma("sy", lambda e, tc=tc, t1=t1: e.dma_start(
                    out=rs_in.ap()[tc * SCAN_TC + t1 * 128:
                                   tc * SCAN_TC + (t1 + 1) * 128, :],
                    in_=obuf[:, :]),
                    reads=[("obuf", oc) for oc in range(4)],
                    writes=[("rsw", tc, t1)], label=f"ost{tc}_{t1}")

        emit_ldts(0)
        emit_ldbig(0)
        U = N_TC * NDB
        for r in range(U + 5):
            if 0 <= r - 2 < U:
                stB(r - 2)
            if 0 <= r - 4 < U:
                stC_cmu(r - 4)
            if 0 <= r - 4 < U:
                stD1(r - 4)
            if r < U:
                stA(r)
            if 0 <= r - 3 < U:
                stC_scn(r - 3)
            if 0 <= r - 3 < U:
                stC2(r - 3)
            if 0 <= r - 4 < U:
                stD2(r - 4)
            if 0 <= r - 1 < U:
                stA2(r - 1)
        g.cc(lambda e: e.collective_compute(
            "ReduceScatter", ALU.add, replica_groups=[CORE_IDS],
            ins=[rs_in.ap()], outs=[rs_out.ap()]),
            reads=[("rsw", tc, t1) for tc in range(N_TC)
                   for t1 in range(SCAN_TC // 128)],
            writes=[("dr", "rs_out")], label="rsop")
        if debug:
            g.dma("sy", lambda e: e.dma_start(
                out=dmp["d_rs"].ap(), in_=rs_out.ap()),
                reads=[("dr", "rs_out")], label="dump_rs")
        st6.close()
        g.fence()

        st7 = contextlib.ExitStack()
        x4buf = sb("x4buf", [128, NTT * D], st=st7)
        ob_sb = sb("ob_sb", [128, D], st=st7)
        x3t = sb("x3t", [128, D], st=st7)
        rst = sb("rst", [128, D], st=st7)
        g.dma("sy", lambda e: e.dma_start(out=ob_sb[:, :], in_=outb_rep.ap()),
              writes=[("ob",)], label="ldob")
        for tt in range(NTT):
            g.dma("sy", lambda e, tt=tt: e.dma_start(
                out=x3t[:, :],
                in_=x3_d.ap()[tt * 128:(tt + 1) * 128, :]),
                reads=[("dr", "x3_d")], writes=[("x3t",)],
                label=f"ldx3_{tt}")
            g.dma("sy", lambda e, tt=tt: e.dma_start(
                out=rst[:, :],
                in_=rs_out.ap()[tt * 128:(tt + 1) * 128, :]),
                reads=[("dr", "rs_out")], writes=[("rstt",)],
                label=f"ldrs_{tt}")

            def a1(e, tt=tt):
                return e.tensor_tensor(
                    x4buf[:, tt * D:(tt + 1) * D],
                    x3t[:, :], rst[:, :], ALU.add)
            g.op("v", a1, reads=[("x3t",), ("rstt",)],
                 writes=[("x4a", tt)], label=f"x4a{tt}")

            def a2(e, tt=tt):
                return e.tensor_tensor(
                    x4buf[:, tt * D:(tt + 1) * D],
                    x4buf[:, tt * D:(tt + 1) * D], ob_sb[:, :], ALU.add)
            g.op("v", a2, reads=[("x4a", tt), ("ob",)],
                 writes=[("row", "x4", tt)], label=f"x4b{tt}")
        build_ffn(x4buf, "x4", "x5", out_rows)
        st7.close()

        g.op("sy", lambda e: e.nop(),
             reads=[("outst", tt) for tt in range(NTT)], label="final")

        g.plan()
        build_program._last_g = g
        g.emit(nc, semmap)

    return nc


def _prep_inputs(inputs):
    x = np.asarray(inputs["L0_attn_a_x"])
    wqkv = np.asarray(inputs["L0_attn_a_Wqkv_w"])
    bqkv = np.asarray(inputs["L0_attn_a_Wqkv_b"])
    in_w = np.asarray(inputs["L0_attn_b_in_w"])
    in_b = np.asarray(inputs["L0_attn_b_in_b"])
    conv_w = np.asarray(inputs["L0_attn_b_conv_w"])
    conv_b = np.asarray(inputs["L0_attn_b_conv_b"])
    xproj_w = np.asarray(inputs["L0_attn_b_xproj_w"])
    dt_w = np.asarray(inputs["L0_attn_b_dt_w"])
    dt_b = np.asarray(inputs["L0_attn_b_dt_b"])
    a_log = np.asarray(inputs["L0_attn_b_A_log"])
    d_ssm = np.asarray(inputs["L0_attn_b_D"])
    out_w = np.asarray(inputs["L0_attn_b_out_w"])
    out_b = np.asarray(inputs["L0_attn_b_out_b"])
    w_base = np.asarray(inputs["L0_ffn_W_base"])
    b_base = np.asarray(inputs["L0_ffn_b_base"])
    router_w = np.asarray(inputs["L0_ffn_router_W"])
    a_lora = np.asarray(inputs["L0_ffn_A"])
    bw_lora = np.asarray(inputs["L0_ffn_Bw"])

    C = np.ascontiguousarray

    inv = (1.0 / 10000.0 ** (np.arange(0, 32, 2, dtype=np.float32) / 32.0))
    t = np.arange(T, dtype=np.float32)
    fr = np.outer(t, inv)
    cosg = np.tile(np.cos(fr).T.astype(np.float32), (8, 1))
    sing = np.tile(np.sin(fr).T.astype(np.float32), (8, 1))
    for base in (0, 64):
        for u in (0, 2):
            cosg[base + u * 16: base + (u + 1) * 16] *= np.float32(SCALE)
            sing[base + u * 16: base + (u + 1) * 16] *= np.float32(SCALE)
    idb = np.eye(128, dtype=np.float32).astype(BF16NP)
    idf = np.eye(128, dtype=np.float32)
    trm = np.triu(np.full((128, 128), -10000.0, np.float32), 1).astype(BF16NP)
    ones1 = np.ones((1, 128), np.float32)
    wbT = C(w_base.T)
    bbase_rep = np.tile(b_base[None, :], (128, 1)).astype(np.float32)
    routerT = C(router_w.T)
    aT = C(a_lora.reshape(128, D).T)
    bwT = C(np.transpose(bw_lora, (0, 2, 1)).reshape(128, D))
    outb_rep = np.tile(out_b[None, :], (128, 1)).astype(np.float32)

    maps = []
    for c in range(NCORES):
        m = {}
        m["x_rows"] = C(x[0, c * TPC:(c + 1) * TPC, :])
        m["cid_in"] = np.array([[c]], np.int32)
        h0, h1 = 2 * c, 2 * c + 1
        unit_list = [("q", h0), ("k", h0), ("q", h1), ("k", h1)]

        def urows(kind, hh):
            base = (0 if kind == "q" else D) + hh * 128
            return base
        rot_idx = []
        for part in range(2):
            for kind, hh in unit_list:
                b0 = urows(kind, hh) + part * 16
                rot_idx.extend(range(b0, b0 + 16))
        m["w_rot"] = C(wqkv[rot_idx, :].T)
        m["b_rot"] = C(bqkv[rot_idx].reshape(128, 1))
        wp = []
        bp = []
        for kind, hh in unit_list:
            b0 = urows(kind, hh)
            wp.append(wqkv[b0 + 32:b0 + 128, :])
            bp.append(bqkv[b0 + 32:b0 + 128])
        m["w_pass"] = C(np.concatenate(wp, axis=0).T)
        bps = np.stack(bp, axis=1)
        bps[:, 0] *= np.float32(SCALE)
        bps[:, 2] *= np.float32(SCALE)
        m["b_pass"] = C(bps)
        vrows = list(range(2 * D + h0 * 128, 2 * D + h0 * 128 + 128)) + \
            list(range(2 * D + h1 * 128, 2 * D + h1 * 128 + 128))
        m["w_v"] = C(wqkv[vrows, :].T)
        m["b_v"] = C(bqkv[vrows].reshape(2, 128).T)
        perm = np.zeros((128, 128), np.float32)
        for u in range(4):
            for j_ in range(16):
                perm[u * 16 + j_, u * 32 + j_] = 1.0
                perm[64 + u * 16 + j_, u * 32 + 16 + j_] = 1.0
        m["perm_all"] = perm.astype(BF16NP)
        m["cosg"] = cosg
        m["sing"] = sing
        m["wbT"] = wbT
        m["bbase_rep"] = bbase_rep
        m["routerT"] = routerT
        m["aT"] = aT
        m["bwT"] = bwT
        s = slice(c * DIPC, (c + 1) * DIPC)
        m["inwT_sl"] = C(np.concatenate(
            [in_w[c * DIPC:(c + 1) * DIPC, :],
             in_w[DI + c * DIPC:DI + (c + 1) * DIPC, :]], axis=0).T)
        m["inb_sl"] = C(np.concatenate(
            [in_b[c * DIPC:(c + 1) * DIPC],
             in_b[DI + c * DIPC:DI + (c + 1) * DIPC]]).reshape(8, 128).T)
        m["convw_sl"] = C(conv_w[s, 0, :].reshape(4, 128, 4).transpose(1, 0, 2).reshape(128, 16))
        m["convb_sl"] = C(conv_b[s].reshape(4, 128).T)
        m["xprojT_sl"] = C(xproj_w[:, s].T)
        m["dtwT_sl"] = C(dt_w[s, :].T)
        m["dtb_sl"] = C(dt_b[s].reshape(4, 128).T)
        m["alog_sl"] = C(a_log[s, :].reshape(4, 128, NST).transpose(1, 0, 2).reshape(128, 4 * NST))
        m["d_sl"] = C(d_ssm[s].reshape(4, 128).T)
        m["outwT_sl"] = C(out_w[:, s].T)
        m["outb_rep"] = outb_rep
        m["idb"] = idb
        m["idf"] = idf
        m["trm"] = trm
        m["ones1"] = ones1
        maps.append(m)
    return maps


_NC_CACHE = {}


def kernel(**inputs):
    import os
    debug = bool(int(os.environ.get("MK_DEBUG", "0")))
    maps = _prep_inputs(inputs)
    key = ("nc", debug)
    if key not in _NC_CACHE:
        _NC_CACHE[key] = build_program(debug=debug)
    nc = _NC_CACHE[key]
    res = run_bass_kernel_spmd(nc, maps, core_ids=CORE_IDS)
    if debug:
        kernel._dbg = res.results
    out = np.concatenate([res.results[c]["out_rows"] for c in range(NCORES)],
                         axis=0)
    return out.reshape(1, T, D).astype(np.float32)


# version 21
# speedup vs baseline: 1.0029x; 1.0029x over previous
import contextlib

import numpy as np
import ml_dtypes

import concourse.bass as bass
import concourse.mybir as mybir
from concourse.bass_utils import run_bass_kernel_spmd

F32 = mybir.dt.float32
BF16 = mybir.dt.bfloat16
I32 = mybir.dt.int32
AF = mybir.ActivationFunctionType
ALU = mybir.AluOpType
AX = mybir.AxisListType
BF16NP = ml_dtypes.bfloat16

NCORES = 8
T = 8192
D = 2048
TPC = T // NCORES
NTT = TPC // 128
DI = 4096
DIPC = DI // NCORES
NDB = DIPC // 128
NST = 16
RK = 128
EPS = 1e-5
SCALE = float(1.0 / np.sqrt(128.0))
QKV_SC = 1024
IP_SC = 1024
NSC = T // QKV_SC
SCAN_TC = 256
N_TC = T // SCAN_TC

ENGS = ("sy", "pe", "v", "a", "g")
BKNAME = {"sy": "sync", "pe": "tensor", "v": "vector", "a": "scalar",
          "g": "gpsimd"}
CORE_IDS = list(range(NCORES))


class Op:
    __slots__ = ("eng", "fn", "deps", "kind", "ndma", "needed", "ms", "waits",
                 "label", "dma_sem", "dma_val")

    def __init__(self, eng, fn, deps, kind, label, ndma=1):
        self.eng = eng
        self.fn = fn
        self.deps = deps
        self.kind = kind
        self.ndma = ndma
        self.needed = False
        self.ms = None
        self.waits = None
        self.label = label


N_DMASEM = 64


class G:

    def __init__(self):
        self.ops = []
        self.reg = {}
        self.last = {e: None for e in ENGS}
        self.dma_idx = 0
        self.dma_prev = [None] * N_DMASEM
        self.dma_cnt = [0] * N_DMASEM
        self.cur_dma_sem = None

    def op(self, eng, fn, reads=(), writes=(), kind="c", label="", ndma=1):
        deps = set()
        for k in reads:
            st = self.reg.get(k)
            if st is not None and st[0] is not None:
                deps.add(st[0])
        for k in writes:
            st = self.reg.get(k)
            if st is not None:
                deps.update(st[1])
                if st[0] is not None:
                    deps.add(st[0])
        o = Op(eng, fn, deps, kind, label, ndma)
        if kind == "d":
            si = self.dma_idx % N_DMASEM
            self.dma_idx += 1
            o.dma_sem = si
            if self.dma_prev[si] is not None:
                deps.add(self.dma_prev[si])
            self.dma_prev[si] = o
            self.dma_cnt[si] += 16 * ndma
            o.dma_val = self.dma_cnt[si]
        self.ops.append(o)
        for k in reads:
            self.reg.setdefault(k, [None, []])[1].append(o)
        for k in writes:
            self.reg[k] = [o, []]
        self.last[eng] = o
        return o

    def dma(self, eng, fn, reads=(), writes=(), label=""):
        return self.op(eng, fn, reads, writes, kind="d", label=label)

    def dmam(self, eng, fn, n, reads=(), writes=(), label=""):
        return self.op(eng, fn, reads, writes, kind="d", label=label, ndma=n)

    def cc(self, fn, reads=(), writes=(), label=""):
        return self.op("g", fn, reads, writes, kind="cc", label=label)

    def fence(self):
        lasts = set(o for o in self.last.values() if o is not None)
        for e in ENGS:
            o = Op(e, (lambda eng: eng.nop()), set(lasts), "c", "fence")
            self.ops.append(o)
            self.last[e] = o
        self.reg.clear()

    def plan(self):
        for o in self.ops:
            for d in o.deps:
                d.needed = True
        cnt = {}
        for o in self.ops:
            if o.kind == "d":
                o.ms = (("ds", o.dma_sem), o.dma_val)
            elif o.kind == "cc":
                cnt["cc"] = cnt.get("cc", 0) + 1
                o.ms = ("cc", cnt["cc"])
            elif o.needed:
                key = ("c", o.eng)
                cnt[key] = cnt.get(key, 0) + 1
                o.ms = (key, cnt[key])
        waited = {e: {} for e in ENGS}
        for o in self.ops:
            w = {}
            for d in o.deps:
                if d.ms is None:
                    continue
                k, v = d.ms
                if w.get(k, 0) < v:
                    w[k] = v
            W = waited[o.eng]
            o.waits = []
            for k, v in sorted(w.items(), key=lambda kv: str(kv[0])):
                if W.get(k, 0) < v:
                    o.waits.append((k, v))
                    W[k] = v

    def emit(self, nc, semmap):
        with nc.Block() as bk:
            def mk(ename):
                def body(eng):
                    for o in self.ops:
                        if o.eng != ename:
                            continue
                        for k, v in o.waits:
                            eng.wait_ge(semmap[k], v)
                        if o.kind == "d":
                            self.cur_dma_sem = semmap[("ds", o.dma_sem)]
                        inst = o.fn(eng)
                        if o.kind == "d":
                            inst.then_inc(semmap[("ds", o.dma_sem)], 16)
                        elif o.kind == "cc":
                            inst.then_inc(semmap["cc"], 1)
                        elif o.needed:
                            inst.then_inc(semmap[("c", ename)], 1)
                return body
            for ename in ENGS:
                getattr(bk, BKNAME[ename])(mk(ename))


def build_program(debug=False):
    nc = bass.Bass("TRN2", target_bir_lowering=False, num_devices=NCORES)
    g = G()

    def inp(name, shape, dt=F32):
        return nc.dram_tensor(name, shape, dt, kind="ExternalInput")

    x_rows = inp("x_rows", [TPC, D])
    cid_in = inp("cid_in", [1, 1], I32)
    w_rot = inp("w_rot", [D, 128])
    b_rot = inp("b_rot", [128, 1])
    w_pass = inp("w_pass", [D, 384])
    b_pass = inp("b_pass", [96, 4])
    w_v = inp("w_v", [D, 256])
    b_v = inp("b_v", [128, 2])
    perm_all = inp("perm_all", [128, 128], BF16)
    cosg = inp("cosg", [128, T])
    sing = inp("sing", [128, T])
    wbT = inp("wbT", [D, D])
    bbase_rep = inp("bbase_rep", [128, D])
    routerT = inp("routerT", [D, 8])
    aT = inp("aT", [D, 128])
    bwT = inp("bwT", [128, D])
    inwT_sl = inp("inwT_sl", [D, 1024])
    inb_sl = inp("inb_sl", [128, 8])
    convw_sl = inp("convw_sl", [128, 16])
    convb_sl = inp("convb_sl", [128, 4])
    xprojT_sl = inp("xprojT_sl", [DIPC, 160])
    dtwT_sl = inp("dtwT_sl", [RK, DIPC])
    dtb_sl = inp("dtb_sl", [128, 4])
    alog_sl = inp("alog_sl", [128, 4 * NST])
    d_sl = inp("d_sl", [128, 4])
    outwT_sl = inp("outwT_sl", [DIPC, D])
    outb_rep = inp("outb_rep", [128, D])
    idb = inp("idb", [128, 128], BF16)
    idf = inp("idf", [128, 128])
    trm = inp("trm", [128, 128], BF16)
    ones1 = inp("ones1", [1, 128])

    out_rows = nc.dram_tensor("out_rows", [TPC, D], F32, kind="ExternalOutput")
    dmp = {}
    if debug:
        for nm, shape, dt in (("d_xnf", [D, TPC], BF16),
                              ("d_ao", [T, 256], BF16),
                              ("d_x3", [TPC, D], F32),
                              ("d_sp", [160, T], F32),
                              ("d_hs", [DIPC, T], BF16),
                              ("d_sg", [DIPC, T], BF16),
                              ("d_rs", [TPC, D], F32),
                              ("d_x2", [TPC, D], F32),
                              ("d_fb", [128, D], F32),
                              ("d_lg", [128, 8], F32),
                              ("d_hm", [128, 128], F32),
                              ("d_xn2", [128, D], BF16),
                              ("d_xnT", [128, 16 * 128], BF16)):
            dmp[nm] = nc.dram_tensor(nm, shape, dt, kind="ExternalOutput")

    def dram(name, shape, dt=F32):
        return nc.dram_tensor(name, shape, dt, kind="Internal")

    xn_c = dram("xn_c", [D, TPC], BF16)
    xn_full = dram("xn_full", [NCORES * D, TPC], BF16)
    ao_c = dram("ao_c", [T, 256], BF16)
    ao_full = dram("ao_full", [NCORES * T, 256], BF16)
    sp_c = dram("sp_c", [160, T])
    sp_full = dram("sp_full", [160, T])
    hsb_d = dram("hsb_d", [DIPC, T], BF16)
    sgb_d = dram("sgb_d", [DIPC, T], BF16)
    x3_d = dram("x3_d", [TPC, D])
    rs_in = dram("rs_in", [T, D])
    rs_out = dram("rs_out", [TPC, D])

    stk = contextlib.ExitStack()
    with stk:
        def sb(name, shape, dt=F32, st=None):
            return (st or stk).enter_context(nc.sbuf_tensor(name, shape, dt))

        psA = [stk.enter_context(nc.psum_tensor(f"psA{i}", [128, 512], F32))
               for i in range(4)]
        psB = [stk.enter_context(nc.psum_tensor(f"psB{i}", [128, 512], F32))
               for i in range(2)]
        psM = stk.enter_context(nc.psum_tensor("psM", [128, 512], F32))
        psT = stk.enter_context(nc.psum_tensor("psT", [128, 1024], BF16))

        c_idb = sb("c_idb", [128, 128], BF16)
        c_idf = sb("c_idf", [128, 128])
        c_trm = sb("c_trm", [128, 128], BF16)
        c_ones = sb("c_ones", [1, 128])
        c_onesb = sb("c_onesb", [1, 128], BF16)
        c_cid = sb("c_cid", [1, 1], I32)
        lnsq = sb("lnsq", [128, D])
        lnst = sb("lnst", [128, 16])

        semmap = {}
        for e in ENGS:
            semmap[("c", e)] = stk.enter_context(nc.semaphore(f"sC{e}"))
        for i in range(N_DMASEM):
            semmap[("ds", i)] = stk.enter_context(nc.semaphore(f"sDp{i}"))
        semmap["cc"] = stk.enter_context(nc.semaphore("sCC"))

        g.dma("g", lambda e: e.dma_start(out=c_onesb[:, :], in_=ones1.ap()),
              writes=[("const", "onesb")], label="ldc_onesb")
        for t_, s_, n_ in ((idb, c_idb, "idb"), (idf, c_idf, "idf"),
                           (trm, c_trm, "trm"),
                           (ones1, c_ones, "ones"), (cid_in, c_cid, "cid")):
            g.dma("sy", lambda e, t=t_, s=s_: e.dma_start(out=s[:, :],
                                                          in_=t.ap()),
                  writes=[("const", n_)], label=f"ldc_{n_}")

        def emit_ln(xap, rkey, out_ap, okey, tag):
            g.op("a", lambda e: e.activation(lnsq[:, :], xap, AF.Square),
                 reads=[rkey], writes=[("lnsq",)], label=f"lnsq{tag}")
            g.op("v", lambda e: e.tensor_reduce(
                lnst[:, 0:1], xap, AX.X, ALU.add, negate=True),
                reads=[rkey], writes=[("lt0",)], label=f"lns0{tag}")
            g.op("v", lambda e: e.tensor_reduce(
                lnst[:, 1:2], lnsq[:, :], AX.X, ALU.add),
                reads=[("lnsq",)], writes=[("lt1",)], label=f"lns1{tag}")
            g.op("v", lambda e: e.tensor_scalar(
                lnst[:, 2:3], lnst[:, 0:1], 1.0 / D, None, ALU.mult),
                reads=[("lt0",)], writes=[("lt2",)], label=f"lnm{tag}")
            g.op("v", lambda e: e.tensor_scalar(
                lnst[:, 3:4], lnst[:, 1:2], 1.0 / D, EPS, ALU.mult, ALU.add),
                reads=[("lt1",)], writes=[("lt3",)], label=f"lne{tag}")
            g.op("v", lambda e: e.tensor_tensor(
                lnst[:, 4:5], lnst[:, 2:3], lnst[:, 2:3], ALU.mult),
                reads=[("lt2",)], writes=[("lt4",)], label=f"lnm2{tag}")
            g.op("v", lambda e: e.tensor_tensor(
                lnst[:, 5:6], lnst[:, 3:4], lnst[:, 4:5], ALU.subtract),
                reads=[("lt3",), ("lt4",)], writes=[("lt5",)],
                label=f"lnv{tag}")
            g.op("a", lambda e: e.activation(lnst[:, 6:7], lnst[:, 5:6],
                                             AF.Ln),
                 reads=[("lt5",)], writes=[("lt6",)], label=f"lnl{tag}")
            g.op("a", lambda e: e.activation(lnst[:, 7:8], lnst[:, 6:7],
                                             AF.Exp, scale=-0.5),
                 reads=[("lt6",)], writes=[("lt7",)], label=f"lnr{tag}")
            g.op("v", lambda e: e.tensor_tensor(
                lnst[:, 8:9], lnst[:, 2:3], lnst[:, 7:8], ALU.mult),
                reads=[("lt2",), ("lt7",)], writes=[("lt8",)],
                label=f"lnb{tag}")
            return g.op("a", lambda e: e.activation(
                out_ap, xap, AF.Identity, bias=lnst[:, 8:9], scale=lnst[:, 7:8]),
                reads=[rkey, ("lt7",), ("lt8",)], writes=[okey],
                label=f"lnx{tag}")

        def emit_xnT_phase(rowbuf, rowtag, tag, xn_t, xnt_t):
            for tt in range(NTT):
                rb = rowbuf[:, tt * D:(tt + 1) * D]
                emit_ln(rb, ("row", rowtag, tt), xn_t[:, :], ("xnt_x", tag),
                        f"{tag}_{tt}")
                for half in range(2):
                    def tp(e, half=half):
                        last = None
                        for i in range(8):
                            et = half * 8 + i
                            last = e.transpose(
                                psT[:, i * 128:(i + 1) * 128],
                                xn_t[:, et * 128:(et + 1) * 128], c_idb[:, :])
                        return last
                    g.op("pe", tp, reads=[("xnt_x", tag), ("const", "idb")],
                         writes=[("psT",)], label=f"xtp{tag}{tt}{half}")

                    def ev(e, tt=tt, half=half):
                        return e.tensor_copy(
                            xnt_t[:, :].rearrange("p (et t) -> p et t", et=16)
                            [:, half * 8:(half + 1) * 8,
                             tt * 128:(tt + 1) * 128],
                            psT[:, :].rearrange("p (i t) -> p i t", i=8))
                    g.op("v", ev, reads=[("psT",)],
                         writes=[("xnt_o", tag, tt, half)],
                         label=f"xev{tag}{tt}{half}")
            g.dma("sy", lambda e: e.dma_start(
                out=xn_c.ap().rearrange("(et p) t -> p et t", p=128),
                in_=xnt_t[:, :].rearrange("p (et t) -> p et t", et=16)),
                reads=[("xnt_o", tag, tt, h) for tt in range(NTT)
                       for h in range(2)],
                writes=[("dr", "xn_c")], label=f"xst{tag}")
            g.cc(lambda e: e.collective_compute(
                "AllGather", ALU.bypass, replica_groups=[CORE_IDS],
                ins=[xn_c.ap()], outs=[xn_full.ap()]),
                reads=[("dr", "xn_c")], writes=[("dr", "xn_full")],
                label=f"ag{tag}")

        st0 = contextlib.ExitStack()
        xrow0 = sb("xrow0", [128, NTT * D], st=st0)
        xnsc0 = sb("xnsc0", [128, D], BF16, st=st0)
        xnts0 = sb("xnts0", [128, NTT * D], BF16, st=st0)
        g.dma("sy", lambda e: e.dma_start(
            out=xrow0[:, :].rearrange("p (t d) -> p t d", t=NTT),
            in_=x_rows.ap().rearrange("(t p) d -> p t d", p=128)),
            writes=[("row", "x1", tt) for tt in range(NTT)], label="ldx")
        emit_xnT_phase(xrow0, "x1", "x1", xnsc0, xnts0)
        if debug:
            g.dma("sy", lambda e: e.dma_start(
                out=dmp["d_xnf"].ap(), in_=xn_full.ap()[0:D, :]),
                reads=[("dr", "xn_full")], label="dump_xnf")
        st0.close()
        g.fence()

        stA = contextlib.ExitStack()
        units = [sb(f"unit{u}", [128, T], BF16, st=stA) for u in range(4)]
        vtm = sb("vtm", [128, 64 * 256], BF16, st=stA)
        vtmv = vtm[:, :].rearrange("p (tt h d) -> p tt h d", tt=64, h=2)

        stP1 = contextlib.ExitStack()
        wrot_sb = sb("wrot_sb", [128, 16 * 128], BF16, st=stP1)
        wpass_sb = sb("wpass_sb", [128, 16 * 384], BF16, st=stP1)
        wv_sb = sb("wv_sb", [128, 16 * 256], BF16, st=stP1)
        brot_sb = sb("brot_sb", [128, 1], st=stP1)
        bpass_sb = sb("bpass_sb", [96, 4], st=stP1)
        bv_sb = sb("bv_sb", [128, 2], st=stP1)
        perm_sb = sb("perm_sb", [128, 128], BF16, st=stP1)
        xnq = sb("xnq", [128, 16 * QKV_SC], BF16, st=stP1)
        rotg = sb("rotg", [128, QKV_SC], st=stP1)
        r12b = sb("r12b", [128, QKV_SC], BF16, st=stP1)
        rs1 = sb("rs1", [64, QKV_SC], st=stP1)
        rs2 = sb("rs2", [64, QKV_SC], st=stP1)
        csc = sb("csc", [128, QKV_SC], st=stP1)
        css = sb("css", [128, QKV_SC], st=stP1)
        vfm = sb("vfm", [128, QKV_SC], BF16, st=stP1)

        for t_, s_, n_ in ((w_rot, wrot_sb, 128), (w_pass, wpass_sb, 384),
                           (w_v, wv_sb, 256)):
            g.dma("g", lambda e, t=t_, s=s_, n=n_: e.dma_start(
                out=s[:, :].rearrange("p (et o) -> p et o", et=16),
                in_=t.ap().rearrange("(et p) o -> p et o", p=128)),
                writes=[("w", t_.name)], label=f"ldw_{t_.name}")
        g.dma("g", lambda e: e.dma_start(out=perm_sb[:, :],
                                         in_=perm_all.ap()),
              writes=[("w", "perm")], label="ldperm")
        for t_, s_ in ((b_rot, brot_sb), (b_pass, bpass_sb), (b_v, bv_sb)):
            g.dma("sy", lambda e, t=t_, s=s_: e.dma_start(out=s[:, :],
                                                          in_=t.ap()),
                  writes=[("w", t_.name)], label=f"ldb_{t_.name}")

        for sc in range(NSC):
            w0 = sc * QKV_SC
            g.dma("sy", lambda e, sc=sc: e.dma_start(
                out=xnq[:, :].rearrange("p (et t) -> p et t", et=16),
                in_=xn_full.ap().rearrange("(r et p) t -> r p et t",
                                           p=128, et=16)[sc]),
                reads=[("dr", "xn_full")], writes=[("xnq",)],
                label=f"ldxnq{sc}")
            g.dma("sy", lambda e, w0=w0: e.dma_start(
                out=csc[:, :], in_=cosg.ap()[:, w0:w0 + QKV_SC]),
                writes=[("csc",)], label=f"ldcos{sc}")
            g.dma("sy", lambda e, w0=w0: e.dma_start(
                out=css[:, :], in_=sing.ap()[:, w0:w0 + QKV_SC]),
                writes=[("css",)], label=f"ldsin{sc}")
            for fw in range(2):
                def mm(e, fw=fw):
                    last = None
                    for et in range(16):
                        last = e.matmul(
                            psA[fw][:, :],
                            wrot_sb[:, et * 128:(et + 1) * 128],
                            xnq[:, et * QKV_SC + fw * 512:
                                et * QKV_SC + fw * 512 + 512],
                            start=(et == 0), stop=(et == 15))
                    return last
                g.op("pe", mm, reads=[("xnq",), ("w", "w_rot")],
                     writes=[("psA", fw)], label=f"rmm{sc}{fw}")
                g.op("a", lambda e, fw=fw: e.activation(
                    rotg[:, fw * 512:fw * 512 + 512], psA[fw][:, :],
                    AF.Identity, bias=brot_sb[:, 0:1]),
                    reads=[("psA", fw), ("w", "b_rot")], writes=[("rotg",)],
                    label=f"rev{sc}{fw}")
            g.op("v", lambda e: e.tensor_tensor(
                rs1[:, :], rotg[0:64, :], csc[0:64, :], ALU.mult),
                reads=[("rotg",), ("csc",)], writes=[("rs1",)],
                label=f"rp1_{sc}")
            g.op("v", lambda e: e.tensor_tensor(
                rs2[:, :], rotg[64:128, :], css[64:128, :], ALU.mult),
                reads=[("rotg",), ("css",)], writes=[("rs2",)],
                label=f"rp2_{sc}")
            g.op("v", lambda e: e.tensor_tensor(
                r12b[0:64, :], rs1[:, :], rs2[:, :], ALU.subtract),
                reads=[("rs1",), ("rs2",)], writes=[("r12a",)],
                label=f"rp3_{sc}")
            g.op("v", lambda e: e.tensor_tensor(
                rs1[:, :], rotg[0:64, :], css[0:64, :], ALU.mult),
                reads=[("rotg",), ("css",)], writes=[("rs1",)],
                label=f"rp4_{sc}")
            g.op("v", lambda e: e.tensor_tensor(
                rs2[:, :], rotg[64:128, :], csc[64:128, :], ALU.mult),
                reads=[("rotg",), ("csc",)], writes=[("rs2",)],
                label=f"rp5_{sc}")
            g.op("v", lambda e: e.tensor_tensor(
                r12b[64:128, :], rs1[:, :], rs2[:, :], ALU.add),
                reads=[("rs1",), ("rs2",)], writes=[("r12b",)],
                label=f"rp6_{sc}")
            for u in range(4):
                for fw in range(2):
                    def pmm(e, u=u, fw=fw):
                        return e.matmul(
                            psM[0:32, :],
                            perm_sb[:, u * 32:(u + 1) * 32],
                            r12b[:, fw * 512:fw * 512 + 512],
                            start=True, stop=True)
                    g.op("pe", pmm,
                         reads=[("r12a",), ("r12b",), ("w", "perm")],
                         writes=[("psM",)], label=f"pmm{sc}{u}{fw}")

                    def pev(e, u=u, fw=fw, w0=w0):
                        return e.tensor_copy(
                            units[u][96:128,
                                     w0 + fw * 512:w0 + fw * 512 + 512],
                            psM[0:32, :])
                    g.op("v", pev, reads=[("psM",)], writes=[("un", u, sc)],
                         label=f"pev{sc}{u}{fw}")
            for u in range(4):
                for fw in range(2):
                    def mm(e, u=u, fw=fw):
                        last = None
                        for et in range(16):
                            last = e.matmul(
                                psA[2 + fw][0:96, :],
                                wpass_sb[:, et * 384 + u * 96:
                                         et * 384 + (u + 1) * 96],
                                xnq[:, et * QKV_SC + fw * 512:
                                    et * QKV_SC + fw * 512 + 512],
                                start=(et == 0), stop=(et == 15))
                        return last
                    g.op("pe", mm, reads=[("xnq",), ("w", "w_pass")],
                         writes=[("psA", 2 + fw)], label=f"qmm{sc}{u}{fw}")

                    def qev(e, u=u, fw=fw, w0=w0):
                        return e.activation(
                            units[u][0:96, w0 + fw * 512:w0 + fw * 512 + 512],
                            psA[2 + fw][0:96, :], AF.Identity,
                            bias=bpass_sb[:, u:u + 1],
                            scale=SCALE if u in (0, 2) else 1.0)
                    g.op("a", qev, reads=[("psA", 2 + fw), ("w", "b_pass")],
                         writes=[("un", u, sc)], label=f"qev{sc}{u}{fw}")
            for vh in range(2):
                for fw in range(2):
                    def mm(e, vh=vh, fw=fw):
                        last = None
                        for et in range(16):
                            last = e.matmul(
                                psB[fw][:, :],
                                wv_sb[:, et * 256 + vh * 128:
                                      et * 256 + (vh + 1) * 128],
                                xnq[:, et * QKV_SC + fw * 512:
                                    et * QKV_SC + fw * 512 + 512],
                                start=(et == 0), stop=(et == 15))
                        return last
                    g.op("pe", mm, reads=[("xnq",), ("w", "w_v")],
                         writes=[("psB", fw)], label=f"vmm{sc}{vh}{fw}")

                    def vev(e, vh=vh, fw=fw):
                        return e.activation(
                            vfm[:, fw * 512:fw * 512 + 512], psB[fw][:, :],
                            AF.Identity, bias=bv_sb[:, vh:vh + 1])
                    g.op("a", vev, reads=[("psB", fw), ("w", "b_v")],
                         writes=[("vfm",)], label=f"vev{sc}{vh}{fw}")

                def vtp(e):
                    last = None
                    for i in range(8):
                        last = e.transpose(
                            psT[:, i * 128:(i + 1) * 128],
                            vfm[:, i * 128:(i + 1) * 128], c_idb[:, :])
                    return last
                g.op("pe", vtp, reads=[("vfm",), ("const", "idb")],
                     writes=[("psT",)], label=f"vtp{sc}{vh}")

                def vte(e, vh=vh, sc=sc):
                    return e.tensor_copy(
                        vtmv[:, sc * 8:(sc + 1) * 8, vh, :],
                        psT[:, :].rearrange("p (i d) -> p i d", i=8))
                g.op("v", vte, reads=[("psT",)], writes=[("vt", vh, sc)],
                     label=f"vte{sc}{vh}")
        stP1.close()

        stP2 = contextlib.ExitStack()
        p2rs = [sb(f"p2r{i}", [128, 2 * T], BF16, st=stP2) for i in range(2)]
        aosg = [sb(f"aosg{i}", [128, 256], BF16, st=stP2)
                for i in range(2)]
        ptb = [sb(f"ptb{i}", [128, 1024], BF16, st=stP2) for i in range(2)]
        avb = sb("avb", [128, 256], st=stP2)
        nmx = sb("nmx", [128, 64], st=stP2)
        ssum = sb("ssum", [128, 64], st=stP2)
        fbuf = sb("fbuf", [128, 64], st=stP2)
        wtm = sb("wtm", [128, 64], st=stP2)
        nmn = sb("nmn", [128, 4], st=stP2)
        rsum = sb("rsum", [128, 4], st=stP2)
        rrec = sb("rrec", [128, 4], st=stP2)

        un_reads = [("un", u, sc) for u in range(4) for sc in range(NSC)]
        vt_reads = [("vt", vh, sc) for vh in range(2) for sc in range(NSC)]

        SBK = [("psB", 0), ("psB", 1), ("psA", 0), ("psA", 1)]
        SBT = [psB[0], psB[1], psA[0], psA[1]]

        def S_thunks(hh, g2, qt, pp):
            uq, uk = units[2 * hh], units[2 * hh + 1]
            j = 2 * g2 + qt
            kvlen = 128 * j + 128
            kvspan = 256 * g2 + 256
            nw = (kvlen + 511) // 512
            p2 = p2rs[pp]
            base = pp * 32 + qt * 16
            th = []
            if qt == 0 and kvlen < kvspan:
                def t_ms():
                    g.op("v", lambda e: e.memset(p2[:, kvlen:kvspan], 0),
                         reads=[], writes=[("p2", pp, 0)],
                         label=f"mst{hh}_{j}")
                th.append(t_ms)
            for w0_ in range(0, nw, 2):
                ws = [w_ for w_ in (w0_, w0_ + 1) if w_ < nw]

                def t_batch(ws=ws, j=j, kvlen=kvlen, nw=nw, base=base,
                            uq=uq, uk=uk, p2=p2, pp=pp, qt=qt, hh=hh):
                    bkeys = [SBK[w_ % 4] for w_ in ws]

                    def smm(e):
                        last = None
                        for w_ in ws:
                            fwid = min(512, kvlen - 512 * w_)
                            isl = w_ == nw - 1
                            last = e.matmul(
                                SBT[w_ % 4][:, :fwid],
                                uq[:, j * 128:(j + 1) * 128],
                                uk[:, w_ * 512:w_ * 512 + fwid],
                                start=True, stop=not isl)
                            if isl:
                                last = e.matmul(
                                    SBT[w_ % 4][:, fwid - 128:fwid],
                                    c_idb[:, :], c_trm[:, :],
                                    start=False, stop=True)
                        return last
                    g.op("pe", smm,
                         reads=un_reads + [("const", "idb"),
                                           ("const", "trm")],
                         writes=bkeys, label=f"smm{hh}_{j}_{ws[0]}")

                    def cmx(e):
                        last = None
                        for w_ in ws:
                            fwid = min(512, kvlen - 512 * w_)
                            last = e.tensor_reduce(
                                nmx[:, base + w_:base + w_ + 1],
                                SBT[w_ % 4][:, :fwid], AX.X, ALU.max,
                                negate=True)
                        return last
                    g.op("v", cmx, reads=bkeys,
                         writes=[("nmx", pp, qt, ws[0])],
                         label=f"cmx{hh}_{j}_{ws[0]}")

                    def exc(e):
                        last = None
                        for w_ in ws:
                            fwid = min(512, kvlen - 512 * w_)
                            last = e.activation(
                                p2[:, qt * T + w_ * 512:
                                   qt * T + w_ * 512 + fwid],
                                SBT[w_ % 4][:, :fwid], AF.Exp,
                                bias=nmx[:, base + w_:base + w_ + 1],
                                scale=1.0,
                                accum_out=ssum[:, base + w_:base + w_ + 1])
                        return last
                    g.op("a", exc, reads=bkeys + [("nmx", pp, qt, ws[0])],
                         writes=[("p2", pp, qt), ("ssum", pp, qt, ws[0])],
                         label=f"exc{hh}_{j}_{ws[0]}")
                th.append(t_batch)

            def t_fin(j=j, nw=nw, base=base, pp=pp, qt=qt, hh=hh, p2=p2,
                      kvlen=kvlen):
                rds = [("nmx", pp, qt, w0_) for w0_ in range(0, nw, 2)]
                srds = [("ssum", pp, qt, w0_) for w0_ in range(0, nw, 2)]
                g.op("v", lambda e: e.tensor_reduce(
                    nmn[:, 2 * pp + qt:2 * pp + qt + 1],
                    nmx[:, base:base + nw], AX.X, ALU.min),
                    reads=rds, writes=[("nmn", pp, qt)], label=f"nmn{hh}_{j}")
                g.op("a", lambda e: e.activation(
                    fbuf[:, base:base + nw], nmx[:, base:base + nw],
                    AF.Exp, bias=nmn[:, 2 * pp + qt:2 * pp + qt + 1],
                    scale=-1.0),
                    reads=rds + [("nmn", pp, qt)], writes=[("fct", pp, qt)],
                    label=f"fct{hh}_{j}")
                g.op("g", lambda e: e.tensor_tensor(
                    wtm[:, base:base + nw], fbuf[:, base:base + nw],
                    ssum[:, base:base + nw], ALU.mult),
                    reads=srds + [("fct", pp, qt)], writes=[("wtm", pp, qt)],
                    label=f"wtm{hh}_{j}")
                g.op("v", lambda e: e.tensor_reduce(
                    rsum[:, 2 * pp + qt:2 * pp + qt + 1],
                    wtm[:, base:base + nw], AX.X, ALU.add),
                    reads=[("wtm", pp, qt)], writes=[("rsm", pp, qt)],
                    label=f"wsm{hh}_{j}")
                for w0_ in range(0, nw, 2):
                    ws = [w_ for w_ in (w0_, w0_ + 1) if w_ < nw]

                    def rsc(e, ws=ws):
                        last = None
                        for w_ in ws:
                            fwid = min(512, kvlen - 512 * w_)
                            sl = p2[:, qt * T + w_ * 512:
                                    qt * T + w_ * 512 + fwid]
                            last = e.tensor_scalar(
                                sl, sl, fbuf[:, base + w_:base + w_ + 1],
                                None, ALU.mult)
                        return last
                    g.op("v", rsc, reads=[("fct", pp, qt), ("p2", pp, qt)],
                         writes=[("p2", pp, qt)],
                         label=f"rsc{hh}_{j}_{ws[0]}")
            th.append(t_fin)
            return th

        def B_thunks(hh, g2, pp):
            kvspan = 256 * g2 + 256
            nkc = kvspan // 128
            p2 = p2rs[pp]
            th = []

            def t_rec():
                g.op("v", lambda e: e.reciprocal(
                    rrec[:, 2 * pp:2 * pp + 2], rsum[:, 2 * pp:2 * pp + 2]),
                    reads=[("rsm", pp, 0), ("rsm", pp, 1)],
                    writes=[("rrec", pp)], label=f"rec{hh}_{g2}")
            th.append(t_rec)
            batches = [list(range(kb, min(kb + 2, nkc)))
                       for kb in range(0, nkc, 2)]
            nb = len(batches)

            psA2b = psA[2][:, :].bitcast(BF16)

            def mk_ptp(bi):
                kcs = batches[bi]
                ch = bi % 2
                tgt = psT if ch == 0 else psA2b
                tkey = ("psT",) if ch == 0 else ("psA", 2)

                def ptp(e, kcs=kcs, tgt=tgt):
                    last = None
                    for ii, kc in enumerate(kcs):
                        for qt in range(2):
                            last = e.transpose(
                                tgt[:, ii * 256 + qt * 128:
                                    ii * 256 + (qt + 1) * 128],
                                p2[:, qt * T + kc * 128:
                                   qt * T + kc * 128 + 128],
                                c_idb[:, :])
                    return last
                g.op("pe", ptp,
                     reads=[("p2", pp, 0), ("p2", pp, 1), ("const", "idb")],
                     writes=[tkey], label=f"ptp{hh}_{g2}_{bi}")

                peng = "v" if ch == 0 else "a"

                def pte(e, kcs=kcs, tgt=tgt, ch=ch, peng=peng):
                    if peng == "v":
                        return e.tensor_copy(
                            ptb[ch][:, :len(kcs) * 256],
                            tgt[:, :len(kcs) * 256])
                    return e.activation(
                        ptb[ch][:, :len(kcs) * 256],
                        tgt[:, :len(kcs) * 256], AF.Copy)
                g.op(peng, pte, reads=[tkey], writes=[("ptb", ch)],
                     label=f"pte{hh}_{g2}_{bi}")

            def mk_avm(bi):
                kcs = batches[bi]
                ch = bi % 2

                def avm(e, kcs=kcs, ch=ch, hh=hh, bi=bi, nb=nb):
                    last = None
                    for ii, kc in enumerate(kcs):
                        last = e.matmul(
                            psM[:, :256], vtmv[:, kc, hh, :],
                            ptb[ch][:, ii * 256:(ii + 1) * 256],
                            start=(bi == 0 and ii == 0),
                            stop=(bi == nb - 1 and ii == len(kcs) - 1))
                    return last
                g.op("pe", avm, reads=vt_reads + [("ptb", ch), ("psM",)],
                     writes=[("psM",)], label=f"avm{hh}_{g2}_{bi}")

            th.append(lambda: mk_ptp(0))
            for bi in range(nb):
                def t_step(bi=bi):
                    if bi + 1 < nb:
                        mk_ptp(bi + 1)
                    mk_avm(bi)
                th.append(t_step)

            def t_tail():
                g.op("v", lambda e: e.tensor_copy(avb[:, :], psM[:, :256]),
                     reads=[("psM",)], writes=[("avb",), ("psM",)],
                     label=f"ave{hh}_{g2}")
                for qt in range(2):
                    j = 2 * g2 + qt

                    def avt(e, qt=qt):
                        return e.transpose(
                            psM[:, :128], avb[:, qt * 128:(qt + 1) * 128],
                            c_idf[:, :])
                    g.op("pe", avt, reads=[("avb",), ("const", "idf")],
                         writes=[("psM",)], label=f"avt{hh}_{g2}_{qt}")

                    def aoe(e, qt=qt):
                        return e.activation(
                            aosg[pp][:, qt * 128:(qt + 1) * 128],
                            psM[:, :128], AF.Identity,
                            scale=rrec[:, 2 * pp + qt:2 * pp + qt + 1])
                    g.op("a", aoe, reads=[("psM",), ("rrec", pp)],
                         writes=[("aosg", pp, qt)], label=f"aoe{hh}_{g2}_{qt}")
                g.dma("sy", lambda e: e.dma_start(
                    out=ao_c.ap().rearrange("(jt p) hd -> p jt hd", p=128)
                    [:, 2 * g2:2 * g2 + 2, hh * 128:(hh + 1) * 128],
                    in_=aosg[pp][:, :].rearrange("p (jt d) -> p jt d", jt=2)),
                    reads=[("aosg", pp, 0), ("aosg", pp, 1)],
                    writes=[("dr", "aoc", hh, g2)], label=f"aost{hh}_{g2}")
            th.append(t_tail)
            return th

        units_list = [(hh, g2) for hh in range(2) for g2 in range(32)]
        prev_bt = []
        for i, (hh, g2) in enumerate(units_list):
            pp = i % 2
            th = S_thunks(hh, g2, 0, pp) + S_thunks(hh, g2, 1, pp)
            nS, nB = len(th), len(prev_bt)
            bi = 0
            for k, t in enumerate(th):
                t()
                want = ((k + 1) * nB) // nS
                while bi < want:
                    prev_bt[bi]()
                    bi += 1
            while bi < nB:
                prev_bt[bi]()
                bi += 1
            prev_bt = B_thunks(hh, g2, pp)
        for t in prev_bt:
            t()
        g.cc(lambda e: e.collective_compute(
            "AllGather", ALU.bypass, replica_groups=[CORE_IDS],
            ins=[ao_c.ap()], outs=[ao_full.ap()]),
            reads=[("dr", "aoc", hh, g2)
                   for hh in range(2) for g2 in range(32)],
            writes=[("dr", "ao_full")], label="agao")
        if debug:
            g.dma("sy", lambda e: e.dma_start(
                out=dmp["d_ao"].ap(), in_=ao_full.ap()[0:T, :]),
                reads=[("dr", "ao_full")], label="dump_ao")
        stP2.close()
        stA.close()
        g.fence()

        stx2 = contextlib.ExitStack()
        x2buf = sb("x2buf", [128, NTT * D], st=stx2)
        stl = contextlib.ExitStack()
        aoin = sb("aoin", [128, NTT * 8 * 256], BF16, st=stl)
        xin2 = sb("xin2", [128, NTT * D], st=stl)
        g.dma("sy", lambda e: e.dma_start(
            out=xin2[:, :].rearrange("p (t d) -> p t d", t=NTT),
            in_=x_rows.ap().rearrange("(t p) d -> p t d", p=128)),
            writes=[("xin2", tt) for tt in range(NTT)], label="ldx2")

        def ld_ao(e):
            dsem = g.cur_dma_sem
            with e.register("r_ao") as r0:
                e.reg_load(r0, c_cid[0:1, 0:1])
                cid = e.snap(r0, min_val=0, max_val=7)
                base = cid * TPC
                last = None
                for r in range(NCORES):
                    last = e.dma_start(
                        out=aoin[:, :].rearrange(
                            "p (t r hd) -> p t r hd", t=NTT, r=8)[:, :, r, :],
                        in_=ao_full.ap()[
                            bass.ds(base + r * T, TPC), :].rearrange(
                            "(t p) hd -> p t hd", p=128))
                    if r < NCORES - 1:
                        last.then_inc(dsem, 16)
                return last
        g.dmam("g", ld_ao, NCORES,
               reads=[("dr", "ao_full"), ("const", "cid")],
               writes=[("aoin", tt) for tt in range(NTT)], label="ldao")
        for tt in range(NTT):
            def xadd(e, tt=tt):
                return e.tensor_tensor(
                    x2buf[:, tt * D:(tt + 1) * D],
                    xin2[:, tt * D:(tt + 1) * D],
                    aoin[:, tt * 2048:(tt + 1) * 2048], ALU.add)
            g.op("v", xadd, reads=[("xin2", tt), ("aoin", tt)],
                 writes=[("row", "x2", tt)], label=f"x2_{tt}")
        if debug:
            g.dma("sy", lambda e: e.dma_start(
                out=dmp["d_x2"].ap().rearrange("(t p) d -> p t d", p=128),
                in_=x2buf[:, :].rearrange("p (t d) -> p t d", t=NTT)),
                reads=[("row", "x2", tt) for tt in range(NTT)],
                label="dump_x2")
        stl.close()
        g.fence()

        def build_ffn(srcbuf, rtag_in, rtag_out, store_out):
            stf = contextlib.ExitStack()
            tg = rtag_out
            wb_sb = sb(f"wb_sb{tg}", [128, 16 * D], BF16, st=stf)
            rt_sb = sb(f"rt_sb{tg}", [128, 16 * 8], BF16, st=stf)
            at_sb = sb(f"at_sb{tg}", [128, 16 * 128], BF16, st=stf)
            bw_sb = sb(f"bw_sb{tg}", [128, D], BF16, st=stf)
            bb_sb = sb(f"bb_sb{tg}", [128, D], st=stf)
            xn2 = sb(f"xn2{tg}", [128, D], BF16, st=stf)
            xn2T = sb(f"xn2T{tg}", [128, 16 * 128], BF16, st=stf)
            lgt = sb(f"lgt{tg}", [128, 8], st=stf)
            lmx = sb(f"lmx{tg}", [128, 1], st=stf)
            msk = sb(f"msk{tg}", [128, 8], st=stf)
            hmb = sb(f"hmb{tg}", [128, 128], BF16, st=stf)
            hmT = sb(f"hmT{tg}", [128, 128], BF16, st=stf)
            fsc = sb(f"fsc{tg}", [128, D], st=stf)
            fob = sb(f"fob{tg}", [128, D], st=stf)

            for t_, s_, et_ in ((wbT, wb_sb, 16), (routerT, rt_sb, 16),
                                (aT, at_sb, 16)):
                g.dma("g", lambda e, t=t_, s=s_, et=et_: e.dma_start(
                    out=s[:, :].rearrange("p (et o) -> p et o", et=et),
                    in_=t.ap().rearrange("(et p) o -> p et o", p=128)),
                    writes=[("fw", t_.name, tg)], label=f"ldf_{t_.name}{tg}")
            g.dma("g", lambda e: e.dma_start(out=bw_sb[:, :], in_=bwT.ap()),
                  writes=[("fw", "bwT", tg)], label=f"ldf_bw{tg}")
            g.dma("sy", lambda e: e.dma_start(out=bb_sb[:, :],
                                              in_=bbase_rep.ap()),
                  writes=[("fw", "bb", tg)], label=f"ldf_bb{tg}")

            for tt in range(NTT):
                xap = srcbuf[:, tt * D:(tt + 1) * D]
                emit_ln(xap, ("row", rtag_in, tt), xn2[:, :], ("fxn", tg),
                        f"{tg}_{tt}")
                for half in range(2):
                    def tp(e, half=half):
                        last = None
                        for i in range(8):
                            et = half * 8 + i
                            last = e.transpose(
                                psT[:, i * 128:(i + 1) * 128],
                                xn2[:, et * 128:(et + 1) * 128], c_idb[:, :])
                        return last
                    g.op("pe", tp, reads=[("fxn", tg), ("const", "idb")],
                         writes=[("psT",)], label=f"ftp{tg}{tt}{half}")

                    def ev(e, half=half):
                        return e.tensor_copy(
                            xn2T[:, :].rearrange("p (et t) -> p et t", et=16)
                            [:, half * 8:(half + 1) * 8, :],
                            psT[:, :].rearrange("p (i t) -> p i t", i=8))
                    g.op("v", ev, reads=[("psT",)], writes=[("fxT", tg)],
                         label=f"fev{tg}{tt}{half}")

                def bigmm(e):
                    last = None
                    for et in range(16):
                        lhs = xn2T[:, et * 128:(et + 1) * 128]
                        for oc in range(4):
                            last = e.matmul(
                                psA[oc][:, :], lhs,
                                wb_sb[:, et * D + oc * 512:
                                      et * D + oc * 512 + 512],
                                start=(et == 0), stop=False)
                        last = e.matmul(
                            psB[0][:, :8], lhs,
                            rt_sb[:, et * 8:(et + 1) * 8],
                            start=(et == 0), stop=(et == 15))
                        last = e.matmul(
                            psB[1][:, :128], lhs,
                            at_sb[:, et * 128:(et + 1) * 128],
                            start=(et == 0), stop=(et == 15))
                    return last
                g.op("pe", bigmm,
                     reads=[("fxT", tg), ("fw", "wbT", tg),
                            ("fw", "routerT", tg), ("fw", "aT", tg)],
                     writes=[("psA", 0), ("psA", 1), ("psA", 2), ("psA", 3),
                             ("psB", 0), ("psB", 1)],
                     label=f"fmm{tg}{tt}")
                g.op("v", lambda e: e.tensor_copy(lgt[:, :], psB[0][:, :8]),
                     reads=[("psB", 0)], writes=[("flg", tg)],
                     label=f"flg{tg}{tt}")
                g.op("v", lambda e: e.tensor_reduce(
                    lmx[:, :], lgt[:, :], AX.X, ALU.max),
                    reads=[("flg", tg)], writes=[("fmx", tg)],
                    label=f"fmx{tg}{tt}")
                g.op("v", lambda e: e.tensor_tensor(
                    msk[:, :], lgt[:, :], lmx[:, 0:1].broadcast_to([128, 8]),
                    ALU.is_equal),
                    reads=[("flg", tg), ("fmx", tg)], writes=[("fms", tg)],
                    label=f"feq{tg}{tt}")
                g.op("v", lambda e: e.tensor_scalar(
                    msk[:, :], msk[:, :], 2.0, None, ALU.mult),
                    reads=[("fms", tg)], writes=[("fms", tg)],
                    label=f"fm2{tg}{tt}")
                def fhm(e):
                    return e.tensor_tensor(
                        hmb[:, :].rearrange("p (ee r) -> p ee r", ee=8),
                        psB[1][:, :128].rearrange("p (ee r) -> p ee r", ee=8),
                        msk[:, :].rearrange("p (ee o) -> p ee o", o=1)
                        .broadcast_to([128, 8, 16]),
                        ALU.mult)
                g.op("v", fhm, reads=[("psB", 1), ("fms", tg)],
                     writes=[("fhm", tg)], label=f"fhm{tg}{tt}")
                g.op("pe", lambda e: e.transpose(
                    psT[:, :128], hmb[:, :], c_idb[:, :]),
                    reads=[("fhm", tg), ("const", "idb")], writes=[("psT",)],
                    label=f"fht{tg}{tt}")
                g.op("v", lambda e: e.tensor_copy(hmT[:, :], psT[:, :128]),
                     reads=[("psT",)], writes=[("fhT", tg)],
                     label=f"fhe{tg}{tt}")

                def dmm(e):
                    last = None
                    for oc in range(4):
                        last = e.matmul(
                            psA[oc][:, :], hmT[:, :],
                            bw_sb[:, oc * 512:oc * 512 + 512],
                            start=False, stop=True)
                    return last
                g.op("pe", dmm, reads=[("fhT", tg), ("fw", "bwT", tg),
                                       ("psA", 0), ("psA", 1), ("psA", 2),
                                       ("psA", 3)],
                     writes=[("psA", 0), ("psA", 1), ("psA", 2), ("psA", 3)],
                     label=f"fdm{tg}{tt}")
                def e1(e):
                    last = None
                    for oc in range(4):
                        last = e.tensor_tensor(
                            fsc[:, oc * 512:(oc + 1) * 512], psA[oc][:, :],
                            bb_sb[:, oc * 512:oc * 512 + 512], ALU.add)
                    return last
                g.op("v", e1,
                     reads=[("psA", 0), ("psA", 1), ("psA", 2), ("psA", 3),
                            ("fw", "bb", tg)],
                     writes=[("fsc", tg)], label=f"fe1{tg}{tt}")
                if store_out is None:
                    def e2(e, tt=tt):
                        return e.tensor_tensor(
                            srcbuf[:, tt * D:(tt + 1) * D], fsc[:, :],
                            srcbuf[:, tt * D:(tt + 1) * D], ALU.add)
                    g.op("v", e2,
                         reads=[("fsc", tg), ("row", rtag_in, tt)],
                         writes=[("row", rtag_out, tt),
                                 ("row", rtag_in, tt)],
                         label=f"fe2{tg}{tt}")
                else:
                    def e2(e, tt=tt):
                        return e.tensor_tensor(
                            fob[:, :], fsc[:, :],
                            srcbuf[:, tt * D:(tt + 1) * D], ALU.add)
                    g.op("v", e2,
                         reads=[("fsc", tg), ("row", rtag_in, tt)],
                         writes=[("fob", tg)], label=f"fe2{tg}{tt}")
                    g.dma("sy", lambda e, tt=tt: e.dma_start(
                        out=store_out.ap()[tt * 128:(tt + 1) * 128, :],
                        in_=fob[:, :]),
                        reads=[("fob", tg)],
                        writes=[("outst", tt)], label=f"fst{tg}{tt}")
            stf.close()

        build_ffn(x2buf, "x2", "x3", None)
        g.fence()

        st4 = contextlib.ExitStack()
        xnsc4 = sb("xnsc4", [128, D], BF16, st=st4)
        xnts4 = sb("xnts4", [128, NTT * D], BF16, st=st4)
        emit_xnT_phase(x2buf, "x3", "x3", xnsc4, xnts4)
        g.dma("sy", lambda e: e.dma_start(
            out=x3_d.ap().rearrange("(t p) d -> p t d", p=128),
            in_=x2buf[:, :].rearrange("p (t d) -> p t d", t=NTT)),
            reads=[("row", "x3", tt) for tt in range(NTT)],
            writes=[("dr", "x3_d")], label="spx3")
        if debug:
            g.dma("sy", lambda e: e.dma_start(
                out=dmp["d_x3"].ap(), in_=x3_d.ap()),
                reads=[("dr", "x3_d")], label="dump_x3")
        st4.close()
        stx2.close()
        g.fence()

        st5 = contextlib.ExitStack()
        inw_sb = sb("inw_sb", [128, 16 * 1024], BF16, st=st5)
        inb_sb = sb("inb_sb", [128, 8], st=st5)
        cw_sb = sb("cw_sb", [128, 16], st=st5)
        cb_sb = sb("cb_sb", [128, 4], st=st5)
        xpj_sb = sb("xpj_sb", [128, 4 * 160], BF16, st=st5)
        xnq5 = sb("xnq5", [128, 16 * IP_SC], BF16, st=st5)
        hbuf = sb("hbuf", [128, 4 * (IP_SC + 3)], st=st5)
        hbv = hbuf[:, :].rearrange("p (db t) -> p db t", db=4)
        halo = sb("halo", [128, 12], st=st5)
        acc1 = sb("acc1", [128, IP_SC], st=st5)
        acc2 = sb("acc2", [128, IP_SC], st=st5)
        hsf = sb("hsf", [128, IP_SC], st=st5)
        hsb = sb("hsb", [128, 4 * IP_SC], BF16, st=st5)
        sgb = sb("sgb", [128, 4 * IP_SC], BF16, st=st5)
        tsacc = sb("tsacc", [128, T], st=st5)
        bcacc = sb("bcacc", [32, T], st=st5)

        g.dma("g", lambda e: e.dma_start(
            out=inw_sb[:, :].rearrange("p (et o) -> p et o", et=16),
            in_=inwT_sl.ap().rearrange("(et p) o -> p et o", p=128)),
            writes=[("w5", "inw")], label="ld_inw")
        g.dma("sy", lambda e: e.dma_start(
            out=inb_sb[:, :], in_=inb_sl.ap()),
            writes=[("w5", "inb")], label="ld_inb")
        g.dma("sy", lambda e: e.dma_start(
            out=cw_sb[:, :], in_=convw_sl.ap()),
            writes=[("w5", "cw")], label="ld_cw")
        g.dma("sy", lambda e: e.dma_start(
            out=cb_sb[:, :], in_=convb_sl.ap()),
            writes=[("w5", "cb")], label="ld_cb")
        g.dma("g", lambda e: e.dma_start(
            out=xpj_sb[:, :].rearrange("p (db o) -> p db o", db=4),
            in_=xprojT_sl.ap().rearrange("(db p) o -> p db o", p=128)),
            writes=[("w5", "xpj")], label="ld_xpj")
        g.op("v", lambda e: e.memset(halo[:, :], 0), writes=[("halo",)],
             label="halo0")

        for sc in range(NSC):
            w0 = sc * IP_SC
            g.dma("sy", lambda e, sc=sc: e.dma_start(
                out=xnq5[:, :].rearrange("p (et t) -> p et t", et=16),
                in_=xn_full.ap().rearrange("(r et p) t -> r p et t",
                                           p=128, et=16)[sc]),
                reads=[("dr", "xn_full")], writes=[("xnq5",)],
                label=f"ld5x{sc}")
            for db in range(4):
                def hcp(e, db=db):
                    return e.tensor_copy(hbv[:, db, 0:3],
                                         halo[:, db * 3:(db + 1) * 3])
                g.op("v", hcp, reads=[("halo",)], writes=[("hb", db)],
                     label=f"hcp{sc}{db}")
            for dt in range(8):
                for fw in range(2):
                    bank = 2 * (dt % 2) + fw
                    def mm(e, dt=dt, fw=fw, bank=bank):
                        last = None
                        for et in range(16):
                            last = e.matmul(
                                psA[bank][:, :],
                                inw_sb[:, et * 1024 + dt * 128:
                                       et * 1024 + (dt + 1) * 128],
                                xnq5[:, et * IP_SC + fw * 512:
                                     et * IP_SC + fw * 512 + 512],
                                start=(et == 0), stop=(et == 15))
                        return last
                    g.op("pe", mm, reads=[("xnq5",), ("w5", "inw")],
                         writes=[("psA", bank)], label=f"imm{sc}{dt}{fw}")
                    if dt < 4:
                        def hev(e, dt=dt, fw=fw, bank=bank):
                            return e.activation(
                                hbv[:, dt, 3 + fw * 512:3 + fw * 512 + 512],
                                psA[bank][:, :], AF.Identity,
                                bias=inb_sb[:, dt:dt + 1])
                        g.op("a", hev, reads=[("psA", bank), ("w5", "inb")],
                             writes=[("hb", dt)], label=f"hev{sc}{dt}{fw}")
                    else:
                        def gev(e, dt=dt, fw=fw, bank=bank):
                            return e.activation(
                                sgb[:, :].rearrange("p (db t) -> p db t",
                                                    db=4)
                                [:, dt - 4, fw * 512:fw * 512 + 512],
                                psA[bank][:, :], AF.Silu,
                                bias=inb_sb[:, dt:dt + 1])
                        g.op("a", gev, reads=[("psA", bank), ("w5", "inb")],
                             writes=[("sg", dt - 4)], label=f"gev{sc}{dt}{fw}")
            for db in range(4):
                g.op("v", lambda e, db=db: e.tensor_scalar(
                    acc1[:, :], hbv[:, db, 0:IP_SC],
                    cw_sb[:, db * 4:db * 4 + 1], None, ALU.mult),
                    reads=[("hb", db), ("w5", "cw")], writes=[("ac1",)],
                    label=f"cv0{sc}{db}")
                g.op("v", lambda e, db=db: e.scalar_tensor_tensor(
                    acc2[:, :], hbv[:, db, 1:1 + IP_SC],
                    cw_sb[:, db * 4 + 1:db * 4 + 2], acc1[:, :],
                    ALU.mult, ALU.add),
                    reads=[("hb", db), ("ac1",), ("w5", "cw")],
                    writes=[("ac2",)], label=f"cv1{sc}{db}")
                g.op("v", lambda e, db=db: e.scalar_tensor_tensor(
                    acc1[:, :], hbv[:, db, 2:2 + IP_SC],
                    cw_sb[:, db * 4 + 2:db * 4 + 3], acc2[:, :],
                    ALU.mult, ALU.add),
                    reads=[("hb", db), ("ac2",), ("w5", "cw")],
                    writes=[("ac1",)], label=f"cv2{sc}{db}")
                g.op("v", lambda e, db=db: e.scalar_tensor_tensor(
                    acc2[:, :], hbv[:, db, 3:3 + IP_SC],
                    cw_sb[:, db * 4 + 3:db * 4 + 4], acc1[:, :],
                    ALU.mult, ALU.add),
                    reads=[("hb", db), ("ac1",), ("w5", "cw")],
                    writes=[("ac2",)], label=f"cv3{sc}{db}")
                def hsv(e, db=db):
                    return e.tensor_copy(halo[:, db * 3:(db + 1) * 3],
                                         hbv[:, db, IP_SC:IP_SC + 3])
                g.op("v", hsv, reads=[("hb", db)], writes=[("halo",)],
                     label=f"hsv{sc}{db}")
                g.op("a", lambda e, db=db: e.activation(
                    hsf[:, :], acc2[:, :], AF.Silu,
                    bias=cb_sb[:, db:db + 1]),
                    reads=[("ac2",), ("w5", "cb")], writes=[("hsf",)],
                    label=f"cvs{sc}{db}")
                def hcast(e, db=db):
                    return e.tensor_copy(
                        hsb[:, db * IP_SC:(db + 1) * IP_SC], hsf[:, :])
                g.op("v", hcast, reads=[("hsf",)], writes=[("hsb", db)],
                     label=f"hct{sc}{db}")
            g.dma("sy", lambda e, w0=w0: e.dma_start(
                out=hsb_d.ap().rearrange("(db p) t -> p db t", p=128)
                [:, :, w0:w0 + IP_SC],
                in_=hsb[:, :].rearrange("p (db t) -> p db t", db=4)),
                reads=[("hsb", db) for db in range(4)],
                writes=[("dr", "hsb", sc)], label=f"sth{sc}")
            g.dma("sy", lambda e, w0=w0: e.dma_start(
                out=sgb_d.ap().rearrange("(db p) t -> p db t", p=128)
                [:, :, w0:w0 + IP_SC],
                in_=sgb[:, :].rearrange("p (db t) -> p db t", db=4)),
                reads=[("sg", db) for db in range(4)],
                writes=[("dr", "sgb", sc)], label=f"stg{sc}")
            for fw in range(2):
                def xmm(e, fw=fw):
                    last = None
                    for db in range(4):
                        last = e.matmul(
                            psB[fw][:, :],
                            xpj_sb[:, db * 160:db * 160 + 128],
                            hsb[:, db * IP_SC + fw * 512:
                                db * IP_SC + fw * 512 + 512],
                            start=(db == 0), stop=(db == 3))
                    return last
                g.op("pe", xmm,
                     reads=[("hsb", db) for db in range(4)] + [("w5", "xpj")],
                     writes=[("psB", fw)], label=f"xts{sc}{fw}")

                def xev(e, fw=fw, w0=w0):
                    return e.tensor_copy(
                        tsacc[:, w0 + fw * 512:w0 + fw * 512 + 512],
                        psB[fw][:, :])
                g.op("v", xev, reads=[("psB", fw)], writes=[("tsa", sc, fw)],
                     label=f"xte{sc}{fw}")

                def bmm(e, fw=fw):
                    last = None
                    for db in range(4):
                        last = e.matmul(
                            psA[fw][0:32, :],
                            xpj_sb[:, db * 160 + 128:db * 160 + 160],
                            hsb[:, db * IP_SC + fw * 512:
                                db * IP_SC + fw * 512 + 512],
                            start=(db == 0), stop=(db == 3))
                    return last
                g.op("pe", bmm,
                     reads=[("hsb", db) for db in range(4)] + [("w5", "xpj")],
                     writes=[("psA", fw)], label=f"xbc{sc}{fw}")

                def bev(e, fw=fw, w0=w0):
                    return e.tensor_copy(
                        bcacc[:, w0 + fw * 512:w0 + fw * 512 + 512],
                        psA[fw][0:32, :])
                g.op("v", bev, reads=[("psA", fw)], writes=[("bca", sc, fw)],
                     label=f"xbe{sc}{fw}")
        g.dma("sy", lambda e: e.dma_start(
            out=sp_c.ap()[0:128, :], in_=tsacc[:, :]),
            reads=[("tsa", sc, fw) for sc in range(NSC) for fw in range(2)],
            writes=[("dr", "spc1")], label="stts")
        g.dma("sy", lambda e: e.dma_start(
            out=sp_c.ap()[128:160, :], in_=bcacc[:, :]),
            reads=[("bca", sc, fw) for sc in range(NSC) for fw in range(2)],
            writes=[("dr", "spc2")], label="stbc")
        g.cc(lambda e: e.collective_compute(
            "AllReduce", ALU.add, replica_groups=[CORE_IDS],
            ins=[sp_c.ap()], outs=[sp_full.ap()]),
            reads=[("dr", "spc1"), ("dr", "spc2")],
            writes=[("dr", "sp_full")], label="arsp")
        if debug:
            g.dma("sy", lambda e: e.dma_start(
                out=dmp["d_sp"].ap(), in_=sp_full.ap()),
                reads=[("dr", "sp_full")], label="dump_sp")
            g.dma("sy", lambda e: e.dma_start(
                out=dmp["d_hs"].ap(), in_=hsb_d.ap()),
                reads=[("dr", "hsb", sc) for sc in range(NSC)],
                label="dump_hs")
            g.dma("sy", lambda e: e.dma_start(
                out=dmp["d_sg"].ap(), in_=sgb_d.ap()),
                reads=[("dr", "sgb", sc) for sc in range(NSC)],
                label="dump_sg")
        st5.close()
        g.fence()

        st6 = contextlib.ExitStack()
        dtw_sb = sb("dtw_sb", [128, DIPC], BF16, st=st6)
        dtb_sb = sb("dtb_sb", [128, 4], st=st6)
        al_sb = sb("al_sb", [128, 4 * NST], st=st6)
        a_sb = sb("a_sb", [128, 4 * NST], st=st6)
        dd_sb = sb("dd_sb", [128, 4], st=st6)
        ow_sb = sb("ow_sb", [128, 4 * D], BF16, st=st6)
        tsch = sb("tsch", [128, 2 * SCAN_TC], BF16, st=st6)
        bcrep = sb("bcrep", [128, 2 * 2 * NST * SCAN_TC], BF16, st=st6)
        dtch = sb("dtch", [128, 4 * SCAN_TC], st=st6)
        dtu = sb("dtu", [128, 2 * SCAN_TC], st=st6)
        spa = sb("spa", [128, 2 * SCAN_TC], st=st6)
        spb = sb("spb", [128, 2 * SCAN_TC], st=st6)
        ach = sb("ach", [128, 2 * NST * SCAN_TC], st=st6)
        bch = sb("bch", [128, 2 * NST * SCAN_TC], st=st6)
        sch = sb("sch", [128, 2 * NST * SCAN_TC], st=st6)
        cmch = sb("cmch", [128, NST * SCAN_TC], st=st6)
        state = sb("state", [128, 4 * NST], st=st6)
        hsch = sb("hsch", [128, 2 * 4 * SCAN_TC], BF16, st=st6)
        sgch = sb("sgch", [128, 2 * 4 * SCAN_TC], BF16, st=st6)
        ybuf = sb("ybuf", [128, 2 * SCAN_TC], st=st6)
        zsc = sb("zsc", [128, SCAN_TC], st=st6)
        zsc2 = sb("zsc2", [128, SCAN_TC], st=st6)
        zb = sb("zb", [128, 2 * 4 * SCAN_TC], BF16, st=st6)
        obuf = sb("obuf", [128, D], st=st6)

        g.dma("g", lambda e: e.dma_start(out=dtw_sb[:, :],
                                         in_=dtwT_sl.ap()),
              writes=[("w6", "dtw")], label="ld_dtw")
        g.dma("sy", lambda e: e.dma_start(
            out=dtb_sb[:, :], in_=dtb_sl.ap()),
            writes=[("w6", "dtb")], label="ld_dtb")
        g.dma("sy", lambda e: e.dma_start(
            out=al_sb[:, :], in_=alog_sl.ap()),
            writes=[("w6", "al")], label="ld_al")
        g.dma("sy", lambda e: e.dma_start(
            out=dd_sb[:, :], in_=d_sl.ap()),
            writes=[("w6", "dd")], label="ld_dd")
        g.dma("g", lambda e: e.dma_start(
            out=ow_sb[:, :].rearrange("p (db o) -> p db o", db=4),
            in_=outwT_sl.ap().rearrange("(db p) o -> p db o", p=128)),
            writes=[("w6", "ow")], label="ld_ow")
        g.op("a", lambda e: e.activation(a_sb[:, :], al_sb[:, :], AF.Exp),
             reads=[("w6", "al")], writes=[("w6", "ae")], label="aexp")
        g.op("v", lambda e: e.tensor_scalar(
            a_sb[:, :], a_sb[:, :], -1.0, None, ALU.mult),
            reads=[("w6", "ae")], writes=[("w6", "a")], label="aneg")
        g.op("v", lambda e: e.memset(state[:, :], 0), writes=[("st", 0),
             ("st", 1), ("st", 2), ("st", 3)], label="st0")

        def emit_ldts(tc):
            w0 = tc * SCAN_TC
            pp_tc = tc % 2
            TSO = pp_tc * SCAN_TC
            g.dma("g", lambda e, w0=w0, TSO=TSO: e.dma_start(
                out=tsch[:, TSO:TSO + SCAN_TC],
                in_=sp_full.ap()[0:128, w0:w0 + SCAN_TC]),
                reads=[("dr", "sp_full")], writes=[("tsch", pp_tc)],
                label=f"ldts{tc}")

        def emit_ldbig(tc):
            w0 = tc * SCAN_TC
            pp_tc = tc % 2
            HSO = pp_tc * 4 * SCAN_TC
            RPO = pp_tc * NST * SCAN_TC

            def ldbc(e, w0=w0, RPO=RPO):
                return e.dma_start(
                    out=bcrep[:, 2 * RPO:2 * RPO + 2 * NST * SCAN_TC]
                    .rearrange("p (n t) -> p n t", n=2 * NST),
                    in_=sp_full.ap().rearrange("(o r) t -> o r t", o=1)
                    [0:1, 128:160, w0:w0 + SCAN_TC]
                    .broadcast_to([128, 2 * NST, SCAN_TC]))
            g.dma("g", ldbc, reads=[("dr", "sp_full")],
                  writes=[("bcrep", pp_tc)], label=f"ldbc{tc}")
            g.dma("sy", lambda e, w0=w0, HSO=HSO: e.dma_start(
                out=hsch[:, HSO:HSO + 4 * SCAN_TC].rearrange(
                    "p (db t) -> p db t", db=4),
                in_=hsb_d.ap().rearrange("(db p) t -> p db t", p=128)
                [:, :, w0:w0 + SCAN_TC]),
                reads=[("dr", "hsb", w0 // IP_SC)], writes=[("hsch", pp_tc)],
                label=f"ldhs{tc}")
            g.dma("sy", lambda e, w0=w0, HSO=HSO: e.dma_start(
                out=sgch[:, HSO:HSO + 4 * SCAN_TC].rearrange(
                    "p (db t) -> p db t", db=4),
                in_=sgb_d.ap().rearrange("(db p) t -> p db t", p=128)
                [:, :, w0:w0 + SCAN_TC]),
                reads=[("dr", "sgb", w0 // IP_SC)], writes=[("sgch", pp_tc)],
                label=f"ldsg{tc}")

        def stA(u):
            tc, db = divmod(u, NDB)
            pp_tc = tc % 2
            TSO = pp_tc * SCAN_TC
            pp = db % 2
            dps = psB[1] if pp == 0 else psB[0]
            if db == 0 and tc + 1 < N_TC:
                emit_ldts(tc + 1)
            if db == 0 and tc >= 1:
                emit_ldbig(tc)

            def dmm(e, db=db, dps=dps, TSO=TSO):
                return e.matmul(
                    dps[:, :SCAN_TC], dtw_sb[:, db * 128:(db + 1) * 128],
                    tsch[:, TSO:TSO + SCAN_TC], start=True, stop=True)
            g.op("pe", dmm, reads=[("tsch", pp_tc), ("w6", "dtw")],
                 writes=[("dtps", pp)], label=f"dmm{tc}_{db}")
            g.op("a", lambda e, db=db, pp=pp, dps=dps: e.activation(
                spa[:, pp * SCAN_TC:(pp + 1) * SCAN_TC],
                dps[:, :SCAN_TC], AF.Abs, bias=dtb_sb[:, db:db + 1]),
                reads=[("dtps", pp), ("w6", "dtb")], writes=[("spa", pp)],
                label=f"sp1{tc}_{db}")
            g.op("a", lambda e, pp=pp: e.activation(
                spa[:, pp * SCAN_TC:(pp + 1) * SCAN_TC],
                spa[:, pp * SCAN_TC:(pp + 1) * SCAN_TC],
                AF.Exp, scale=-1.0),
                reads=[("spa", pp)], writes=[("spa", pp)],
                label=f"sp2{tc}_{db}")
            g.op("a", lambda e, pp=pp: e.activation(
                spa[:, pp * SCAN_TC:(pp + 1) * SCAN_TC],
                spa[:, pp * SCAN_TC:(pp + 1) * SCAN_TC],
                AF.Ln, bias=1.0),
                reads=[("spa", pp)], writes=[("spa", pp)],
                label=f"sp3{tc}_{db}")
            g.op("a", lambda e, db=db, pp=pp, dps=dps: e.activation(
                spb[:, pp * SCAN_TC:(pp + 1) * SCAN_TC],
                dps[:, :SCAN_TC], AF.Relu, bias=dtb_sb[:, db:db + 1]),
                reads=[("dtps", pp), ("w6", "dtb")], writes=[("spb", pp)],
                label=f"sp4{tc}_{db}")
        def stA2(u):
            tc, db = divmod(u, NDB)
            pp = db % 2
            g.op("v", lambda e, db=db, pp=pp: e.tensor_tensor(
                dtch[:, db * SCAN_TC:(db + 1) * SCAN_TC],
                spa[:, pp * SCAN_TC:(pp + 1) * SCAN_TC],
                spb[:, pp * SCAN_TC:(pp + 1) * SCAN_TC], ALU.add),
                reads=[("spa", pp), ("spb", pp)],
                writes=[("dtch", db), ("dtps", pp)],
                label=f"sp5{tc}_{db}")

        def stB(u):
            tc, db = divmod(u, NDB)
            pp_tc = tc % 2
            HSO = pp_tc * 4 * SCAN_TC
            RPO = pp_tc * NST * SCAN_TC
            pp = db % 2

            def aex(e, db=db, pp=pp):
                last = None
                for n in range(NST):
                    last = e.activation(
                        ach[:, (pp * NST + n) * SCAN_TC:
                            (pp * NST + n + 1) * SCAN_TC],
                        dtch[:, db * SCAN_TC:(db + 1) * SCAN_TC],
                        AF.Exp,
                        scale=a_sb[:, db * NST + n:db * NST + n + 1])
                return last
            g.op("a", aex, reads=[("dtch", db), ("w6", "a")],
                 writes=[("ach", pp)], label=f"aex{tc}_{db}")

            par = u % 2

            def du(e, db=db, HSO=HSO, par=par):
                return e.tensor_tensor(
                    dtu[:, par * SCAN_TC:(par + 1) * SCAN_TC],
                    dtch[:, db * SCAN_TC:(db + 1) * SCAN_TC],
                    hsch[:, HSO + db * SCAN_TC:
                         HSO + (db + 1) * SCAN_TC], ALU.mult)
            g.op("g", du, reads=[("dtch", db), ("hsch", pp_tc)],
                 writes=[("dtu", par)], label=f"dtu{tc}_{db}")

            def bop(e, db=db, pp=pp, RPO=RPO, par=par):
                return e.tensor_tensor(
                    bch[:, pp * NST * SCAN_TC:(pp + 1) * NST * SCAN_TC]
                    .rearrange("p (n t) -> p n t", n=NST),
                    dtu[:, par * SCAN_TC:(par + 1) * SCAN_TC]
                    .rearrange("p (o t) -> p o t", o=1)
                    .broadcast_to([128, NST, SCAN_TC]),
                    bcrep[:, 2 * RPO:2 * RPO + NST * SCAN_TC]
                    .rearrange("p (n t) -> p n t", n=NST),
                    ALU.mult)
            g.op("g", bop,
                 reads=[("dtu", par), ("bcrep", pp_tc),
                        ("sch", (u - 1) % 2)],
                 writes=[("bch", pp)], label=f"bop{tc}_{db}")

        def stC_scn(u):
            tc, db = divmod(u, NDB)
            pp = db % 2
            par = u % 2
            SCO = par * NST * SCAN_TC

            def scn(e, db=db, pp=pp, SCO=SCO):
                last = None
                for n in range(NST):
                    last = e.tensor_tensor_scan(
                        sch[:, SCO + n * SCAN_TC:SCO + (n + 1) * SCAN_TC],
                        ach[:, (pp * NST + n) * SCAN_TC:
                            (pp * NST + n + 1) * SCAN_TC],
                        bch[:, (pp * NST + n) * SCAN_TC:
                            (pp * NST + n + 1) * SCAN_TC],
                        state[:, db * NST + n:db * NST + n + 1],
                        ALU.mult, ALU.add)
                return last
            g.op("v", scn,
                 reads=[("ach", pp), ("bch", pp), ("st", db)],
                 writes=[("sch", par), ("ach", pp), ("bch", pp)],
                 label=f"scn{tc}_{db}")

        def stC_cmu(u):
            tc, db = divmod(u, NDB)
            par = u % 2
            SCO = par * NST * SCAN_TC
            pp_tc = tc % 2
            RPO = pp_tc * NST * SCAN_TC

            def cmu(e, RPO=RPO, SCO=SCO):
                return e.tensor_tensor(
                    cmch[:, :],
                    sch[:, SCO:SCO + NST * SCAN_TC],
                    bcrep[:, 2 * RPO + NST * SCAN_TC:
                          2 * RPO + 2 * NST * SCAN_TC],
                    ALU.mult)
            g.op("v", cmu, reads=[("sch", par), ("bcrep", pp_tc)],
                 writes=[("cmch",)], label=f"cmu{tc}_{db}")

        def stC2(u):
            tc, db = divmod(u, NDB)
            par = u % 2
            SCO = par * NST * SCAN_TC

            def stu(e, db=db, SCO=SCO):
                return e.activation(
                    state[:, db * NST:(db + 1) * NST],
                    sch[:, SCO:SCO + NST * SCAN_TC]
                    .rearrange("p (n t) -> p n t", n=NST)
                    [:, :, SCAN_TC - 1:SCAN_TC].rearrange(
                        "p n o -> p (n o)"), AF.Copy)
            g.op("a", stu, reads=[("sch", par)],
                 writes=[("st", db)], label=f"stu{tc}_{db}")

        def stD1(u):
            tc, db = divmod(u, NDB)
            par = u % 2

            def yrd(e, par=par):
                h = NST * SCAN_TC
                while h > 2 * SCAN_TC:
                    h //= 2
                    e.tensor_tensor(cmch[:, :h], cmch[:, :h],
                                    cmch[:, h:2 * h], ALU.add)
                return e.tensor_tensor(
                    ybuf[:, par * SCAN_TC:(par + 1) * SCAN_TC],
                    cmch[:, :SCAN_TC], cmch[:, SCAN_TC:2 * SCAN_TC],
                    ALU.add)
            g.op("v", yrd, reads=[("cmch",)],
                 writes=[("yb", par), ("cmch",)], label=f"yrd{tc}_{db}")

        def stD2(u):
            tc, db = divmod(u, NDB)
            pp_tc = tc % 2
            HSO = pp_tc * 4 * SCAN_TC
            par = u % 2
            ZO = pp_tc * 4 * SCAN_TC

            def z1a(e, db=db, HSO=HSO):
                return e.tensor_tensor(
                    zsc[:, :],
                    hsch[:, HSO + db * SCAN_TC:HSO + (db + 1) * SCAN_TC],
                    dd_sb[:, db:db + 1].broadcast_to([128, SCAN_TC]),
                    ALU.mult)
            g.op("g", z1a, reads=[("hsch", pp_tc), ("w6", "dd")],
                 writes=[("zsc",)], label=f"z1a{tc}_{db}")

            def z1b(e, par=par):
                return e.tensor_tensor(
                    zsc2[:, :], zsc[:, :],
                    ybuf[:, par * SCAN_TC:(par + 1) * SCAN_TC], ALU.add)
            g.op("v", z1b, reads=[("zsc",), ("yb", par)],
                 writes=[("zsc2",)], label=f"z1b{tc}_{db}")

            def z2(e, db=db, HSO=HSO, ZO=ZO):
                return e.tensor_tensor(
                    zb[:, ZO + db * SCAN_TC:ZO + (db + 1) * SCAN_TC],
                    zsc2[:, :],
                    sgch[:, HSO + db * SCAN_TC:HSO + (db + 1) * SCAN_TC],
                    ALU.mult)
            g.op("v", z2, reads=[("zsc2",), ("sgch", pp_tc)],
                 writes=[("zb", pp_tc, db)], label=f"z2_{tc}_{db}")
            if db == NDB - 1:
                emit_oproj(tc)

        def emit_oproj(tc):
            pp_tc = tc % 2
            ZO = pp_tc * 4 * SCAN_TC
            for t1 in range(SCAN_TC // 128):
                def omm(e, t1=t1, ZO=ZO):
                    last = None
                    for db in range(4):
                        for oc in range(4):
                            last = e.matmul(
                                psA[oc][:, :],
                                zb[:, ZO + db * SCAN_TC + t1 * 128:
                                   ZO + db * SCAN_TC + (t1 + 1) * 128],
                                ow_sb[:, db * D + oc * 512:
                                      db * D + oc * 512 + 512],
                                start=(db == 0), stop=(db == 3))
                    return last
                g.op("pe", omm,
                     reads=[("zb", pp_tc, db) for db in range(4)]
                     + [("w6", "ow")],
                     writes=[("psA", 0), ("psA", 1), ("psA", 2), ("psA", 3)],
                     label=f"omm{tc}_{t1}")
                for oc in range(4):
                    def oev(e, oc=oc):
                        return e.activation(
                            obuf[:, oc * 512:(oc + 1) * 512], psA[oc][:, :],
                            AF.Copy)
                    g.op("a", oev, reads=[("psA", oc)], writes=[("obuf", oc)],
                         label=f"oev{tc}_{t1}_{oc}")
                g.dma("sy", lambda e, tc=tc, t1=t1: e.dma_start(
                    out=rs_in.ap()[tc * SCAN_TC + t1 * 128:
                                   tc * SCAN_TC + (t1 + 1) * 128, :],
                    in_=obuf[:, :]),
                    reads=[("obuf", oc) for oc in range(4)],
                    writes=[("rsw", tc, t1)], label=f"ost{tc}_{t1}")

        emit_ldts(0)
        emit_ldbig(0)
        U = N_TC * NDB
        for r in range(U + 5):
            if 0 <= r - 2 < U:
                stB(r - 2)
            if 0 <= r - 4 < U:
                stC_cmu(r - 4)
            if 0 <= r - 4 < U:
                stD1(r - 4)
            if r < U:
                stA(r)
            if 0 <= r - 3 < U:
                stC_scn(r - 3)
            if 0 <= r - 3 < U:
                stC2(r - 3)
            if 0 <= r - 4 < U:
                stD2(r - 4)
            if 0 <= r - 1 < U:
                stA2(r - 1)
        g.cc(lambda e: e.collective_compute(
            "ReduceScatter", ALU.add, replica_groups=[CORE_IDS],
            ins=[rs_in.ap()], outs=[rs_out.ap()]),
            reads=[("rsw", tc, t1) for tc in range(N_TC)
                   for t1 in range(SCAN_TC // 128)],
            writes=[("dr", "rs_out")], label="rsop")
        if debug:
            g.dma("sy", lambda e: e.dma_start(
                out=dmp["d_rs"].ap(), in_=rs_out.ap()),
                reads=[("dr", "rs_out")], label="dump_rs")
        st6.close()
        g.fence()

        st7 = contextlib.ExitStack()
        x4buf = sb("x4buf", [128, NTT * D], st=st7)
        ob_sb = sb("ob_sb", [128, D], st=st7)
        x3t = sb("x3t", [128, D], st=st7)
        rst = sb("rst", [128, D], st=st7)
        g.dma("sy", lambda e: e.dma_start(out=ob_sb[:, :], in_=outb_rep.ap()),
              writes=[("ob",)], label="ldob")
        for tt in range(NTT):
            g.dma("sy", lambda e, tt=tt: e.dma_start(
                out=x3t[:, :],
                in_=x3_d.ap()[tt * 128:(tt + 1) * 128, :]),
                reads=[("dr", "x3_d")], writes=[("x3t",)],
                label=f"ldx3_{tt}")
            g.dma("sy", lambda e, tt=tt: e.dma_start(
                out=rst[:, :],
                in_=rs_out.ap()[tt * 128:(tt + 1) * 128, :]),
                reads=[("dr", "rs_out")], writes=[("rstt",)],
                label=f"ldrs_{tt}")

            def a1(e, tt=tt):
                return e.tensor_tensor(
                    x4buf[:, tt * D:(tt + 1) * D],
                    x3t[:, :], rst[:, :], ALU.add)
            g.op("v", a1, reads=[("x3t",), ("rstt",)],
                 writes=[("x4a", tt)], label=f"x4a{tt}")

            def a2(e, tt=tt):
                return e.tensor_tensor(
                    x4buf[:, tt * D:(tt + 1) * D],
                    x4buf[:, tt * D:(tt + 1) * D], ob_sb[:, :], ALU.add)
            g.op("v", a2, reads=[("x4a", tt), ("ob",)],
                 writes=[("row", "x4", tt)], label=f"x4b{tt}")
        build_ffn(x4buf, "x4", "x5", out_rows)
        st7.close()

        g.op("sy", lambda e: e.nop(),
             reads=[("outst", tt) for tt in range(NTT)], label="final")

        g.plan()
        build_program._last_g = g
        g.emit(nc, semmap)

    return nc


def _prep_inputs(inputs):
    x = np.asarray(inputs["L0_attn_a_x"])
    wqkv = np.asarray(inputs["L0_attn_a_Wqkv_w"])
    bqkv = np.asarray(inputs["L0_attn_a_Wqkv_b"])
    in_w = np.asarray(inputs["L0_attn_b_in_w"])
    in_b = np.asarray(inputs["L0_attn_b_in_b"])
    conv_w = np.asarray(inputs["L0_attn_b_conv_w"])
    conv_b = np.asarray(inputs["L0_attn_b_conv_b"])
    xproj_w = np.asarray(inputs["L0_attn_b_xproj_w"])
    dt_w = np.asarray(inputs["L0_attn_b_dt_w"])
    dt_b = np.asarray(inputs["L0_attn_b_dt_b"])
    a_log = np.asarray(inputs["L0_attn_b_A_log"])
    d_ssm = np.asarray(inputs["L0_attn_b_D"])
    out_w = np.asarray(inputs["L0_attn_b_out_w"])
    out_b = np.asarray(inputs["L0_attn_b_out_b"])
    w_base = np.asarray(inputs["L0_ffn_W_base"])
    b_base = np.asarray(inputs["L0_ffn_b_base"])
    router_w = np.asarray(inputs["L0_ffn_router_W"])
    a_lora = np.asarray(inputs["L0_ffn_A"])
    bw_lora = np.asarray(inputs["L0_ffn_Bw"])

    C = np.ascontiguousarray

    inv = (1.0 / 10000.0 ** (np.arange(0, 32, 2, dtype=np.float32) / 32.0))
    t = np.arange(T, dtype=np.float32)
    fr = np.outer(t, inv)
    cosg = np.tile(np.cos(fr).T.astype(np.float32), (8, 1))
    sing = np.tile(np.sin(fr).T.astype(np.float32), (8, 1))
    for base in (0, 64):
        for u in (0, 2):
            cosg[base + u * 16: base + (u + 1) * 16] *= np.float32(SCALE)
            sing[base + u * 16: base + (u + 1) * 16] *= np.float32(SCALE)
    idb = np.eye(128, dtype=np.float32).astype(BF16NP)
    idf = np.eye(128, dtype=np.float32)
    trm = np.triu(np.full((128, 128), -10000.0, np.float32), 1).astype(BF16NP)
    ones1 = np.ones((1, 128), np.float32)
    wbT = C(w_base.T)
    bbase_rep = np.tile(b_base[None, :], (128, 1)).astype(np.float32)
    routerT = C(router_w.T)
    aT = C(a_lora.reshape(128, D).T)
    bwT = C(np.transpose(bw_lora, (0, 2, 1)).reshape(128, D))
    outb_rep = np.tile(out_b[None, :], (128, 1)).astype(np.float32)

    maps = []
    for c in range(NCORES):
        m = {}
        m["x_rows"] = C(x[0, c * TPC:(c + 1) * TPC, :])
        m["cid_in"] = np.array([[c]], np.int32)
        h0, h1 = 2 * c, 2 * c + 1
        unit_list = [("q", h0), ("k", h0), ("q", h1), ("k", h1)]

        def urows(kind, hh):
            base = (0 if kind == "q" else D) + hh * 128
            return base
        rot_idx = []
        for part in range(2):
            for kind, hh in unit_list:
                b0 = urows(kind, hh) + part * 16
                rot_idx.extend(range(b0, b0 + 16))
        m["w_rot"] = C(wqkv[rot_idx, :].T)
        m["b_rot"] = C(bqkv[rot_idx].reshape(128, 1))
        wp = []
        bp = []
        for kind, hh in unit_list:
            b0 = urows(kind, hh)
            wp.append(wqkv[b0 + 32:b0 + 128, :])
            bp.append(bqkv[b0 + 32:b0 + 128])
        m["w_pass"] = C(np.concatenate(wp, axis=0).T)
        bps = np.stack(bp, axis=1)
        bps[:, 0] *= np.float32(SCALE)
        bps[:, 2] *= np.float32(SCALE)
        m["b_pass"] = C(bps)
        vrows = list(range(2 * D + h0 * 128, 2 * D + h0 * 128 + 128)) + \
            list(range(2 * D + h1 * 128, 2 * D + h1 * 128 + 128))
        m["w_v"] = C(wqkv[vrows, :].T)
        m["b_v"] = C(bqkv[vrows].reshape(2, 128).T)
        perm = np.zeros((128, 128), np.float32)
        for u in range(4):
            for j_ in range(16):
                perm[u * 16 + j_, u * 32 + j_] = 1.0
                perm[64 + u * 16 + j_, u * 32 + 16 + j_] = 1.0
        m["perm_all"] = perm.astype(BF16NP)
        m["cosg"] = cosg
        m["sing"] = sing
        m["wbT"] = wbT
        m["bbase_rep"] = bbase_rep
        m["routerT"] = routerT
        m["aT"] = aT
        m["bwT"] = bwT
        s = slice(c * DIPC, (c + 1) * DIPC)
        m["inwT_sl"] = C(np.concatenate(
            [in_w[c * DIPC:(c + 1) * DIPC, :],
             in_w[DI + c * DIPC:DI + (c + 1) * DIPC, :]], axis=0).T)
        m["inb_sl"] = C(np.concatenate(
            [in_b[c * DIPC:(c + 1) * DIPC],
             in_b[DI + c * DIPC:DI + (c + 1) * DIPC]]).reshape(8, 128).T)
        m["convw_sl"] = C(conv_w[s, 0, :].reshape(4, 128, 4).transpose(1, 0, 2).reshape(128, 16))
        m["convb_sl"] = C(conv_b[s].reshape(4, 128).T)
        m["xprojT_sl"] = C(xproj_w[:, s].T)
        m["dtwT_sl"] = C(dt_w[s, :].T)
        m["dtb_sl"] = C(dt_b[s].reshape(4, 128).T)
        m["alog_sl"] = C(a_log[s, :].reshape(4, 128, NST).transpose(1, 0, 2).reshape(128, 4 * NST))
        m["d_sl"] = C(d_ssm[s].reshape(4, 128).T)
        m["outwT_sl"] = C(out_w[:, s].T)
        m["outb_rep"] = outb_rep
        m["idb"] = idb
        m["idf"] = idf
        m["trm"] = trm
        m["ones1"] = ones1
        maps.append(m)
    return maps


_NC_CACHE = {}


def kernel(**inputs):
    import os
    debug = bool(int(os.environ.get("MK_DEBUG", "0")))
    maps = _prep_inputs(inputs)
    key = ("nc", debug)
    if key not in _NC_CACHE:
        _NC_CACHE[key] = build_program(debug=debug)
    nc = _NC_CACHE[key]
    res = run_bass_kernel_spmd(nc, maps, core_ids=CORE_IDS)
    if debug:
        kernel._dbg = res.results
    out = np.concatenate([res.results[c]["out_rows"] for c in range(NCORES)],
                         axis=0)
    return out.reshape(1, T, D).astype(np.float32)


# version 22
# speedup vs baseline: 1.0543x; 1.0513x over previous
import contextlib

import numpy as np
import ml_dtypes

import concourse.bass as bass
import concourse.mybir as mybir
from concourse.bass_utils import run_bass_kernel_spmd

F32 = mybir.dt.float32
BF16 = mybir.dt.bfloat16
I32 = mybir.dt.int32
AF = mybir.ActivationFunctionType
ALU = mybir.AluOpType
AX = mybir.AxisListType
BF16NP = ml_dtypes.bfloat16

NCORES = 8
T = 8192
D = 2048
TPC = T // NCORES
NTT = TPC // 128
DI = 4096
DIPC = DI // NCORES
NDB = DIPC // 128
NST = 16
RK = 128
EPS = 1e-5
SCALE = float(1.0 / np.sqrt(128.0))
QKV_SC = 1024
IP_SC = 1024
NSC = T // QKV_SC
SCAN_TC = 256
N_TC = T // SCAN_TC

ENGS = ("sy", "pe", "v", "a", "g")
BKNAME = {"sy": "sync", "pe": "tensor", "v": "vector", "a": "scalar",
          "g": "gpsimd"}
CORE_IDS = list(range(NCORES))


class Op:
    __slots__ = ("eng", "fn", "deps", "kind", "ndma", "needed", "ms", "waits",
                 "label", "dma_sem", "dma_val")

    def __init__(self, eng, fn, deps, kind, label, ndma=1):
        self.eng = eng
        self.fn = fn
        self.deps = deps
        self.kind = kind
        self.ndma = ndma
        self.needed = False
        self.ms = None
        self.waits = None
        self.label = label


N_DMASEM = 64


class G:

    def __init__(self):
        self.ops = []
        self.reg = {}
        self.last = {e: None for e in ENGS}
        self.dma_idx = 0
        self.dma_prev = [None] * N_DMASEM
        self.dma_cnt = [0] * N_DMASEM
        self.cur_dma_sem = None

    def op(self, eng, fn, reads=(), writes=(), kind="c", label="", ndma=1):
        deps = set()
        for k in reads:
            st = self.reg.get(k)
            if st is not None and st[0] is not None:
                deps.add(st[0])
        for k in writes:
            st = self.reg.get(k)
            if st is not None:
                deps.update(st[1])
                if st[0] is not None:
                    deps.add(st[0])
        o = Op(eng, fn, deps, kind, label, ndma)
        if kind == "d":
            si = self.dma_idx % N_DMASEM
            self.dma_idx += 1
            o.dma_sem = si
            if self.dma_prev[si] is not None:
                deps.add(self.dma_prev[si])
            self.dma_prev[si] = o
            self.dma_cnt[si] += 16 * ndma
            o.dma_val = self.dma_cnt[si]
        self.ops.append(o)
        for k in reads:
            self.reg.setdefault(k, [None, []])[1].append(o)
        for k in writes:
            self.reg[k] = [o, []]
        self.last[eng] = o
        return o

    def dma(self, eng, fn, reads=(), writes=(), label=""):
        return self.op(eng, fn, reads, writes, kind="d", label=label)

    def dmam(self, eng, fn, n, reads=(), writes=(), label=""):
        return self.op(eng, fn, reads, writes, kind="d", label=label, ndma=n)

    def cc(self, fn, reads=(), writes=(), label=""):
        return self.op("g", fn, reads, writes, kind="cc", label=label)

    def fence(self):
        lasts = set(o for o in self.last.values() if o is not None)
        for e in ENGS:
            o = Op(e, (lambda eng: eng.nop()), set(lasts), "c", "fence")
            self.ops.append(o)
            self.last[e] = o
        self.reg.clear()

    def plan(self):
        for o in self.ops:
            for d in o.deps:
                d.needed = True
        cnt = {}
        for o in self.ops:
            if o.kind == "d":
                o.ms = (("ds", o.dma_sem), o.dma_val)
            elif o.kind == "cc":
                cnt["cc"] = cnt.get("cc", 0) + 1
                o.ms = ("cc", cnt["cc"])
            elif o.needed:
                key = ("c", o.eng)
                cnt[key] = cnt.get(key, 0) + 1
                o.ms = (key, cnt[key])
        waited = {e: {} for e in ENGS}
        for o in self.ops:
            w = {}
            for d in o.deps:
                if d.ms is None:
                    continue
                k, v = d.ms
                if w.get(k, 0) < v:
                    w[k] = v
            W = waited[o.eng]
            o.waits = []
            for k, v in sorted(w.items(), key=lambda kv: str(kv[0])):
                if W.get(k, 0) < v:
                    o.waits.append((k, v))
                    W[k] = v

    def emit(self, nc, semmap):
        with nc.Block() as bk:
            def mk(ename):
                def body(eng):
                    for o in self.ops:
                        if o.eng != ename:
                            continue
                        for k, v in o.waits:
                            eng.wait_ge(semmap[k], v)
                        if o.kind == "d":
                            self.cur_dma_sem = semmap[("ds", o.dma_sem)]
                        inst = o.fn(eng)
                        if o.kind == "d":
                            inst.then_inc(semmap[("ds", o.dma_sem)], 16)
                        elif o.kind == "cc":
                            inst.then_inc(semmap["cc"], 1)
                        elif o.needed:
                            inst.then_inc(semmap[("c", ename)], 1)
                return body
            for ename in ENGS:
                getattr(bk, BKNAME[ename])(mk(ename))


def build_program(debug=False):
    nc = bass.Bass("TRN2", target_bir_lowering=False, num_devices=NCORES)
    g = G()

    def inp(name, shape, dt=F32):
        return nc.dram_tensor(name, shape, dt, kind="ExternalInput")

    x_rows = inp("x_rows", [TPC, D])
    cid_in = inp("cid_in", [1, 1], I32)
    w_rot = inp("w_rot", [D, 128])
    b_rot = inp("b_rot", [128, 1])
    w_pass = inp("w_pass", [D, 384])
    b_pass = inp("b_pass", [96, 4])
    w_v = inp("w_v", [D, 256])
    b_v = inp("b_v", [128, 2])
    perm_all = inp("perm_all", [128, 128], BF16)
    cosg = inp("cosg", [128, T])
    sing = inp("sing", [128, T])
    wbT = inp("wbT", [D, D])
    bbase_rep = inp("bbase_rep", [128, D])
    routerT = inp("routerT", [D, 8])
    aT = inp("aT", [D, 128])
    bwT = inp("bwT", [128, D])
    inwT_sl = inp("inwT_sl", [D, 1024])
    inb_sl = inp("inb_sl", [128, 8])
    convw_sl = inp("convw_sl", [128, 16])
    convb_sl = inp("convb_sl", [128, 4])
    xprojT_sl = inp("xprojT_sl", [DIPC, 160])
    dtwT_sl = inp("dtwT_sl", [RK, DIPC])
    dtb_sl = inp("dtb_sl", [128, 4])
    alog_sl = inp("alog_sl", [128, 4 * NST])
    d_sl = inp("d_sl", [128, 4])
    outwT_sl = inp("outwT_sl", [DIPC, D])
    outb_rep = inp("outb_rep", [128, D])
    idb = inp("idb", [128, 128], BF16)
    idf = inp("idf", [128, 128])
    trm = inp("trm", [128, 128], BF16)
    ones1 = inp("ones1", [1, 128])

    out_rows = nc.dram_tensor("out_rows", [TPC, D], F32, kind="ExternalOutput")
    dmp = {}
    if debug:
        for nm, shape, dt in (("d_xnf", [D, TPC], BF16),
                              ("d_ao", [T, 256], BF16),
                              ("d_x3", [TPC, D], F32),
                              ("d_sp", [160, T], F32),
                              ("d_hs", [DIPC, T], BF16),
                              ("d_sg", [DIPC, T], BF16),
                              ("d_rs", [TPC, D], F32),
                              ("d_x2", [TPC, D], F32),
                              ("d_fb", [128, D], F32),
                              ("d_lg", [128, 8], F32),
                              ("d_hm", [128, 128], F32),
                              ("d_xn2", [128, D], BF16),
                              ("d_xnT", [128, 16 * 128], BF16)):
            dmp[nm] = nc.dram_tensor(nm, shape, dt, kind="ExternalOutput")

    def dram(name, shape, dt=F32):
        return nc.dram_tensor(name, shape, dt, kind="Internal")

    xn_c = dram("xn_c", [D, TPC], BF16)
    xn_full = dram("xn_full", [NCORES * D, TPC], BF16)
    ao_c = dram("ao_c", [T, 256], BF16)
    ao_full = dram("ao_full", [NCORES * T, 256], BF16)
    sp_c = dram("sp_c", [160, T])
    sp_full = dram("sp_full", [160, T])
    hsb_d = dram("hsb_d", [DIPC, T], BF16)
    sgb_d = dram("sgb_d", [DIPC, T], BF16)
    x3_d = dram("x3_d", [TPC, D])
    rs_in = dram("rs_in", [T, D])
    rs_out = dram("rs_out", [TPC, D])

    stk = contextlib.ExitStack()
    with stk:
        def sb(name, shape, dt=F32, st=None):
            return (st or stk).enter_context(nc.sbuf_tensor(name, shape, dt))

        psA = [stk.enter_context(nc.psum_tensor(f"psA{i}", [128, 512], F32))
               for i in range(4)]
        psB = [stk.enter_context(nc.psum_tensor(f"psB{i}", [128, 512], F32))
               for i in range(2)]
        psM = stk.enter_context(nc.psum_tensor("psM", [128, 512], F32))
        psT = stk.enter_context(nc.psum_tensor("psT", [128, 1024], BF16))

        c_idb = sb("c_idb", [128, 128], BF16)
        c_idf = sb("c_idf", [128, 128])
        c_trm = sb("c_trm", [128, 128], BF16)
        c_ones = sb("c_ones", [1, 128])
        c_onesb = sb("c_onesb", [1, 128], BF16)
        c_cid = sb("c_cid", [1, 1], I32)
        lnsq = sb("lnsq", [128, D])
        lnst = sb("lnst", [128, 16])

        semmap = {}
        for e in ENGS:
            semmap[("c", e)] = stk.enter_context(nc.semaphore(f"sC{e}"))
        for i in range(N_DMASEM):
            semmap[("ds", i)] = stk.enter_context(nc.semaphore(f"sDp{i}"))
        semmap["cc"] = stk.enter_context(nc.semaphore("sCC"))

        g.dma("g", lambda e: e.dma_start(out=c_onesb[:, :], in_=ones1.ap()),
              writes=[("const", "onesb")], label="ldc_onesb")
        for t_, s_, n_ in ((idb, c_idb, "idb"), (idf, c_idf, "idf"),
                           (trm, c_trm, "trm"),
                           (ones1, c_ones, "ones"), (cid_in, c_cid, "cid")):
            g.dma("sy", lambda e, t=t_, s=s_: e.dma_start(out=s[:, :],
                                                          in_=t.ap()),
                  writes=[("const", n_)], label=f"ldc_{n_}")

        def emit_ln(xap, rkey, out_ap, okey, tag):
            g.op("a", lambda e: e.activation(lnsq[:, :], xap, AF.Square),
                 reads=[rkey], writes=[("lnsq",)], label=f"lnsq{tag}")
            g.op("v", lambda e: e.tensor_reduce(
                lnst[:, 0:1], xap, AX.X, ALU.add, negate=True),
                reads=[rkey], writes=[("lt0",)], label=f"lns0{tag}")
            g.op("v", lambda e: e.tensor_reduce(
                lnst[:, 1:2], lnsq[:, :], AX.X, ALU.add),
                reads=[("lnsq",)], writes=[("lt1",)], label=f"lns1{tag}")
            g.op("v", lambda e: e.tensor_scalar(
                lnst[:, 2:3], lnst[:, 0:1], 1.0 / D, None, ALU.mult),
                reads=[("lt0",)], writes=[("lt2",)], label=f"lnm{tag}")
            g.op("v", lambda e: e.tensor_scalar(
                lnst[:, 3:4], lnst[:, 1:2], 1.0 / D, EPS, ALU.mult, ALU.add),
                reads=[("lt1",)], writes=[("lt3",)], label=f"lne{tag}")
            g.op("v", lambda e: e.tensor_tensor(
                lnst[:, 4:5], lnst[:, 2:3], lnst[:, 2:3], ALU.mult),
                reads=[("lt2",)], writes=[("lt4",)], label=f"lnm2{tag}")
            g.op("v", lambda e: e.tensor_tensor(
                lnst[:, 5:6], lnst[:, 3:4], lnst[:, 4:5], ALU.subtract),
                reads=[("lt3",), ("lt4",)], writes=[("lt5",)],
                label=f"lnv{tag}")
            g.op("a", lambda e: e.activation(lnst[:, 6:7], lnst[:, 5:6],
                                             AF.Ln),
                 reads=[("lt5",)], writes=[("lt6",)], label=f"lnl{tag}")
            g.op("a", lambda e: e.activation(lnst[:, 7:8], lnst[:, 6:7],
                                             AF.Exp, scale=-0.5),
                 reads=[("lt6",)], writes=[("lt7",)], label=f"lnr{tag}")
            g.op("v", lambda e: e.tensor_tensor(
                lnst[:, 8:9], lnst[:, 2:3], lnst[:, 7:8], ALU.mult),
                reads=[("lt2",), ("lt7",)], writes=[("lt8",)],
                label=f"lnb{tag}")
            return g.op("a", lambda e: e.activation(
                out_ap, xap, AF.Identity, bias=lnst[:, 8:9], scale=lnst[:, 7:8]),
                reads=[rkey, ("lt7",), ("lt8",)], writes=[okey],
                label=f"lnx{tag}")

        def emit_xnT_phase(rowbuf, rowtag, tag, xn_t, xnt_t):
            for tt in range(NTT):
                rb = rowbuf[:, tt * D:(tt + 1) * D]
                emit_ln(rb, ("row", rowtag, tt), xn_t[:, :], ("xnt_x", tag),
                        f"{tag}_{tt}")
                for half in range(2):
                    def tp(e, half=half):
                        last = None
                        for i in range(8):
                            et = half * 8 + i
                            last = e.transpose(
                                psT[:, i * 128:(i + 1) * 128],
                                xn_t[:, et * 128:(et + 1) * 128], c_idb[:, :])
                        return last
                    g.op("pe", tp, reads=[("xnt_x", tag), ("const", "idb")],
                         writes=[("psT",)], label=f"xtp{tag}{tt}{half}")

                    def ev(e, tt=tt, half=half):
                        return e.tensor_copy(
                            xnt_t[:, :].rearrange("p (et t) -> p et t", et=16)
                            [:, half * 8:(half + 1) * 8,
                             tt * 128:(tt + 1) * 128],
                            psT[:, :].rearrange("p (i t) -> p i t", i=8))
                    g.op("v", ev, reads=[("psT",)],
                         writes=[("xnt_o", tag, tt, half)],
                         label=f"xev{tag}{tt}{half}")
            g.dma("sy", lambda e: e.dma_start(
                out=xn_c.ap().rearrange("(et p) t -> p et t", p=128),
                in_=xnt_t[:, :].rearrange("p (et t) -> p et t", et=16)),
                reads=[("xnt_o", tag, tt, h) for tt in range(NTT)
                       for h in range(2)],
                writes=[("dr", "xn_c")], label=f"xst{tag}")
            g.cc(lambda e: e.collective_compute(
                "AllGather", ALU.bypass, replica_groups=[CORE_IDS],
                ins=[xn_c.ap()], outs=[xn_full.ap()]),
                reads=[("dr", "xn_c")], writes=[("dr", "xn_full")],
                label=f"ag{tag}")

        st0 = contextlib.ExitStack()
        xrow0 = sb("xrow0", [128, NTT * D], st=st0)
        xnsc0 = sb("xnsc0", [128, D], BF16, st=st0)
        xnts0 = sb("xnts0", [128, NTT * D], BF16, st=st0)
        g.dma("sy", lambda e: e.dma_start(
            out=xrow0[:, :].rearrange("p (t d) -> p t d", t=NTT),
            in_=x_rows.ap().rearrange("(t p) d -> p t d", p=128)),
            writes=[("row", "x1", tt) for tt in range(NTT)], label="ldx")
        emit_xnT_phase(xrow0, "x1", "x1", xnsc0, xnts0)
        if debug:
            g.dma("sy", lambda e: e.dma_start(
                out=dmp["d_xnf"].ap(), in_=xn_full.ap()[0:D, :]),
                reads=[("dr", "xn_full")], label="dump_xnf")
        st0.close()
        g.fence()

        stA = contextlib.ExitStack()
        units = [sb(f"unit{u}", [128, T], BF16, st=stA) for u in range(4)]
        vtm = sb("vtm", [128, 64 * 256], BF16, st=stA)
        vtmv = vtm[:, :].rearrange("p (tt h d) -> p tt h d", tt=64, h=2)

        stP1 = contextlib.ExitStack()
        wrot_sb = sb("wrot_sb", [128, 16 * 128], BF16, st=stP1)
        wpass_sb = sb("wpass_sb", [128, 16 * 384], BF16, st=stP1)
        wv_sb = sb("wv_sb", [128, 16 * 256], BF16, st=stP1)
        brot_sb = sb("brot_sb", [128, 1], st=stP1)
        bpass_sb = sb("bpass_sb", [96, 4], st=stP1)
        bv_sb = sb("bv_sb", [128, 2], st=stP1)
        perm_sb = sb("perm_sb", [128, 128], BF16, st=stP1)
        xnq = sb("xnq", [128, 16 * QKV_SC], BF16, st=stP1)
        rotg = sb("rotg", [128, QKV_SC], st=stP1)
        r12b = sb("r12b", [128, QKV_SC], BF16, st=stP1)
        rs1 = sb("rs1", [64, QKV_SC], st=stP1)
        rs2 = sb("rs2", [64, QKV_SC], st=stP1)
        csc = sb("csc", [128, QKV_SC], st=stP1)
        css = sb("css", [128, QKV_SC], st=stP1)
        vfm = sb("vfm", [128, QKV_SC], BF16, st=stP1)

        for t_, s_, n_ in ((w_rot, wrot_sb, 128), (w_pass, wpass_sb, 384),
                           (w_v, wv_sb, 256)):
            g.dma("g", lambda e, t=t_, s=s_, n=n_: e.dma_start(
                out=s[:, :].rearrange("p (et o) -> p et o", et=16),
                in_=t.ap().rearrange("(et p) o -> p et o", p=128)),
                writes=[("w", t_.name)], label=f"ldw_{t_.name}")
        g.dma("g", lambda e: e.dma_start(out=perm_sb[:, :],
                                         in_=perm_all.ap()),
              writes=[("w", "perm")], label="ldperm")
        for t_, s_ in ((b_rot, brot_sb), (b_pass, bpass_sb), (b_v, bv_sb)):
            g.dma("sy", lambda e, t=t_, s=s_: e.dma_start(out=s[:, :],
                                                          in_=t.ap()),
                  writes=[("w", t_.name)], label=f"ldb_{t_.name}")

        for sc in range(NSC):
            w0 = sc * QKV_SC
            g.dma("sy", lambda e, sc=sc: e.dma_start(
                out=xnq[:, :].rearrange("p (et t) -> p et t", et=16),
                in_=xn_full.ap().rearrange("(r et p) t -> r p et t",
                                           p=128, et=16)[sc]),
                reads=[("dr", "xn_full")], writes=[("xnq",)],
                label=f"ldxnq{sc}")
            g.dma("sy", lambda e, w0=w0: e.dma_start(
                out=csc[:, :], in_=cosg.ap()[:, w0:w0 + QKV_SC]),
                writes=[("csc",)], label=f"ldcos{sc}")
            g.dma("sy", lambda e, w0=w0: e.dma_start(
                out=css[:, :], in_=sing.ap()[:, w0:w0 + QKV_SC]),
                writes=[("css",)], label=f"ldsin{sc}")
            for fw in range(2):
                def mm(e, fw=fw):
                    last = None
                    for et in range(16):
                        last = e.matmul(
                            psA[fw][:, :],
                            wrot_sb[:, et * 128:(et + 1) * 128],
                            xnq[:, et * QKV_SC + fw * 512:
                                et * QKV_SC + fw * 512 + 512],
                            start=(et == 0), stop=(et == 15))
                    return last
                g.op("pe", mm, reads=[("xnq",), ("w", "w_rot")],
                     writes=[("psA", fw)], label=f"rmm{sc}{fw}")
                g.op("a", lambda e, fw=fw: e.activation(
                    rotg[:, fw * 512:fw * 512 + 512], psA[fw][:, :],
                    AF.Identity, bias=brot_sb[:, 0:1]),
                    reads=[("psA", fw), ("w", "b_rot")], writes=[("rotg",)],
                    label=f"rev{sc}{fw}")
            g.op("v", lambda e: e.tensor_tensor(
                rs1[:, :], rotg[0:64, :], csc[0:64, :], ALU.mult),
                reads=[("rotg",), ("csc",)], writes=[("rs1",)],
                label=f"rp1_{sc}")
            g.op("v", lambda e: e.tensor_tensor(
                rs2[:, :], rotg[64:128, :], css[64:128, :], ALU.mult),
                reads=[("rotg",), ("css",)], writes=[("rs2",)],
                label=f"rp2_{sc}")
            g.op("v", lambda e: e.tensor_tensor(
                r12b[0:64, :], rs1[:, :], rs2[:, :], ALU.subtract),
                reads=[("rs1",), ("rs2",)], writes=[("r12a",)],
                label=f"rp3_{sc}")
            g.op("v", lambda e: e.tensor_tensor(
                rs1[:, :], rotg[0:64, :], css[0:64, :], ALU.mult),
                reads=[("rotg",), ("css",)], writes=[("rs1",)],
                label=f"rp4_{sc}")
            g.op("v", lambda e: e.tensor_tensor(
                rs2[:, :], rotg[64:128, :], csc[64:128, :], ALU.mult),
                reads=[("rotg",), ("csc",)], writes=[("rs2",)],
                label=f"rp5_{sc}")
            g.op("v", lambda e: e.tensor_tensor(
                r12b[64:128, :], rs1[:, :], rs2[:, :], ALU.add),
                reads=[("rs1",), ("rs2",)], writes=[("r12b",)],
                label=f"rp6_{sc}")
            for u in range(4):
                for fw in range(2):
                    def pmm(e, u=u, fw=fw):
                        return e.matmul(
                            psM[0:32, :],
                            perm_sb[:, u * 32:(u + 1) * 32],
                            r12b[:, fw * 512:fw * 512 + 512],
                            start=True, stop=True)
                    g.op("pe", pmm,
                         reads=[("r12a",), ("r12b",), ("w", "perm")],
                         writes=[("psM",)], label=f"pmm{sc}{u}{fw}")

                    def pev(e, u=u, fw=fw, w0=w0):
                        return e.tensor_copy(
                            units[u][96:128,
                                     w0 + fw * 512:w0 + fw * 512 + 512],
                            psM[0:32, :])
                    g.op("v", pev, reads=[("psM",)], writes=[("un", u, sc)],
                         label=f"pev{sc}{u}{fw}")
            for u in range(4):
                for fw in range(2):
                    def mm(e, u=u, fw=fw):
                        last = None
                        for et in range(16):
                            last = e.matmul(
                                psA[2 + fw][0:96, :],
                                wpass_sb[:, et * 384 + u * 96:
                                         et * 384 + (u + 1) * 96],
                                xnq[:, et * QKV_SC + fw * 512:
                                    et * QKV_SC + fw * 512 + 512],
                                start=(et == 0), stop=(et == 15))
                        return last
                    g.op("pe", mm, reads=[("xnq",), ("w", "w_pass")],
                         writes=[("psA", 2 + fw)], label=f"qmm{sc}{u}{fw}")

                    def qev(e, u=u, fw=fw, w0=w0):
                        return e.activation(
                            units[u][0:96, w0 + fw * 512:w0 + fw * 512 + 512],
                            psA[2 + fw][0:96, :], AF.Identity,
                            bias=bpass_sb[:, u:u + 1],
                            scale=SCALE if u in (0, 2) else 1.0)
                    g.op("a", qev, reads=[("psA", 2 + fw), ("w", "b_pass")],
                         writes=[("un", u, sc)], label=f"qev{sc}{u}{fw}")
            for vh in range(2):
                for fw in range(2):
                    def mm(e, vh=vh, fw=fw):
                        last = None
                        for et in range(16):
                            last = e.matmul(
                                psB[fw][:, :],
                                wv_sb[:, et * 256 + vh * 128:
                                      et * 256 + (vh + 1) * 128],
                                xnq[:, et * QKV_SC + fw * 512:
                                    et * QKV_SC + fw * 512 + 512],
                                start=(et == 0), stop=(et == 15))
                        return last
                    g.op("pe", mm, reads=[("xnq",), ("w", "w_v")],
                         writes=[("psB", fw)], label=f"vmm{sc}{vh}{fw}")

                    def vev(e, vh=vh, fw=fw):
                        return e.activation(
                            vfm[:, fw * 512:fw * 512 + 512], psB[fw][:, :],
                            AF.Identity, bias=bv_sb[:, vh:vh + 1])
                    g.op("a", vev, reads=[("psB", fw), ("w", "b_v")],
                         writes=[("vfm",)], label=f"vev{sc}{vh}{fw}")

                def vtp(e):
                    last = None
                    for i in range(8):
                        last = e.transpose(
                            psT[:, i * 128:(i + 1) * 128],
                            vfm[:, i * 128:(i + 1) * 128], c_idb[:, :])
                    return last
                g.op("pe", vtp, reads=[("vfm",), ("const", "idb")],
                     writes=[("psT",)], label=f"vtp{sc}{vh}")

                def vte(e, vh=vh, sc=sc):
                    return e.tensor_copy(
                        vtmv[:, sc * 8:(sc + 1) * 8, vh, :],
                        psT[:, :].rearrange("p (i d) -> p i d", i=8))
                g.op("v", vte, reads=[("psT",)], writes=[("vt", vh, sc)],
                     label=f"vte{sc}{vh}")
        stP1.close()

        stP2 = contextlib.ExitStack()
        p2rs = [sb(f"p2r{i}", [128, 2 * T], BF16, st=stP2) for i in range(2)]
        aosg = [sb(f"aosg{i}", [128, 256], BF16, st=stP2)
                for i in range(2)]
        ptb = [sb(f"ptb{i}", [128, 1024], BF16, st=stP2) for i in range(2)]
        avb = sb("avb", [128, 256], st=stP2)
        nmx = sb("nmx", [128, 64], st=stP2)
        ssum = sb("ssum", [128, 64], st=stP2)
        fbuf = sb("fbuf", [128, 64], st=stP2)
        wtm = sb("wtm", [128, 64], st=stP2)
        nmn = sb("nmn", [128, 4], st=stP2)
        rsum = sb("rsum", [128, 4], st=stP2)
        rrec = sb("rrec", [128, 4], st=stP2)

        un_reads = [("un", u, sc) for u in range(4) for sc in range(NSC)]
        vt_reads = [("vt", vh, sc) for vh in range(2) for sc in range(NSC)]

        SBK = [("psB", 0), ("psB", 1), ("psA", 0), ("psA", 1)]
        SBT = [psB[0], psB[1], psA[0], psA[1]]

        def S_thunks(hh, g2, qt, pp):
            uq, uk = units[2 * hh], units[2 * hh + 1]
            j = 2 * g2 + qt
            kvlen = 128 * j + 128
            kvspan = 256 * g2 + 256
            nw = (kvlen + 511) // 512
            p2 = p2rs[pp]
            base = pp * 32 + qt * 16
            th = []
            if qt == 0 and kvlen < kvspan:
                def t_ms():
                    g.op("v", lambda e: e.memset(p2[:, kvlen:kvspan], 0),
                         reads=[], writes=[("p2", pp, 0)],
                         label=f"mst{hh}_{j}")
                th.append(t_ms)
            for w0_ in range(0, nw, 2):
                ws = [w_ for w_ in (w0_, w0_ + 1) if w_ < nw]

                def t_batch(ws=ws, j=j, kvlen=kvlen, nw=nw, base=base,
                            uq=uq, uk=uk, p2=p2, pp=pp, qt=qt, hh=hh):
                    bkeys = [SBK[w_ % 4] for w_ in ws]

                    def smm(e):
                        last = None
                        for w_ in ws:
                            fwid = min(512, kvlen - 512 * w_)
                            isl = w_ == nw - 1
                            last = e.matmul(
                                SBT[w_ % 4][:, :fwid],
                                uq[:, j * 128:(j + 1) * 128],
                                uk[:, w_ * 512:w_ * 512 + fwid],
                                start=True, stop=not isl)
                            if isl:
                                last = e.matmul(
                                    SBT[w_ % 4][:, fwid - 128:fwid],
                                    c_idb[:, :], c_trm[:, :],
                                    start=False, stop=True)
                        return last
                    g.op("pe", smm,
                         reads=un_reads + [("const", "idb"),
                                           ("const", "trm")],
                         writes=bkeys, label=f"smm{hh}_{j}_{ws[0]}")

                    def cmx(e):
                        last = None
                        for w_ in ws:
                            fwid = min(512, kvlen - 512 * w_)
                            last = e.tensor_reduce(
                                nmx[:, base + w_:base + w_ + 1],
                                SBT[w_ % 4][:, :fwid], AX.X, ALU.max,
                                negate=True)
                        return last
                    g.op("v", cmx, reads=bkeys,
                         writes=[("nmx", pp, qt, ws[0])],
                         label=f"cmx{hh}_{j}_{ws[0]}")

                    def exc(e):
                        last = None
                        for w_ in ws:
                            fwid = min(512, kvlen - 512 * w_)
                            last = e.activation(
                                p2[:, qt * T + w_ * 512:
                                   qt * T + w_ * 512 + fwid],
                                SBT[w_ % 4][:, :fwid], AF.Exp,
                                bias=nmx[:, base + w_:base + w_ + 1],
                                scale=1.0,
                                accum_out=ssum[:, base + w_:base + w_ + 1])
                        return last
                    g.op("a", exc, reads=bkeys + [("nmx", pp, qt, ws[0])],
                         writes=[("p2", pp, qt), ("ssum", pp, qt, ws[0])],
                         label=f"exc{hh}_{j}_{ws[0]}")
                th.append(t_batch)

            def t_fin(j=j, nw=nw, base=base, pp=pp, qt=qt, hh=hh, p2=p2,
                      kvlen=kvlen):
                rds = [("nmx", pp, qt, w0_) for w0_ in range(0, nw, 2)]
                srds = [("ssum", pp, qt, w0_) for w0_ in range(0, nw, 2)]
                g.op("v", lambda e: e.tensor_reduce(
                    nmn[:, 2 * pp + qt:2 * pp + qt + 1],
                    nmx[:, base:base + nw], AX.X, ALU.min),
                    reads=rds, writes=[("nmn", pp, qt)], label=f"nmn{hh}_{j}")
                g.op("a", lambda e: e.activation(
                    fbuf[:, base:base + nw], nmx[:, base:base + nw],
                    AF.Exp, bias=nmn[:, 2 * pp + qt:2 * pp + qt + 1],
                    scale=-1.0),
                    reads=rds + [("nmn", pp, qt)], writes=[("fct", pp, qt)],
                    label=f"fct{hh}_{j}")
                g.op("g", lambda e: e.tensor_tensor(
                    wtm[:, base:base + nw], fbuf[:, base:base + nw],
                    ssum[:, base:base + nw], ALU.mult),
                    reads=srds + [("fct", pp, qt)], writes=[("wtm", pp, qt)],
                    label=f"wtm{hh}_{j}")
                g.op("v", lambda e: e.tensor_reduce(
                    rsum[:, 2 * pp + qt:2 * pp + qt + 1],
                    wtm[:, base:base + nw], AX.X, ALU.add),
                    reads=[("wtm", pp, qt)], writes=[("rsm", pp, qt)],
                    label=f"wsm{hh}_{j}")
                for w0_ in range(0, nw, 2):
                    ws = [w_ for w_ in (w0_, w0_ + 1) if w_ < nw]

                    def rsc(e, ws=ws):
                        last = None
                        for w_ in ws:
                            fwid = min(512, kvlen - 512 * w_)
                            sl = p2[:, qt * T + w_ * 512:
                                    qt * T + w_ * 512 + fwid]
                            last = e.tensor_scalar(
                                sl, sl, fbuf[:, base + w_:base + w_ + 1],
                                None, ALU.mult)
                        return last
                    g.op("v", rsc, reads=[("fct", pp, qt), ("p2", pp, qt)],
                         writes=[("p2", pp, qt)],
                         label=f"rsc{hh}_{j}_{ws[0]}")
            th.append(t_fin)
            return th

        def B_thunks(hh, g2, pp):
            kvspan = 256 * g2 + 256
            nkc = kvspan // 128
            p2 = p2rs[pp]
            th = []

            def t_rec():
                g.op("v", lambda e: e.reciprocal(
                    rrec[:, 2 * pp:2 * pp + 2], rsum[:, 2 * pp:2 * pp + 2]),
                    reads=[("rsm", pp, 0), ("rsm", pp, 1)],
                    writes=[("rrec", pp)], label=f"rec{hh}_{g2}")
            th.append(t_rec)
            batches = [list(range(kb, min(kb + 2, nkc)))
                       for kb in range(0, nkc, 2)]
            nb = len(batches)

            psA2b = psA[2][:, :].bitcast(BF16)

            def mk_ptp(bi):
                kcs = batches[bi]
                ch = bi % 2
                tgt = psT if ch == 0 else psA2b
                tkey = ("psT",) if ch == 0 else ("psA", 2)

                def ptp(e, kcs=kcs, tgt=tgt):
                    last = None
                    for ii, kc in enumerate(kcs):
                        for qt in range(2):
                            last = e.transpose(
                                tgt[:, ii * 256 + qt * 128:
                                    ii * 256 + (qt + 1) * 128],
                                p2[:, qt * T + kc * 128:
                                   qt * T + kc * 128 + 128],
                                c_idb[:, :])
                    return last
                g.op("pe", ptp,
                     reads=[("p2", pp, 0), ("p2", pp, 1), ("const", "idb")],
                     writes=[tkey], label=f"ptp{hh}_{g2}_{bi}")

                peng = "v" if ch == 0 else "a"

                def pte(e, kcs=kcs, tgt=tgt, ch=ch, peng=peng):
                    if peng == "v":
                        return e.tensor_copy(
                            ptb[ch][:, :len(kcs) * 256],
                            tgt[:, :len(kcs) * 256])
                    return e.activation(
                        ptb[ch][:, :len(kcs) * 256],
                        tgt[:, :len(kcs) * 256], AF.Copy)
                g.op(peng, pte, reads=[tkey], writes=[("ptb", ch)],
                     label=f"pte{hh}_{g2}_{bi}")

            def mk_avm(bi):
                kcs = batches[bi]
                ch = bi % 2

                def avm(e, kcs=kcs, ch=ch, hh=hh, bi=bi, nb=nb):
                    last = None
                    for ii, kc in enumerate(kcs):
                        last = e.matmul(
                            psM[:, :256], vtmv[:, kc, hh, :],
                            ptb[ch][:, ii * 256:(ii + 1) * 256],
                            start=(bi == 0 and ii == 0),
                            stop=(bi == nb - 1 and ii == len(kcs) - 1))
                    return last
                g.op("pe", avm, reads=vt_reads + [("ptb", ch), ("psM",)],
                     writes=[("psM",)], label=f"avm{hh}_{g2}_{bi}")

            th.append(lambda: mk_ptp(0))
            for bi in range(nb):
                def t_step(bi=bi):
                    if bi + 1 < nb:
                        mk_ptp(bi + 1)
                    mk_avm(bi)
                th.append(t_step)

            def t_tail():
                g.op("v", lambda e: e.tensor_copy(avb[:, :], psM[:, :256]),
                     reads=[("psM",)], writes=[("avb",), ("psM",)],
                     label=f"ave{hh}_{g2}")
                for qt in range(2):
                    j = 2 * g2 + qt

                    def avt(e, qt=qt):
                        return e.transpose(
                            psM[:, :128], avb[:, qt * 128:(qt + 1) * 128],
                            c_idf[:, :])
                    g.op("pe", avt, reads=[("avb",), ("const", "idf")],
                         writes=[("psM",)], label=f"avt{hh}_{g2}_{qt}")

                    def aoe(e, qt=qt):
                        return e.activation(
                            aosg[pp][:, qt * 128:(qt + 1) * 128],
                            psM[:, :128], AF.Identity,
                            scale=rrec[:, 2 * pp + qt:2 * pp + qt + 1])
                    g.op("a", aoe, reads=[("psM",), ("rrec", pp)],
                         writes=[("aosg", pp, qt)], label=f"aoe{hh}_{g2}_{qt}")
                g.dma("sy", lambda e: e.dma_start(
                    out=ao_c.ap().rearrange("(jt p) hd -> p jt hd", p=128)
                    [:, 2 * g2:2 * g2 + 2, hh * 128:(hh + 1) * 128],
                    in_=aosg[pp][:, :].rearrange("p (jt d) -> p jt d", jt=2)),
                    reads=[("aosg", pp, 0), ("aosg", pp, 1)],
                    writes=[("dr", "aoc", hh, g2)], label=f"aost{hh}_{g2}")
            th.append(t_tail)
            return th

        units_list = [(hh, g2) for hh in range(2) for g2 in range(32)]
        prev_bt = []
        for i, (hh, g2) in enumerate(units_list):
            pp = i % 2
            th = S_thunks(hh, g2, 0, pp) + S_thunks(hh, g2, 1, pp)
            nS, nB = len(th), len(prev_bt)
            bi = 0
            for k, t in enumerate(th):
                t()
                want = ((k + 1) * nB) // nS
                while bi < want:
                    prev_bt[bi]()
                    bi += 1
            while bi < nB:
                prev_bt[bi]()
                bi += 1
            prev_bt = B_thunks(hh, g2, pp)
        for t in prev_bt:
            t()
        g.cc(lambda e: e.collective_compute(
            "AllGather", ALU.bypass, replica_groups=[CORE_IDS],
            ins=[ao_c.ap()], outs=[ao_full.ap()]),
            reads=[("dr", "aoc", hh, g2)
                   for hh in range(2) for g2 in range(32)],
            writes=[("dr", "ao_full")], label="agao")
        if debug:
            g.dma("sy", lambda e: e.dma_start(
                out=dmp["d_ao"].ap(), in_=ao_full.ap()[0:T, :]),
                reads=[("dr", "ao_full")], label="dump_ao")
        stP2.close()
        stA.close()
        g.fence()

        stx2 = contextlib.ExitStack()
        x2buf = sb("x2buf", [128, NTT * D], st=stx2)
        stl = contextlib.ExitStack()
        aoin = sb("aoin", [128, NTT * 8 * 256], BF16, st=stl)
        xin2 = sb("xin2", [128, NTT * D], st=stl)
        g.dma("sy", lambda e: e.dma_start(
            out=xin2[:, :].rearrange("p (t d) -> p t d", t=NTT),
            in_=x_rows.ap().rearrange("(t p) d -> p t d", p=128)),
            writes=[("xin2", tt) for tt in range(NTT)], label="ldx2")

        def ld_ao(e):
            dsem = g.cur_dma_sem
            with e.register("r_ao") as r0:
                e.reg_load(r0, c_cid[0:1, 0:1])
                cid = e.snap(r0, min_val=0, max_val=7)
                base = cid * TPC
                last = None
                for r in range(NCORES):
                    last = e.dma_start(
                        out=aoin[:, :].rearrange(
                            "p (t r hd) -> p t r hd", t=NTT, r=8)[:, :, r, :],
                        in_=ao_full.ap()[
                            bass.ds(base + r * T, TPC), :].rearrange(
                            "(t p) hd -> p t hd", p=128))
                    if r < NCORES - 1:
                        last.then_inc(dsem, 16)
                return last
        g.dmam("g", ld_ao, NCORES,
               reads=[("dr", "ao_full"), ("const", "cid")],
               writes=[("aoin", tt) for tt in range(NTT)], label="ldao")
        for tt in range(NTT):
            def xadd(e, tt=tt):
                return e.tensor_tensor(
                    x2buf[:, tt * D:(tt + 1) * D],
                    xin2[:, tt * D:(tt + 1) * D],
                    aoin[:, tt * 2048:(tt + 1) * 2048], ALU.add)
            g.op("v", xadd, reads=[("xin2", tt), ("aoin", tt)],
                 writes=[("row", "x2", tt)], label=f"x2_{tt}")
        if debug:
            g.dma("sy", lambda e: e.dma_start(
                out=dmp["d_x2"].ap().rearrange("(t p) d -> p t d", p=128),
                in_=x2buf[:, :].rearrange("p (t d) -> p t d", t=NTT)),
                reads=[("row", "x2", tt) for tt in range(NTT)],
                label="dump_x2")
        stl.close()
        g.fence()

        def build_ffn(srcbuf, rtag_in, rtag_out, store_out):
            stf = contextlib.ExitStack()
            tg = rtag_out
            wb_sb = sb(f"wb_sb{tg}", [128, 16 * D], BF16, st=stf)
            rt_sb = sb(f"rt_sb{tg}", [128, 16 * 8], BF16, st=stf)
            at_sb = sb(f"at_sb{tg}", [128, 16 * 128], BF16, st=stf)
            bw_sb = sb(f"bw_sb{tg}", [128, D], BF16, st=stf)
            bb_sb = sb(f"bb_sb{tg}", [128, D], st=stf)
            xn2 = sb(f"xn2{tg}", [128, D], BF16, st=stf)
            xn2T = sb(f"xn2T{tg}", [128, 16 * 128], BF16, st=stf)
            lgt = sb(f"lgt{tg}", [128, 8], st=stf)
            lmx = sb(f"lmx{tg}", [128, 1], st=stf)
            msk = sb(f"msk{tg}", [128, 8], st=stf)
            hmb = sb(f"hmb{tg}", [128, 128], BF16, st=stf)
            hmT = sb(f"hmT{tg}", [128, 128], BF16, st=stf)
            fsc = sb(f"fsc{tg}", [128, D], st=stf)
            fob = sb(f"fob{tg}", [128, D], st=stf)

            for t_, s_, et_ in ((wbT, wb_sb, 16), (routerT, rt_sb, 16),
                                (aT, at_sb, 16)):
                g.dma("g", lambda e, t=t_, s=s_, et=et_: e.dma_start(
                    out=s[:, :].rearrange("p (et o) -> p et o", et=et),
                    in_=t.ap().rearrange("(et p) o -> p et o", p=128)),
                    writes=[("fw", t_.name, tg)], label=f"ldf_{t_.name}{tg}")
            g.dma("g", lambda e: e.dma_start(out=bw_sb[:, :], in_=bwT.ap()),
                  writes=[("fw", "bwT", tg)], label=f"ldf_bw{tg}")
            g.dma("sy", lambda e: e.dma_start(out=bb_sb[:, :],
                                              in_=bbase_rep.ap()),
                  writes=[("fw", "bb", tg)], label=f"ldf_bb{tg}")

            for tt in range(NTT):
                xap = srcbuf[:, tt * D:(tt + 1) * D]
                emit_ln(xap, ("row", rtag_in, tt), xn2[:, :], ("fxn", tg),
                        f"{tg}_{tt}")
                for half in range(2):
                    def tp(e, half=half):
                        last = None
                        for i in range(8):
                            et = half * 8 + i
                            last = e.transpose(
                                psT[:, i * 128:(i + 1) * 128],
                                xn2[:, et * 128:(et + 1) * 128], c_idb[:, :])
                        return last
                    g.op("pe", tp, reads=[("fxn", tg), ("const", "idb")],
                         writes=[("psT",)], label=f"ftp{tg}{tt}{half}")

                    def ev(e, half=half):
                        return e.tensor_copy(
                            xn2T[:, :].rearrange("p (et t) -> p et t", et=16)
                            [:, half * 8:(half + 1) * 8, :],
                            psT[:, :].rearrange("p (i t) -> p i t", i=8))
                    g.op("v", ev, reads=[("psT",)], writes=[("fxT", tg)],
                         label=f"fev{tg}{tt}{half}")

                def bigmm(e):
                    last = None
                    for et in range(16):
                        lhs = xn2T[:, et * 128:(et + 1) * 128]
                        for oc in range(4):
                            last = e.matmul(
                                psA[oc][:, :], lhs,
                                wb_sb[:, et * D + oc * 512:
                                      et * D + oc * 512 + 512],
                                start=(et == 0), stop=False)
                        last = e.matmul(
                            psB[0][:, :8], lhs,
                            rt_sb[:, et * 8:(et + 1) * 8],
                            start=(et == 0), stop=(et == 15))
                        last = e.matmul(
                            psB[1][:, :128], lhs,
                            at_sb[:, et * 128:(et + 1) * 128],
                            start=(et == 0), stop=(et == 15))
                    return last
                g.op("pe", bigmm,
                     reads=[("fxT", tg), ("fw", "wbT", tg),
                            ("fw", "routerT", tg), ("fw", "aT", tg)],
                     writes=[("psA", 0), ("psA", 1), ("psA", 2), ("psA", 3),
                             ("psB", 0), ("psB", 1)],
                     label=f"fmm{tg}{tt}")
                g.op("v", lambda e: e.tensor_copy(lgt[:, :], psB[0][:, :8]),
                     reads=[("psB", 0)], writes=[("flg", tg)],
                     label=f"flg{tg}{tt}")
                g.op("v", lambda e: e.tensor_reduce(
                    lmx[:, :], lgt[:, :], AX.X, ALU.max),
                    reads=[("flg", tg)], writes=[("fmx", tg)],
                    label=f"fmx{tg}{tt}")
                g.op("v", lambda e: e.tensor_tensor(
                    msk[:, :], lgt[:, :], lmx[:, 0:1].broadcast_to([128, 8]),
                    ALU.is_equal),
                    reads=[("flg", tg), ("fmx", tg)], writes=[("fms", tg)],
                    label=f"feq{tg}{tt}")
                g.op("v", lambda e: e.tensor_scalar(
                    msk[:, :], msk[:, :], 2.0, None, ALU.mult),
                    reads=[("fms", tg)], writes=[("fms", tg)],
                    label=f"fm2{tg}{tt}")
                def fhm(e):
                    return e.tensor_tensor(
                        hmb[:, :].rearrange("p (ee r) -> p ee r", ee=8),
                        psB[1][:, :128].rearrange("p (ee r) -> p ee r", ee=8),
                        msk[:, :].rearrange("p (ee o) -> p ee o", o=1)
                        .broadcast_to([128, 8, 16]),
                        ALU.mult)
                g.op("v", fhm, reads=[("psB", 1), ("fms", tg)],
                     writes=[("fhm", tg)], label=f"fhm{tg}{tt}")
                g.op("pe", lambda e: e.transpose(
                    psT[:, :128], hmb[:, :], c_idb[:, :]),
                    reads=[("fhm", tg), ("const", "idb")], writes=[("psT",)],
                    label=f"fht{tg}{tt}")
                g.op("v", lambda e: e.tensor_copy(hmT[:, :], psT[:, :128]),
                     reads=[("psT",)], writes=[("fhT", tg)],
                     label=f"fhe{tg}{tt}")

                def dmm(e):
                    last = None
                    for oc in range(4):
                        last = e.matmul(
                            psA[oc][:, :], hmT[:, :],
                            bw_sb[:, oc * 512:oc * 512 + 512],
                            start=False, stop=True)
                    return last
                g.op("pe", dmm, reads=[("fhT", tg), ("fw", "bwT", tg),
                                       ("psA", 0), ("psA", 1), ("psA", 2),
                                       ("psA", 3)],
                     writes=[("psA", 0), ("psA", 1), ("psA", 2), ("psA", 3)],
                     label=f"fdm{tg}{tt}")
                def e1(e):
                    last = None
                    for oc in range(4):
                        last = e.tensor_tensor(
                            fsc[:, oc * 512:(oc + 1) * 512], psA[oc][:, :],
                            bb_sb[:, oc * 512:oc * 512 + 512], ALU.add)
                    return last
                g.op("v", e1,
                     reads=[("psA", 0), ("psA", 1), ("psA", 2), ("psA", 3),
                            ("fw", "bb", tg)],
                     writes=[("fsc", tg)], label=f"fe1{tg}{tt}")
                if store_out is None:
                    def e2(e, tt=tt):
                        return e.tensor_tensor(
                            srcbuf[:, tt * D:(tt + 1) * D], fsc[:, :],
                            srcbuf[:, tt * D:(tt + 1) * D], ALU.add)
                    g.op("v", e2,
                         reads=[("fsc", tg), ("row", rtag_in, tt)],
                         writes=[("row", rtag_out, tt),
                                 ("row", rtag_in, tt)],
                         label=f"fe2{tg}{tt}")
                else:
                    def e2(e, tt=tt):
                        return e.tensor_tensor(
                            fob[:, :], fsc[:, :],
                            srcbuf[:, tt * D:(tt + 1) * D], ALU.add)
                    g.op("v", e2,
                         reads=[("fsc", tg), ("row", rtag_in, tt)],
                         writes=[("fob", tg)], label=f"fe2{tg}{tt}")
                    g.dma("sy", lambda e, tt=tt: e.dma_start(
                        out=store_out.ap()[tt * 128:(tt + 1) * 128, :],
                        in_=fob[:, :]),
                        reads=[("fob", tg)],
                        writes=[("outst", tt)], label=f"fst{tg}{tt}")
            stf.close()

        build_ffn(x2buf, "x2", "x3", None)
        g.fence()

        st4 = contextlib.ExitStack()
        xnsc4 = sb("xnsc4", [128, D], BF16, st=st4)
        xnts4 = sb("xnts4", [128, NTT * D], BF16, st=st4)
        emit_xnT_phase(x2buf, "x3", "x3", xnsc4, xnts4)
        g.dma("sy", lambda e: e.dma_start(
            out=x3_d.ap().rearrange("(t p) d -> p t d", p=128),
            in_=x2buf[:, :].rearrange("p (t d) -> p t d", t=NTT)),
            reads=[("row", "x3", tt) for tt in range(NTT)],
            writes=[("dr", "x3_d")], label="spx3")
        if debug:
            g.dma("sy", lambda e: e.dma_start(
                out=dmp["d_x3"].ap(), in_=x3_d.ap()),
                reads=[("dr", "x3_d")], label="dump_x3")
        st4.close()
        stx2.close()
        g.fence()

        st5 = contextlib.ExitStack()
        inw_sb = sb("inw_sb", [128, 16 * 1024], BF16, st=st5)
        inb_sb = sb("inb_sb", [128, 8], st=st5)
        cw_sb = sb("cw_sb", [128, 16], st=st5)
        cb_sb = sb("cb_sb", [128, 4], st=st5)
        xpj_sb = sb("xpj_sb", [128, 4 * 160], BF16, st=st5)
        xnq5 = sb("xnq5", [128, 16 * IP_SC], BF16, st=st5)
        hbuf = sb("hbuf", [128, 4 * (IP_SC + 3)], st=st5)
        hbv = hbuf[:, :].rearrange("p (db t) -> p db t", db=4)
        halo = sb("halo", [128, 12], st=st5)
        acc1 = sb("acc1", [128, IP_SC], st=st5)
        acc2 = sb("acc2", [128, IP_SC], st=st5)
        hsf = sb("hsf", [128, IP_SC], st=st5)
        hsb = sb("hsb", [128, 4 * IP_SC], BF16, st=st5)
        sgb = sb("sgb", [128, 4 * IP_SC], BF16, st=st5)
        tsacc = sb("tsacc", [128, T], st=st5)
        bcacc = sb("bcacc", [32, T], st=st5)

        g.dma("g", lambda e: e.dma_start(
            out=inw_sb[:, :].rearrange("p (et o) -> p et o", et=16),
            in_=inwT_sl.ap().rearrange("(et p) o -> p et o", p=128)),
            writes=[("w5", "inw")], label="ld_inw")
        g.dma("sy", lambda e: e.dma_start(
            out=inb_sb[:, :], in_=inb_sl.ap()),
            writes=[("w5", "inb")], label="ld_inb")
        g.dma("sy", lambda e: e.dma_start(
            out=cw_sb[:, :], in_=convw_sl.ap()),
            writes=[("w5", "cw")], label="ld_cw")
        g.dma("sy", lambda e: e.dma_start(
            out=cb_sb[:, :], in_=convb_sl.ap()),
            writes=[("w5", "cb")], label="ld_cb")
        g.dma("g", lambda e: e.dma_start(
            out=xpj_sb[:, :].rearrange("p (db o) -> p db o", db=4),
            in_=xprojT_sl.ap().rearrange("(db p) o -> p db o", p=128)),
            writes=[("w5", "xpj")], label="ld_xpj")
        g.op("v", lambda e: e.memset(halo[:, :], 0), writes=[("halo",)],
             label="halo0")

        for sc in range(NSC):
            w0 = sc * IP_SC
            g.dma("sy", lambda e, sc=sc: e.dma_start(
                out=xnq5[:, :].rearrange("p (et t) -> p et t", et=16),
                in_=xn_full.ap().rearrange("(r et p) t -> r p et t",
                                           p=128, et=16)[sc]),
                reads=[("dr", "xn_full")], writes=[("xnq5",)],
                label=f"ld5x{sc}")
            for db in range(4):
                def hcp(e, db=db):
                    return e.tensor_copy(hbv[:, db, 0:3],
                                         halo[:, db * 3:(db + 1) * 3])
                g.op("v", hcp, reads=[("halo",)], writes=[("hb", db)],
                     label=f"hcp{sc}{db}")
            for dt in range(8):
                for fw in range(2):
                    bank = 2 * (dt % 2) + fw
                    def mm(e, dt=dt, fw=fw, bank=bank):
                        last = None
                        for et in range(16):
                            last = e.matmul(
                                psA[bank][:, :],
                                inw_sb[:, et * 1024 + dt * 128:
                                       et * 1024 + (dt + 1) * 128],
                                xnq5[:, et * IP_SC + fw * 512:
                                     et * IP_SC + fw * 512 + 512],
                                start=(et == 0), stop=(et == 15))
                        return last
                    g.op("pe", mm, reads=[("xnq5",), ("w5", "inw")],
                         writes=[("psA", bank)], label=f"imm{sc}{dt}{fw}")
                    if dt < 4:
                        def hev(e, dt=dt, fw=fw, bank=bank):
                            return e.activation(
                                hbv[:, dt, 3 + fw * 512:3 + fw * 512 + 512],
                                psA[bank][:, :], AF.Identity,
                                bias=inb_sb[:, dt:dt + 1])
                        g.op("a", hev, reads=[("psA", bank), ("w5", "inb")],
                             writes=[("hb", dt)], label=f"hev{sc}{dt}{fw}")
                    else:
                        def gev(e, dt=dt, fw=fw, bank=bank):
                            return e.activation(
                                sgb[:, :].rearrange("p (db t) -> p db t",
                                                    db=4)
                                [:, dt - 4, fw * 512:fw * 512 + 512],
                                psA[bank][:, :], AF.Silu,
                                bias=inb_sb[:, dt:dt + 1])
                        g.op("a", gev, reads=[("psA", bank), ("w5", "inb")],
                             writes=[("sg", dt - 4)], label=f"gev{sc}{dt}{fw}")
            for db in range(4):
                g.op("v", lambda e, db=db: e.tensor_scalar(
                    acc1[:, :], hbv[:, db, 0:IP_SC],
                    cw_sb[:, db * 4:db * 4 + 1], None, ALU.mult),
                    reads=[("hb", db), ("w5", "cw")], writes=[("ac1",)],
                    label=f"cv0{sc}{db}")
                g.op("v", lambda e, db=db: e.scalar_tensor_tensor(
                    acc2[:, :], hbv[:, db, 1:1 + IP_SC],
                    cw_sb[:, db * 4 + 1:db * 4 + 2], acc1[:, :],
                    ALU.mult, ALU.add),
                    reads=[("hb", db), ("ac1",), ("w5", "cw")],
                    writes=[("ac2",)], label=f"cv1{sc}{db}")
                g.op("v", lambda e, db=db: e.scalar_tensor_tensor(
                    acc1[:, :], hbv[:, db, 2:2 + IP_SC],
                    cw_sb[:, db * 4 + 2:db * 4 + 3], acc2[:, :],
                    ALU.mult, ALU.add),
                    reads=[("hb", db), ("ac2",), ("w5", "cw")],
                    writes=[("ac1",)], label=f"cv2{sc}{db}")
                g.op("v", lambda e, db=db: e.scalar_tensor_tensor(
                    acc2[:, :], hbv[:, db, 3:3 + IP_SC],
                    cw_sb[:, db * 4 + 3:db * 4 + 4], acc1[:, :],
                    ALU.mult, ALU.add),
                    reads=[("hb", db), ("ac1",), ("w5", "cw")],
                    writes=[("ac2",)], label=f"cv3{sc}{db}")
                def hsv(e, db=db):
                    return e.tensor_copy(halo[:, db * 3:(db + 1) * 3],
                                         hbv[:, db, IP_SC:IP_SC + 3])
                g.op("v", hsv, reads=[("hb", db)], writes=[("halo",)],
                     label=f"hsv{sc}{db}")
                g.op("a", lambda e, db=db: e.activation(
                    hsf[:, :], acc2[:, :], AF.Silu,
                    bias=cb_sb[:, db:db + 1]),
                    reads=[("ac2",), ("w5", "cb")], writes=[("hsf",)],
                    label=f"cvs{sc}{db}")
                def hcast(e, db=db):
                    return e.tensor_copy(
                        hsb[:, db * IP_SC:(db + 1) * IP_SC], hsf[:, :])
                g.op("v", hcast, reads=[("hsf",)], writes=[("hsb", db)],
                     label=f"hct{sc}{db}")
            g.dma("sy", lambda e, w0=w0: e.dma_start(
                out=hsb_d.ap().rearrange("(db p) t -> p db t", p=128)
                [:, :, w0:w0 + IP_SC],
                in_=hsb[:, :].rearrange("p (db t) -> p db t", db=4)),
                reads=[("hsb", db) for db in range(4)],
                writes=[("dr", "hsb", sc)], label=f"sth{sc}")
            g.dma("sy", lambda e, w0=w0: e.dma_start(
                out=sgb_d.ap().rearrange("(db p) t -> p db t", p=128)
                [:, :, w0:w0 + IP_SC],
                in_=sgb[:, :].rearrange("p (db t) -> p db t", db=4)),
                reads=[("sg", db) for db in range(4)],
                writes=[("dr", "sgb", sc)], label=f"stg{sc}")
            for fw in range(2):
                def xmm(e, fw=fw):
                    last = None
                    for db in range(4):
                        last = e.matmul(
                            psB[fw][:, :],
                            xpj_sb[:, db * 160:db * 160 + 128],
                            hsb[:, db * IP_SC + fw * 512:
                                db * IP_SC + fw * 512 + 512],
                            start=(db == 0), stop=(db == 3))
                    return last
                g.op("pe", xmm,
                     reads=[("hsb", db) for db in range(4)] + [("w5", "xpj")],
                     writes=[("psB", fw)], label=f"xts{sc}{fw}")

                def xev(e, fw=fw, w0=w0):
                    return e.tensor_copy(
                        tsacc[:, w0 + fw * 512:w0 + fw * 512 + 512],
                        psB[fw][:, :])
                g.op("v", xev, reads=[("psB", fw)], writes=[("tsa", sc, fw)],
                     label=f"xte{sc}{fw}")

                def bmm(e, fw=fw):
                    last = None
                    for db in range(4):
                        last = e.matmul(
                            psA[fw][0:32, :],
                            xpj_sb[:, db * 160 + 128:db * 160 + 160],
                            hsb[:, db * IP_SC + fw * 512:
                                db * IP_SC + fw * 512 + 512],
                            start=(db == 0), stop=(db == 3))
                    return last
                g.op("pe", bmm,
                     reads=[("hsb", db) for db in range(4)] + [("w5", "xpj")],
                     writes=[("psA", fw)], label=f"xbc{sc}{fw}")

                def bev(e, fw=fw, w0=w0):
                    return e.tensor_copy(
                        bcacc[:, w0 + fw * 512:w0 + fw * 512 + 512],
                        psA[fw][0:32, :])
                g.op("v", bev, reads=[("psA", fw)], writes=[("bca", sc, fw)],
                     label=f"xbe{sc}{fw}")
        g.dma("sy", lambda e: e.dma_start(
            out=sp_c.ap()[0:128, :], in_=tsacc[:, :]),
            reads=[("tsa", sc, fw) for sc in range(NSC) for fw in range(2)],
            writes=[("dr", "spc1")], label="stts")
        g.dma("sy", lambda e: e.dma_start(
            out=sp_c.ap()[128:160, :], in_=bcacc[:, :]),
            reads=[("bca", sc, fw) for sc in range(NSC) for fw in range(2)],
            writes=[("dr", "spc2")], label="stbc")
        g.cc(lambda e: e.collective_compute(
            "AllReduce", ALU.add, replica_groups=[CORE_IDS],
            ins=[sp_c.ap()], outs=[sp_full.ap()]),
            reads=[("dr", "spc1"), ("dr", "spc2")],
            writes=[("dr", "sp_full")], label="arsp")
        if debug:
            g.dma("sy", lambda e: e.dma_start(
                out=dmp["d_sp"].ap(), in_=sp_full.ap()),
                reads=[("dr", "sp_full")], label="dump_sp")
            g.dma("sy", lambda e: e.dma_start(
                out=dmp["d_hs"].ap(), in_=hsb_d.ap()),
                reads=[("dr", "hsb", sc) for sc in range(NSC)],
                label="dump_hs")
            g.dma("sy", lambda e: e.dma_start(
                out=dmp["d_sg"].ap(), in_=sgb_d.ap()),
                reads=[("dr", "sgb", sc) for sc in range(NSC)],
                label="dump_sg")
        st5.close()
        g.fence()

        st6 = contextlib.ExitStack()
        dtw_sb = sb("dtw_sb", [128, DIPC], BF16, st=st6)
        dtb_sb = sb("dtb_sb", [128, 4], st=st6)
        al_sb = sb("al_sb", [128, 4 * NST], st=st6)
        a_sb = sb("a_sb", [128, 4 * NST], st=st6)
        dd_sb = sb("dd_sb", [128, 4], st=st6)
        ow_sb = sb("ow_sb", [128, 4 * D], BF16, st=st6)
        tsch = sb("tsch", [128, 2 * SCAN_TC], BF16, st=st6)
        bcrep = sb("bcrep", [128, 2 * 2 * NST * SCAN_TC], BF16, st=st6)
        dtch = sb("dtch", [128, 4 * SCAN_TC], st=st6)
        dtu = sb("dtu", [128, 2 * SCAN_TC], st=st6)
        spa = sb("spa", [128, 2 * SCAN_TC], st=st6)
        spb = sb("spb", [128, 2 * SCAN_TC], st=st6)
        ach = sb("ach", [128, 2 * NST * SCAN_TC], st=st6)
        bch = sb("bch", [128, 2 * NST * SCAN_TC], BF16, st=st6)
        sch = sb("sch", [128, 2 * NST * SCAN_TC], st=st6)
        cmch = sb("cmch", [128, NST * SCAN_TC], st=st6)
        state = sb("state", [128, 4 * NST], st=st6)
        hsch = sb("hsch", [128, 2 * 4 * SCAN_TC], BF16, st=st6)
        sgch = sb("sgch", [128, 2 * 4 * SCAN_TC], BF16, st=st6)
        ybuf = sb("ybuf", [128, 2 * SCAN_TC], st=st6)
        zsc = sb("zsc", [128, SCAN_TC], st=st6)
        zsc2 = sb("zsc2", [128, SCAN_TC], st=st6)
        zb = sb("zb", [128, 2 * 4 * SCAN_TC], BF16, st=st6)
        obuf = sb("obuf", [128, D], st=st6)

        g.dma("g", lambda e: e.dma_start(out=dtw_sb[:, :],
                                         in_=dtwT_sl.ap()),
              writes=[("w6", "dtw")], label="ld_dtw")
        g.dma("sy", lambda e: e.dma_start(
            out=dtb_sb[:, :], in_=dtb_sl.ap()),
            writes=[("w6", "dtb")], label="ld_dtb")
        g.dma("sy", lambda e: e.dma_start(
            out=al_sb[:, :], in_=alog_sl.ap()),
            writes=[("w6", "al")], label="ld_al")
        g.dma("sy", lambda e: e.dma_start(
            out=dd_sb[:, :], in_=d_sl.ap()),
            writes=[("w6", "dd")], label="ld_dd")
        g.dma("g", lambda e: e.dma_start(
            out=ow_sb[:, :].rearrange("p (db o) -> p db o", db=4),
            in_=outwT_sl.ap().rearrange("(db p) o -> p db o", p=128)),
            writes=[("w6", "ow")], label="ld_ow")
        g.op("a", lambda e: e.activation(a_sb[:, :], al_sb[:, :], AF.Exp),
             reads=[("w6", "al")], writes=[("w6", "ae")], label="aexp")
        g.op("v", lambda e: e.tensor_scalar(
            a_sb[:, :], a_sb[:, :], -1.0, None, ALU.mult),
            reads=[("w6", "ae")], writes=[("w6", "a")], label="aneg")
        g.op("v", lambda e: e.memset(state[:, :], 0), writes=[("st", 0),
             ("st", 1), ("st", 2), ("st", 3)], label="st0")

        def emit_ldts(tc):
            w0 = tc * SCAN_TC
            pp_tc = tc % 2
            TSO = pp_tc * SCAN_TC
            g.dma("g", lambda e, w0=w0, TSO=TSO: e.dma_start(
                out=tsch[:, TSO:TSO + SCAN_TC],
                in_=sp_full.ap()[0:128, w0:w0 + SCAN_TC]),
                reads=[("dr", "sp_full")], writes=[("tsch", pp_tc)],
                label=f"ldts{tc}")

        def emit_ldbig(tc):
            w0 = tc * SCAN_TC
            pp_tc = tc % 2
            HSO = pp_tc * 4 * SCAN_TC
            RPO = pp_tc * NST * SCAN_TC

            def ldbc(e, w0=w0, RPO=RPO):
                return e.dma_start(
                    out=bcrep[:, 2 * RPO:2 * RPO + 2 * NST * SCAN_TC]
                    .rearrange("p (n t) -> p n t", n=2 * NST),
                    in_=sp_full.ap().rearrange("(o r) t -> o r t", o=1)
                    [0:1, 128:160, w0:w0 + SCAN_TC]
                    .broadcast_to([128, 2 * NST, SCAN_TC]))
            g.dma("g", ldbc, reads=[("dr", "sp_full")],
                  writes=[("bcrep", pp_tc)], label=f"ldbc{tc}")
            g.dma("sy", lambda e, w0=w0, HSO=HSO: e.dma_start(
                out=hsch[:, HSO:HSO + 4 * SCAN_TC].rearrange(
                    "p (db t) -> p db t", db=4),
                in_=hsb_d.ap().rearrange("(db p) t -> p db t", p=128)
                [:, :, w0:w0 + SCAN_TC]),
                reads=[("dr", "hsb", w0 // IP_SC)], writes=[("hsch", pp_tc)],
                label=f"ldhs{tc}")
            g.dma("sy", lambda e, w0=w0, HSO=HSO: e.dma_start(
                out=sgch[:, HSO:HSO + 4 * SCAN_TC].rearrange(
                    "p (db t) -> p db t", db=4),
                in_=sgb_d.ap().rearrange("(db p) t -> p db t", p=128)
                [:, :, w0:w0 + SCAN_TC]),
                reads=[("dr", "sgb", w0 // IP_SC)], writes=[("sgch", pp_tc)],
                label=f"ldsg{tc}")

        def stA(u):
            tc, db = divmod(u, NDB)
            pp_tc = tc % 2
            TSO = pp_tc * SCAN_TC
            pp = db % 2
            dps = psB[1] if pp == 0 else psB[0]
            if db == 0 and tc + 1 < N_TC:
                emit_ldts(tc + 1)
            if db == 0 and tc >= 1:
                emit_ldbig(tc)

            def dmm(e, db=db, dps=dps, TSO=TSO):
                return e.matmul(
                    dps[:, :SCAN_TC], dtw_sb[:, db * 128:(db + 1) * 128],
                    tsch[:, TSO:TSO + SCAN_TC], start=True, stop=True)
            g.op("pe", dmm, reads=[("tsch", pp_tc), ("w6", "dtw")],
                 writes=[("dtps", pp)], label=f"dmm{tc}_{db}")
            g.op("a", lambda e, db=db, pp=pp, dps=dps: e.activation(
                spa[:, pp * SCAN_TC:(pp + 1) * SCAN_TC],
                dps[:, :SCAN_TC], AF.Abs, bias=dtb_sb[:, db:db + 1]),
                reads=[("dtps", pp), ("w6", "dtb")], writes=[("spa", pp)],
                label=f"sp1{tc}_{db}")
            g.op("a", lambda e, pp=pp: e.activation(
                spa[:, pp * SCAN_TC:(pp + 1) * SCAN_TC],
                spa[:, pp * SCAN_TC:(pp + 1) * SCAN_TC],
                AF.Exp, scale=-1.0),
                reads=[("spa", pp)], writes=[("spa", pp)],
                label=f"sp2{tc}_{db}")
            g.op("a", lambda e, pp=pp: e.activation(
                spa[:, pp * SCAN_TC:(pp + 1) * SCAN_TC],
                spa[:, pp * SCAN_TC:(pp + 1) * SCAN_TC],
                AF.Ln, bias=1.0),
                reads=[("spa", pp)], writes=[("spa", pp)],
                label=f"sp3{tc}_{db}")
            g.op("a", lambda e, db=db, pp=pp, dps=dps: e.activation(
                spb[:, pp * SCAN_TC:(pp + 1) * SCAN_TC],
                dps[:, :SCAN_TC], AF.Relu, bias=dtb_sb[:, db:db + 1]),
                reads=[("dtps", pp), ("w6", "dtb")], writes=[("spb", pp)],
                label=f"sp4{tc}_{db}")
        def stA2(u):
            tc, db = divmod(u, NDB)
            pp = db % 2
            g.op("v", lambda e, db=db, pp=pp: e.tensor_tensor(
                dtch[:, db * SCAN_TC:(db + 1) * SCAN_TC],
                spa[:, pp * SCAN_TC:(pp + 1) * SCAN_TC],
                spb[:, pp * SCAN_TC:(pp + 1) * SCAN_TC], ALU.add),
                reads=[("spa", pp), ("spb", pp)],
                writes=[("dtch", db), ("dtps", pp)],
                label=f"sp5{tc}_{db}")

        def stB(u):
            tc, db = divmod(u, NDB)
            pp_tc = tc % 2
            HSO = pp_tc * 4 * SCAN_TC
            RPO = pp_tc * NST * SCAN_TC
            pp = db % 2

            def aex(e, db=db, pp=pp):
                last = None
                for n in range(NST):
                    last = e.activation(
                        ach[:, (pp * NST + n) * SCAN_TC:
                            (pp * NST + n + 1) * SCAN_TC],
                        dtch[:, db * SCAN_TC:(db + 1) * SCAN_TC],
                        AF.Exp,
                        scale=a_sb[:, db * NST + n:db * NST + n + 1])
                return last
            g.op("a", aex, reads=[("dtch", db), ("w6", "a")],
                 writes=[("ach", pp)], label=f"aex{tc}_{db}")

            par = u % 2

            def du(e, db=db, HSO=HSO, par=par):
                return e.tensor_tensor(
                    dtu[:, par * SCAN_TC:(par + 1) * SCAN_TC],
                    dtch[:, db * SCAN_TC:(db + 1) * SCAN_TC],
                    hsch[:, HSO + db * SCAN_TC:
                         HSO + (db + 1) * SCAN_TC], ALU.mult)
            g.op("v", du, reads=[("dtch", db), ("hsch", pp_tc)],
                 writes=[("dtu", par)], label=f"dtu{tc}_{db}")

            def bop(e, db=db, pp=pp, RPO=RPO, par=par):
                return e.tensor_tensor(
                    bch[:, pp * NST * SCAN_TC:(pp + 1) * NST * SCAN_TC]
                    .rearrange("p (n t) -> p n t", n=NST),
                    dtu[:, par * SCAN_TC:(par + 1) * SCAN_TC]
                    .rearrange("p (o t) -> p o t", o=1)
                    .broadcast_to([128, NST, SCAN_TC]),
                    bcrep[:, 2 * RPO:2 * RPO + NST * SCAN_TC]
                    .rearrange("p (n t) -> p n t", n=NST),
                    ALU.mult)
            g.op("v", bop, reads=[("dtu", par), ("bcrep", pp_tc)],
                 writes=[("bch", pp)], label=f"bop{tc}_{db}")

        def stC_scn(u):
            tc, db = divmod(u, NDB)
            pp = db % 2
            par = u % 2
            SCO = par * NST * SCAN_TC

            def scn(e, db=db, pp=pp, SCO=SCO):
                last = None
                for n in range(NST):
                    last = e.tensor_tensor_scan(
                        sch[:, SCO + n * SCAN_TC:SCO + (n + 1) * SCAN_TC],
                        ach[:, (pp * NST + n) * SCAN_TC:
                            (pp * NST + n + 1) * SCAN_TC],
                        bch[:, (pp * NST + n) * SCAN_TC:
                            (pp * NST + n + 1) * SCAN_TC],
                        state[:, db * NST + n:db * NST + n + 1],
                        ALU.mult, ALU.add)
                return last
            g.op("v", scn,
                 reads=[("ach", pp), ("bch", pp), ("st", db)],
                 writes=[("sch", par), ("ach", pp), ("bch", pp)],
                 label=f"scn{tc}_{db}")

        def stC_cmu(u):
            tc, db = divmod(u, NDB)
            par = u % 2
            SCO = par * NST * SCAN_TC
            pp_tc = tc % 2
            RPO = pp_tc * NST * SCAN_TC

            def cmu(e, RPO=RPO, SCO=SCO):
                return e.tensor_tensor(
                    cmch[:, :],
                    sch[:, SCO:SCO + NST * SCAN_TC],
                    bcrep[:, 2 * RPO + NST * SCAN_TC:
                          2 * RPO + 2 * NST * SCAN_TC],
                    ALU.mult)
            g.op("v", cmu, reads=[("sch", par), ("bcrep", pp_tc)],
                 writes=[("cmch",)], label=f"cmu{tc}_{db}")

        def stC2(u):
            tc, db = divmod(u, NDB)
            par = u % 2
            SCO = par * NST * SCAN_TC

            def stu(e, db=db, SCO=SCO):
                return e.activation(
                    state[:, db * NST:(db + 1) * NST],
                    sch[:, SCO:SCO + NST * SCAN_TC]
                    .rearrange("p (n t) -> p n t", n=NST)
                    [:, :, SCAN_TC - 1:SCAN_TC].rearrange(
                        "p n o -> p (n o)"), AF.Copy)
            g.op("a", stu, reads=[("sch", par)],
                 writes=[("st", db)], label=f"stu{tc}_{db}")

        def stD1(u):
            tc, db = divmod(u, NDB)
            par = u % 2

            def yrd(e, par=par):
                h = NST * SCAN_TC
                while h > 2 * SCAN_TC:
                    h //= 2
                    e.tensor_tensor(cmch[:, :h], cmch[:, :h],
                                    cmch[:, h:2 * h], ALU.add)
                return e.tensor_tensor(
                    ybuf[:, par * SCAN_TC:(par + 1) * SCAN_TC],
                    cmch[:, :SCAN_TC], cmch[:, SCAN_TC:2 * SCAN_TC],
                    ALU.add)
            g.op("v", yrd, reads=[("cmch",)],
                 writes=[("yb", par), ("cmch",)], label=f"yrd{tc}_{db}")

        def stD2(u):
            tc, db = divmod(u, NDB)
            pp_tc = tc % 2
            HSO = pp_tc * 4 * SCAN_TC
            par = u % 2
            ZO = pp_tc * 4 * SCAN_TC

            def z1a(e, db=db, HSO=HSO):
                return e.tensor_tensor(
                    zsc[:, :],
                    hsch[:, HSO + db * SCAN_TC:HSO + (db + 1) * SCAN_TC],
                    dd_sb[:, db:db + 1].broadcast_to([128, SCAN_TC]),
                    ALU.mult)
            g.op("v", z1a, reads=[("hsch", pp_tc), ("w6", "dd")],
                 writes=[("zsc",)], label=f"z1a{tc}_{db}")

            def z1b(e, par=par):
                return e.tensor_tensor(
                    zsc2[:, :], zsc[:, :],
                    ybuf[:, par * SCAN_TC:(par + 1) * SCAN_TC], ALU.add)
            g.op("v", z1b, reads=[("zsc",), ("yb", par)],
                 writes=[("zsc2",)], label=f"z1b{tc}_{db}")

            def z2(e, db=db, HSO=HSO, ZO=ZO):
                return e.tensor_tensor(
                    zb[:, ZO + db * SCAN_TC:ZO + (db + 1) * SCAN_TC],
                    zsc2[:, :],
                    sgch[:, HSO + db * SCAN_TC:HSO + (db + 1) * SCAN_TC],
                    ALU.mult)
            g.op("v", z2, reads=[("zsc2",), ("sgch", pp_tc)],
                 writes=[("zb", pp_tc, db)], label=f"z2_{tc}_{db}")
            if db == NDB - 1:
                emit_oproj(tc)

        def emit_oproj(tc):
            pp_tc = tc % 2
            ZO = pp_tc * 4 * SCAN_TC
            for t1 in range(SCAN_TC // 128):
                def omm(e, t1=t1, ZO=ZO):
                    last = None
                    for db in range(4):
                        for oc in range(4):
                            last = e.matmul(
                                psA[oc][:, :],
                                zb[:, ZO + db * SCAN_TC + t1 * 128:
                                   ZO + db * SCAN_TC + (t1 + 1) * 128],
                                ow_sb[:, db * D + oc * 512:
                                      db * D + oc * 512 + 512],
                                start=(db == 0), stop=(db == 3))
                    return last
                g.op("pe", omm,
                     reads=[("zb", pp_tc, db) for db in range(4)]
                     + [("w6", "ow")],
                     writes=[("psA", 0), ("psA", 1), ("psA", 2), ("psA", 3)],
                     label=f"omm{tc}_{t1}")
                for oc in range(4):
                    def oev(e, oc=oc):
                        return e.activation(
                            obuf[:, oc * 512:(oc + 1) * 512], psA[oc][:, :],
                            AF.Copy)
                    g.op("a", oev, reads=[("psA", oc)], writes=[("obuf", oc)],
                         label=f"oev{tc}_{t1}_{oc}")
                g.dma("sy", lambda e, tc=tc, t1=t1: e.dma_start(
                    out=rs_in.ap()[tc * SCAN_TC + t1 * 128:
                                   tc * SCAN_TC + (t1 + 1) * 128, :],
                    in_=obuf[:, :]),
                    reads=[("obuf", oc) for oc in range(4)],
                    writes=[("rsw", tc, t1)], label=f"ost{tc}_{t1}")

        emit_ldts(0)
        emit_ldbig(0)
        U = N_TC * NDB
        for r in range(U + 5):
            if 0 <= r - 2 < U:
                stB(r - 2)
            if 0 <= r - 4 < U:
                stC_cmu(r - 4)
            if 0 <= r - 4 < U:
                stD1(r - 4)
            if r < U:
                stA(r)
            if 0 <= r - 3 < U:
                stC_scn(r - 3)
            if 0 <= r - 3 < U:
                stC2(r - 3)
            if 0 <= r - 4 < U:
                stD2(r - 4)
            if 0 <= r - 1 < U:
                stA2(r - 1)
        g.cc(lambda e: e.collective_compute(
            "ReduceScatter", ALU.add, replica_groups=[CORE_IDS],
            ins=[rs_in.ap()], outs=[rs_out.ap()]),
            reads=[("rsw", tc, t1) for tc in range(N_TC)
                   for t1 in range(SCAN_TC // 128)],
            writes=[("dr", "rs_out")], label="rsop")
        if debug:
            g.dma("sy", lambda e: e.dma_start(
                out=dmp["d_rs"].ap(), in_=rs_out.ap()),
                reads=[("dr", "rs_out")], label="dump_rs")
        st6.close()
        g.fence()

        st7 = contextlib.ExitStack()
        x4buf = sb("x4buf", [128, NTT * D], st=st7)
        ob_sb = sb("ob_sb", [128, D], st=st7)
        x3t = sb("x3t", [128, D], st=st7)
        rst = sb("rst", [128, D], st=st7)
        g.dma("sy", lambda e: e.dma_start(out=ob_sb[:, :], in_=outb_rep.ap()),
              writes=[("ob",)], label="ldob")
        for tt in range(NTT):
            g.dma("sy", lambda e, tt=tt: e.dma_start(
                out=x3t[:, :],
                in_=x3_d.ap()[tt * 128:(tt + 1) * 128, :]),
                reads=[("dr", "x3_d")], writes=[("x3t",)],
                label=f"ldx3_{tt}")
            g.dma("sy", lambda e, tt=tt: e.dma_start(
                out=rst[:, :],
                in_=rs_out.ap()[tt * 128:(tt + 1) * 128, :]),
                reads=[("dr", "rs_out")], writes=[("rstt",)],
                label=f"ldrs_{tt}")

            def a1(e, tt=tt):
                return e.tensor_tensor(
                    x4buf[:, tt * D:(tt + 1) * D],
                    x3t[:, :], rst[:, :], ALU.add)
            g.op("v", a1, reads=[("x3t",), ("rstt",)],
                 writes=[("x4a", tt)], label=f"x4a{tt}")

            def a2(e, tt=tt):
                return e.tensor_tensor(
                    x4buf[:, tt * D:(tt + 1) * D],
                    x4buf[:, tt * D:(tt + 1) * D], ob_sb[:, :], ALU.add)
            g.op("v", a2, reads=[("x4a", tt), ("ob",)],
                 writes=[("row", "x4", tt)], label=f"x4b{tt}")
        build_ffn(x4buf, "x4", "x5", out_rows)
        st7.close()

        g.op("sy", lambda e: e.nop(),
             reads=[("outst", tt) for tt in range(NTT)], label="final")

        g.plan()
        build_program._last_g = g
        g.emit(nc, semmap)

    return nc


def _prep_inputs(inputs):
    x = np.asarray(inputs["L0_attn_a_x"])
    wqkv = np.asarray(inputs["L0_attn_a_Wqkv_w"])
    bqkv = np.asarray(inputs["L0_attn_a_Wqkv_b"])
    in_w = np.asarray(inputs["L0_attn_b_in_w"])
    in_b = np.asarray(inputs["L0_attn_b_in_b"])
    conv_w = np.asarray(inputs["L0_attn_b_conv_w"])
    conv_b = np.asarray(inputs["L0_attn_b_conv_b"])
    xproj_w = np.asarray(inputs["L0_attn_b_xproj_w"])
    dt_w = np.asarray(inputs["L0_attn_b_dt_w"])
    dt_b = np.asarray(inputs["L0_attn_b_dt_b"])
    a_log = np.asarray(inputs["L0_attn_b_A_log"])
    d_ssm = np.asarray(inputs["L0_attn_b_D"])
    out_w = np.asarray(inputs["L0_attn_b_out_w"])
    out_b = np.asarray(inputs["L0_attn_b_out_b"])
    w_base = np.asarray(inputs["L0_ffn_W_base"])
    b_base = np.asarray(inputs["L0_ffn_b_base"])
    router_w = np.asarray(inputs["L0_ffn_router_W"])
    a_lora = np.asarray(inputs["L0_ffn_A"])
    bw_lora = np.asarray(inputs["L0_ffn_Bw"])

    C = np.ascontiguousarray

    inv = (1.0 / 10000.0 ** (np.arange(0, 32, 2, dtype=np.float32) / 32.0))
    t = np.arange(T, dtype=np.float32)
    fr = np.outer(t, inv)
    cosg = np.tile(np.cos(fr).T.astype(np.float32), (8, 1))
    sing = np.tile(np.sin(fr).T.astype(np.float32), (8, 1))
    for base in (0, 64):
        for u in (0, 2):
            cosg[base + u * 16: base + (u + 1) * 16] *= np.float32(SCALE)
            sing[base + u * 16: base + (u + 1) * 16] *= np.float32(SCALE)
    idb = np.eye(128, dtype=np.float32).astype(BF16NP)
    idf = np.eye(128, dtype=np.float32)
    trm = np.triu(np.full((128, 128), -10000.0, np.float32), 1).astype(BF16NP)
    ones1 = np.ones((1, 128), np.float32)
    wbT = C(w_base.T)
    bbase_rep = np.tile(b_base[None, :], (128, 1)).astype(np.float32)
    routerT = C(router_w.T)
    aT = C(a_lora.reshape(128, D).T)
    bwT = C(np.transpose(bw_lora, (0, 2, 1)).reshape(128, D))
    outb_rep = np.tile(out_b[None, :], (128, 1)).astype(np.float32)

    maps = []
    for c in range(NCORES):
        m = {}
        m["x_rows"] = C(x[0, c * TPC:(c + 1) * TPC, :])
        m["cid_in"] = np.array([[c]], np.int32)
        h0, h1 = 2 * c, 2 * c + 1
        unit_list = [("q", h0), ("k", h0), ("q", h1), ("k", h1)]

        def urows(kind, hh):
            base = (0 if kind == "q" else D) + hh * 128
            return base
        rot_idx = []
        for part in range(2):
            for kind, hh in unit_list:
                b0 = urows(kind, hh) + part * 16
                rot_idx.extend(range(b0, b0 + 16))
        m["w_rot"] = C(wqkv[rot_idx, :].T)
        m["b_rot"] = C(bqkv[rot_idx].reshape(128, 1))
        wp = []
        bp = []
        for kind, hh in unit_list:
            b0 = urows(kind, hh)
            wp.append(wqkv[b0 + 32:b0 + 128, :])
            bp.append(bqkv[b0 + 32:b0 + 128])
        m["w_pass"] = C(np.concatenate(wp, axis=0).T)
        bps = np.stack(bp, axis=1)
        bps[:, 0] *= np.float32(SCALE)
        bps[:, 2] *= np.float32(SCALE)
        m["b_pass"] = C(bps)
        vrows = list(range(2 * D + h0 * 128, 2 * D + h0 * 128 + 128)) + \
            list(range(2 * D + h1 * 128, 2 * D + h1 * 128 + 128))
        m["w_v"] = C(wqkv[vrows, :].T)
        m["b_v"] = C(bqkv[vrows].reshape(2, 128).T)
        perm = np.zeros((128, 128), np.float32)
        for u in range(4):
            for j_ in range(16):
                perm[u * 16 + j_, u * 32 + j_] = 1.0
                perm[64 + u * 16 + j_, u * 32 + 16 + j_] = 1.0
        m["perm_all"] = perm.astype(BF16NP)
        m["cosg"] = cosg
        m["sing"] = sing
        m["wbT"] = wbT
        m["bbase_rep"] = bbase_rep
        m["routerT"] = routerT
        m["aT"] = aT
        m["bwT"] = bwT
        s = slice(c * DIPC, (c + 1) * DIPC)
        m["inwT_sl"] = C(np.concatenate(
            [in_w[c * DIPC:(c + 1) * DIPC, :],
             in_w[DI + c * DIPC:DI + (c + 1) * DIPC, :]], axis=0).T)
        m["inb_sl"] = C(np.concatenate(
            [in_b[c * DIPC:(c + 1) * DIPC],
             in_b[DI + c * DIPC:DI + (c + 1) * DIPC]]).reshape(8, 128).T)
        m["convw_sl"] = C(conv_w[s, 0, :].reshape(4, 128, 4).transpose(1, 0, 2).reshape(128, 16))
        m["convb_sl"] = C(conv_b[s].reshape(4, 128).T)
        m["xprojT_sl"] = C(xproj_w[:, s].T)
        m["dtwT_sl"] = C(dt_w[s, :].T)
        m["dtb_sl"] = C(dt_b[s].reshape(4, 128).T)
        m["alog_sl"] = C(a_log[s, :].reshape(4, 128, NST).transpose(1, 0, 2).reshape(128, 4 * NST))
        m["d_sl"] = C(d_ssm[s].reshape(4, 128).T)
        m["outwT_sl"] = C(out_w[:, s].T)
        m["outb_rep"] = outb_rep
        m["idb"] = idb
        m["idf"] = idf
        m["trm"] = trm
        m["ones1"] = ones1
        maps.append(m)
    return maps


_NC_CACHE = {}


def kernel(**inputs):
    import os
    debug = bool(int(os.environ.get("MK_DEBUG", "0")))
    maps = _prep_inputs(inputs)
    key = ("nc", debug)
    if key not in _NC_CACHE:
        _NC_CACHE[key] = build_program(debug=debug)
    nc = _NC_CACHE[key]
    res = run_bass_kernel_spmd(nc, maps, core_ids=CORE_IDS)
    if debug:
        kernel._dbg = res.results
    out = np.concatenate([res.results[c]["out_rows"] for c in range(NCORES)],
                         axis=0)
    return out.reshape(1, T, D).astype(np.float32)


# version 23
# speedup vs baseline: 1.0589x; 1.0043x over previous
import contextlib

import numpy as np
import ml_dtypes

import concourse.bass as bass
import concourse.mybir as mybir
from concourse.bass_utils import run_bass_kernel_spmd

F32 = mybir.dt.float32
BF16 = mybir.dt.bfloat16
I32 = mybir.dt.int32
AF = mybir.ActivationFunctionType
ALU = mybir.AluOpType
AX = mybir.AxisListType
BF16NP = ml_dtypes.bfloat16

NCORES = 8
T = 8192
D = 2048
TPC = T // NCORES
NTT = TPC // 128
DI = 4096
DIPC = DI // NCORES
NDB = DIPC // 128
NST = 16
RK = 128
EPS = 1e-5
SCALE = float(1.0 / np.sqrt(128.0))
QKV_SC = 1024
IP_SC = 1024
NSC = T // QKV_SC
SCAN_TC = 256
N_TC = T // SCAN_TC

ENGS = ("sy", "pe", "v", "a", "g")
BKNAME = {"sy": "sync", "pe": "tensor", "v": "vector", "a": "scalar",
          "g": "gpsimd"}
CORE_IDS = list(range(NCORES))


class Op:
    __slots__ = ("eng", "fn", "deps", "kind", "ndma", "needed", "ms", "waits",
                 "label", "dma_sem", "dma_val")

    def __init__(self, eng, fn, deps, kind, label, ndma=1):
        self.eng = eng
        self.fn = fn
        self.deps = deps
        self.kind = kind
        self.ndma = ndma
        self.needed = False
        self.ms = None
        self.waits = None
        self.label = label


N_DMASEM = 64


class G:

    def __init__(self):
        self.ops = []
        self.reg = {}
        self.last = {e: None for e in ENGS}
        self.dma_idx = 0
        self.dma_prev = [None] * N_DMASEM
        self.dma_cnt = [0] * N_DMASEM
        self.cur_dma_sem = None

    def op(self, eng, fn, reads=(), writes=(), kind="c", label="", ndma=1):
        deps = set()
        for k in reads:
            st = self.reg.get(k)
            if st is not None and st[0] is not None:
                deps.add(st[0])
        for k in writes:
            st = self.reg.get(k)
            if st is not None:
                deps.update(st[1])
                if st[0] is not None:
                    deps.add(st[0])
        o = Op(eng, fn, deps, kind, label, ndma)
        if kind == "d":
            si = self.dma_idx % N_DMASEM
            self.dma_idx += 1
            o.dma_sem = si
            if self.dma_prev[si] is not None:
                deps.add(self.dma_prev[si])
            self.dma_prev[si] = o
            self.dma_cnt[si] += 16 * ndma
            o.dma_val = self.dma_cnt[si]
        self.ops.append(o)
        for k in reads:
            self.reg.setdefault(k, [None, []])[1].append(o)
        for k in writes:
            self.reg[k] = [o, []]
        self.last[eng] = o
        return o

    def dma(self, eng, fn, reads=(), writes=(), label=""):
        return self.op(eng, fn, reads, writes, kind="d", label=label)

    def dmam(self, eng, fn, n, reads=(), writes=(), label=""):
        return self.op(eng, fn, reads, writes, kind="d", label=label, ndma=n)

    def cc(self, fn, reads=(), writes=(), label=""):
        return self.op("g", fn, reads, writes, kind="cc", label=label)

    def fence(self):
        lasts = set(o for o in self.last.values() if o is not None)
        for e in ENGS:
            o = Op(e, (lambda eng: eng.nop()), set(lasts), "c", "fence")
            self.ops.append(o)
            self.last[e] = o
        self.reg.clear()

    def plan(self):
        for o in self.ops:
            for d in o.deps:
                d.needed = True
        cnt = {}
        for o in self.ops:
            if o.kind == "d":
                o.ms = (("ds", o.dma_sem), o.dma_val)
            elif o.kind == "cc":
                cnt["cc"] = cnt.get("cc", 0) + 1
                o.ms = ("cc", cnt["cc"])
            elif o.needed:
                key = ("c", o.eng)
                cnt[key] = cnt.get(key, 0) + 1
                o.ms = (key, cnt[key])
        waited = {e: {} for e in ENGS}
        for o in self.ops:
            w = {}
            for d in o.deps:
                if d.ms is None:
                    continue
                k, v = d.ms
                if w.get(k, 0) < v:
                    w[k] = v
            W = waited[o.eng]
            o.waits = []
            for k, v in sorted(w.items(), key=lambda kv: str(kv[0])):
                if W.get(k, 0) < v:
                    o.waits.append((k, v))
                    W[k] = v

    def emit(self, nc, semmap):
        with nc.Block() as bk:
            def mk(ename):
                def body(eng):
                    for o in self.ops:
                        if o.eng != ename:
                            continue
                        for k, v in o.waits:
                            eng.wait_ge(semmap[k], v)
                        if o.kind == "d":
                            self.cur_dma_sem = semmap[("ds", o.dma_sem)]
                        inst = o.fn(eng)
                        if o.kind == "d":
                            inst.then_inc(semmap[("ds", o.dma_sem)], 16)
                        elif o.kind == "cc":
                            inst.then_inc(semmap["cc"], 1)
                        elif o.needed:
                            inst.then_inc(semmap[("c", ename)], 1)
                return body
            for ename in ENGS:
                getattr(bk, BKNAME[ename])(mk(ename))


def build_program(debug=False):
    nc = bass.Bass("TRN2", target_bir_lowering=False, num_devices=NCORES)
    g = G()

    def inp(name, shape, dt=F32):
        return nc.dram_tensor(name, shape, dt, kind="ExternalInput")

    x_rows = inp("x_rows", [TPC, D])
    cid_in = inp("cid_in", [1, 1], I32)
    w_rot = inp("w_rot", [D, 128])
    b_rot = inp("b_rot", [128, 1])
    w_pass = inp("w_pass", [D, 384])
    b_pass = inp("b_pass", [96, 4])
    w_v = inp("w_v", [D, 256])
    b_v = inp("b_v", [128, 2])
    perm_all = inp("perm_all", [128, 128], BF16)
    cosg = inp("cosg", [128, T])
    sing = inp("sing", [128, T])
    wbT = inp("wbT", [D, D])
    bbase_rep = inp("bbase_rep", [128, D])
    routerT = inp("routerT", [D, 8])
    aT = inp("aT", [D, 128])
    bwT = inp("bwT", [128, D])
    inwT_sl = inp("inwT_sl", [D, 1024])
    inb_sl = inp("inb_sl", [128, 8])
    convw_sl = inp("convw_sl", [128, 16])
    convb_sl = inp("convb_sl", [128, 4])
    xprojT_sl = inp("xprojT_sl", [DIPC, 160])
    dtwT_sl = inp("dtwT_sl", [RK, DIPC])
    dtb_sl = inp("dtb_sl", [128, 4])
    alog_sl = inp("alog_sl", [128, 4 * NST])
    d_sl = inp("d_sl", [128, 4])
    outwT_sl = inp("outwT_sl", [DIPC, D])
    outb_rep = inp("outb_rep", [128, D])
    idb = inp("idb", [128, 128], BF16)
    idf = inp("idf", [128, 128])
    trm = inp("trm", [128, 128], BF16)
    ones1 = inp("ones1", [1, 128])

    out_rows = nc.dram_tensor("out_rows", [TPC, D], F32, kind="ExternalOutput")
    dmp = {}
    if debug:
        for nm, shape, dt in (("d_xnf", [D, TPC], BF16),
                              ("d_ao", [T, 256], BF16),
                              ("d_x3", [TPC, D], F32),
                              ("d_sp", [160, T], F32),
                              ("d_hs", [DIPC, T], BF16),
                              ("d_sg", [DIPC, T], BF16),
                              ("d_rs", [TPC, D], F32),
                              ("d_x2", [TPC, D], F32),
                              ("d_fb", [128, D], F32),
                              ("d_lg", [128, 8], F32),
                              ("d_hm", [128, 128], F32),
                              ("d_xn2", [128, D], BF16),
                              ("d_xnT", [128, 16 * 128], BF16)):
            dmp[nm] = nc.dram_tensor(nm, shape, dt, kind="ExternalOutput")

    def dram(name, shape, dt=F32):
        return nc.dram_tensor(name, shape, dt, kind="Internal")

    xn_c = dram("xn_c", [D, TPC], BF16)
    xn_full = dram("xn_full", [NCORES * D, TPC], BF16)
    ao_c = dram("ao_c", [T, 256], BF16)
    ao_full = dram("ao_full", [NCORES * T, 256], BF16)
    sp_c = dram("sp_c", [160, T])
    sp_full = dram("sp_full", [160, T])
    hsb_d = dram("hsb_d", [DIPC, T], BF16)
    sgb_d = dram("sgb_d", [DIPC, T], BF16)
    x3_d = dram("x3_d", [TPC, D])
    rs_in = dram("rs_in", [T, D])
    rs_out = dram("rs_out", [TPC, D])

    stk = contextlib.ExitStack()
    with stk:
        def sb(name, shape, dt=F32, st=None):
            return (st or stk).enter_context(nc.sbuf_tensor(name, shape, dt))

        psA = [stk.enter_context(nc.psum_tensor(f"psA{i}", [128, 512], F32))
               for i in range(4)]
        psB = [stk.enter_context(nc.psum_tensor(f"psB{i}", [128, 512], F32))
               for i in range(2)]
        psM = stk.enter_context(nc.psum_tensor("psM", [128, 512], F32))
        psT = stk.enter_context(nc.psum_tensor("psT", [128, 1024], BF16))

        c_idb = sb("c_idb", [128, 128], BF16)
        c_idf = sb("c_idf", [128, 128])
        c_trm = sb("c_trm", [128, 128], BF16)
        c_ones = sb("c_ones", [1, 128])
        c_onesb = sb("c_onesb", [1, 128], BF16)
        c_cid = sb("c_cid", [1, 1], I32)
        lnsq = sb("lnsq", [128, D])
        lnst = sb("lnst", [128, 16])

        semmap = {}
        for e in ENGS:
            semmap[("c", e)] = stk.enter_context(nc.semaphore(f"sC{e}"))
        for i in range(N_DMASEM):
            semmap[("ds", i)] = stk.enter_context(nc.semaphore(f"sDp{i}"))
        semmap["cc"] = stk.enter_context(nc.semaphore("sCC"))

        g.dma("g", lambda e: e.dma_start(out=c_onesb[:, :], in_=ones1.ap()),
              writes=[("const", "onesb")], label="ldc_onesb")
        for t_, s_, n_ in ((idb, c_idb, "idb"), (idf, c_idf, "idf"),
                           (trm, c_trm, "trm"),
                           (ones1, c_ones, "ones"), (cid_in, c_cid, "cid")):
            g.dma("sy", lambda e, t=t_, s=s_: e.dma_start(out=s[:, :],
                                                          in_=t.ap()),
                  writes=[("const", n_)], label=f"ldc_{n_}")

        def emit_ln(xap, rkey, out_ap, okey, tag):
            g.op("a", lambda e: e.activation(lnsq[:, :], xap, AF.Square),
                 reads=[rkey], writes=[("lnsq",)], label=f"lnsq{tag}")
            g.op("v", lambda e: e.tensor_reduce(
                lnst[:, 0:1], xap, AX.X, ALU.add, negate=True),
                reads=[rkey], writes=[("lt0",)], label=f"lns0{tag}")
            g.op("v", lambda e: e.tensor_reduce(
                lnst[:, 1:2], lnsq[:, :], AX.X, ALU.add),
                reads=[("lnsq",)], writes=[("lt1",)], label=f"lns1{tag}")
            g.op("v", lambda e: e.tensor_scalar(
                lnst[:, 2:3], lnst[:, 0:1], 1.0 / D, None, ALU.mult),
                reads=[("lt0",)], writes=[("lt2",)], label=f"lnm{tag}")
            g.op("v", lambda e: e.tensor_scalar(
                lnst[:, 3:4], lnst[:, 1:2], 1.0 / D, EPS, ALU.mult, ALU.add),
                reads=[("lt1",)], writes=[("lt3",)], label=f"lne{tag}")
            g.op("v", lambda e: e.tensor_tensor(
                lnst[:, 4:5], lnst[:, 2:3], lnst[:, 2:3], ALU.mult),
                reads=[("lt2",)], writes=[("lt4",)], label=f"lnm2{tag}")
            g.op("v", lambda e: e.tensor_tensor(
                lnst[:, 5:6], lnst[:, 3:4], lnst[:, 4:5], ALU.subtract),
                reads=[("lt3",), ("lt4",)], writes=[("lt5",)],
                label=f"lnv{tag}")
            g.op("a", lambda e: e.activation(lnst[:, 6:7], lnst[:, 5:6],
                                             AF.Ln),
                 reads=[("lt5",)], writes=[("lt6",)], label=f"lnl{tag}")
            g.op("a", lambda e: e.activation(lnst[:, 7:8], lnst[:, 6:7],
                                             AF.Exp, scale=-0.5),
                 reads=[("lt6",)], writes=[("lt7",)], label=f"lnr{tag}")
            g.op("v", lambda e: e.tensor_tensor(
                lnst[:, 8:9], lnst[:, 2:3], lnst[:, 7:8], ALU.mult),
                reads=[("lt2",), ("lt7",)], writes=[("lt8",)],
                label=f"lnb{tag}")
            return g.op("a", lambda e: e.activation(
                out_ap, xap, AF.Identity, bias=lnst[:, 8:9], scale=lnst[:, 7:8]),
                reads=[rkey, ("lt7",), ("lt8",)], writes=[okey],
                label=f"lnx{tag}")

        def emit_xnT_phase(rowbuf, rowtag, tag, xn_t, xnt_t):
            for tt in range(NTT):
                rb = rowbuf[:, tt * D:(tt + 1) * D]
                emit_ln(rb, ("row", rowtag, tt), xn_t[:, :], ("xnt_x", tag),
                        f"{tag}_{tt}")
                for half in range(2):
                    def tp(e, half=half):
                        last = None
                        for i in range(8):
                            et = half * 8 + i
                            last = e.transpose(
                                psT[:, i * 128:(i + 1) * 128],
                                xn_t[:, et * 128:(et + 1) * 128], c_idb[:, :])
                        return last
                    g.op("pe", tp, reads=[("xnt_x", tag), ("const", "idb")],
                         writes=[("psT",)], label=f"xtp{tag}{tt}{half}")

                    def ev(e, tt=tt, half=half):
                        return e.tensor_copy(
                            xnt_t[:, :].rearrange("p (et t) -> p et t", et=16)
                            [:, half * 8:(half + 1) * 8,
                             tt * 128:(tt + 1) * 128],
                            psT[:, :].rearrange("p (i t) -> p i t", i=8))
                    g.op("v", ev, reads=[("psT",)],
                         writes=[("xnt_o", tag, tt, half)],
                         label=f"xev{tag}{tt}{half}")
            g.dma("sy", lambda e: e.dma_start(
                out=xn_c.ap().rearrange("(et p) t -> p et t", p=128),
                in_=xnt_t[:, :].rearrange("p (et t) -> p et t", et=16)),
                reads=[("xnt_o", tag, tt, h) for tt in range(NTT)
                       for h in range(2)],
                writes=[("dr", "xn_c")], label=f"xst{tag}")
            g.cc(lambda e: e.collective_compute(
                "AllGather", ALU.bypass, replica_groups=[CORE_IDS],
                ins=[xn_c.ap()], outs=[xn_full.ap()]),
                reads=[("dr", "xn_c")], writes=[("dr", "xn_full")],
                label=f"ag{tag}")

        st0 = contextlib.ExitStack()
        xrow0 = sb("xrow0", [128, NTT * D], st=st0)
        xnsc0 = sb("xnsc0", [128, D], BF16, st=st0)
        xnts0 = sb("xnts0", [128, NTT * D], BF16, st=st0)
        g.dma("sy", lambda e: e.dma_start(
            out=xrow0[:, :].rearrange("p (t d) -> p t d", t=NTT),
            in_=x_rows.ap().rearrange("(t p) d -> p t d", p=128)),
            writes=[("row", "x1", tt) for tt in range(NTT)], label="ldx")
        emit_xnT_phase(xrow0, "x1", "x1", xnsc0, xnts0)
        if debug:
            g.dma("sy", lambda e: e.dma_start(
                out=dmp["d_xnf"].ap(), in_=xn_full.ap()[0:D, :]),
                reads=[("dr", "xn_full")], label="dump_xnf")
        st0.close()
        g.fence()

        stA = contextlib.ExitStack()
        units = [sb(f"unit{u}", [128, T], BF16, st=stA) for u in range(4)]
        vtm = sb("vtm", [128, 64 * 256], BF16, st=stA)
        vtmv = vtm[:, :].rearrange("p (tt h d) -> p tt h d", tt=64, h=2)

        stP1 = contextlib.ExitStack()
        wrot_sb = sb("wrot_sb", [128, 16 * 128], BF16, st=stP1)
        wpass_sb = sb("wpass_sb", [128, 16 * 384], BF16, st=stP1)
        wv_sb = sb("wv_sb", [128, 16 * 256], BF16, st=stP1)
        brot_sb = sb("brot_sb", [128, 1], st=stP1)
        bpass_sb = sb("bpass_sb", [96, 4], st=stP1)
        bv_sb = sb("bv_sb", [128, 2], st=stP1)
        perm_sb = sb("perm_sb", [128, 128], BF16, st=stP1)
        xnq = sb("xnq", [128, 16 * QKV_SC], BF16, st=stP1)
        rotg = sb("rotg", [128, QKV_SC], st=stP1)
        r12b = sb("r12b", [128, QKV_SC], BF16, st=stP1)
        rs1 = sb("rs1", [64, QKV_SC], st=stP1)
        rs2 = sb("rs2", [64, QKV_SC], st=stP1)
        csc = sb("csc", [128, QKV_SC], st=stP1)
        css = sb("css", [128, QKV_SC], st=stP1)
        vfm = sb("vfm", [128, QKV_SC], BF16, st=stP1)

        for t_, s_, n_ in ((w_rot, wrot_sb, 128), (w_pass, wpass_sb, 384),
                           (w_v, wv_sb, 256)):
            g.dma("g", lambda e, t=t_, s=s_, n=n_: e.dma_start(
                out=s[:, :].rearrange("p (et o) -> p et o", et=16),
                in_=t.ap().rearrange("(et p) o -> p et o", p=128)),
                writes=[("w", t_.name)], label=f"ldw_{t_.name}")
        g.dma("g", lambda e: e.dma_start(out=perm_sb[:, :],
                                         in_=perm_all.ap()),
              writes=[("w", "perm")], label="ldperm")
        for t_, s_ in ((b_rot, brot_sb), (b_pass, bpass_sb), (b_v, bv_sb)):
            g.dma("sy", lambda e, t=t_, s=s_: e.dma_start(out=s[:, :],
                                                          in_=t.ap()),
                  writes=[("w", t_.name)], label=f"ldb_{t_.name}")

        for sc in range(NSC):
            w0 = sc * QKV_SC
            g.dma("sy", lambda e, sc=sc: e.dma_start(
                out=xnq[:, :].rearrange("p (et t) -> p et t", et=16),
                in_=xn_full.ap().rearrange("(r et p) t -> r p et t",
                                           p=128, et=16)[sc]),
                reads=[("dr", "xn_full")], writes=[("xnq",)],
                label=f"ldxnq{sc}")
            g.dma("sy", lambda e, w0=w0: e.dma_start(
                out=csc[:, :], in_=cosg.ap()[:, w0:w0 + QKV_SC]),
                writes=[("csc",)], label=f"ldcos{sc}")
            g.dma("sy", lambda e, w0=w0: e.dma_start(
                out=css[:, :], in_=sing.ap()[:, w0:w0 + QKV_SC]),
                writes=[("css",)], label=f"ldsin{sc}")
            for fw in range(2):
                def mm(e, fw=fw):
                    last = None
                    for et in range(16):
                        last = e.matmul(
                            psA[fw][:, :],
                            wrot_sb[:, et * 128:(et + 1) * 128],
                            xnq[:, et * QKV_SC + fw * 512:
                                et * QKV_SC + fw * 512 + 512],
                            start=(et == 0), stop=(et == 15))
                    return last
                g.op("pe", mm, reads=[("xnq",), ("w", "w_rot")],
                     writes=[("psA", fw)], label=f"rmm{sc}{fw}")
                g.op("a", lambda e, fw=fw: e.activation(
                    rotg[:, fw * 512:fw * 512 + 512], psA[fw][:, :],
                    AF.Identity, bias=brot_sb[:, 0:1]),
                    reads=[("psA", fw), ("w", "b_rot")], writes=[("rotg",)],
                    label=f"rev{sc}{fw}")
            g.op("v", lambda e: e.tensor_tensor(
                rs1[:, :], rotg[0:64, :], csc[0:64, :], ALU.mult),
                reads=[("rotg",), ("csc",)], writes=[("rs1",)],
                label=f"rp1_{sc}")
            g.op("v", lambda e: e.tensor_tensor(
                rs2[:, :], rotg[64:128, :], css[64:128, :], ALU.mult),
                reads=[("rotg",), ("css",)], writes=[("rs2",)],
                label=f"rp2_{sc}")
            g.op("v", lambda e: e.tensor_tensor(
                r12b[0:64, :], rs1[:, :], rs2[:, :], ALU.subtract),
                reads=[("rs1",), ("rs2",)], writes=[("r12a",)],
                label=f"rp3_{sc}")
            g.op("v", lambda e: e.tensor_tensor(
                rs1[:, :], rotg[0:64, :], css[0:64, :], ALU.mult),
                reads=[("rotg",), ("css",)], writes=[("rs1",)],
                label=f"rp4_{sc}")
            g.op("v", lambda e: e.tensor_tensor(
                rs2[:, :], rotg[64:128, :], csc[64:128, :], ALU.mult),
                reads=[("rotg",), ("csc",)], writes=[("rs2",)],
                label=f"rp5_{sc}")
            g.op("v", lambda e: e.tensor_tensor(
                r12b[64:128, :], rs1[:, :], rs2[:, :], ALU.add),
                reads=[("rs1",), ("rs2",)], writes=[("r12b",)],
                label=f"rp6_{sc}")
            for u in range(4):
                for fw in range(2):
                    def pmm(e, u=u, fw=fw):
                        return e.matmul(
                            psM[0:32, :],
                            perm_sb[:, u * 32:(u + 1) * 32],
                            r12b[:, fw * 512:fw * 512 + 512],
                            start=True, stop=True)
                    g.op("pe", pmm,
                         reads=[("r12a",), ("r12b",), ("w", "perm")],
                         writes=[("psM",)], label=f"pmm{sc}{u}{fw}")

                    def pev(e, u=u, fw=fw, w0=w0):
                        return e.tensor_copy(
                            units[u][96:128,
                                     w0 + fw * 512:w0 + fw * 512 + 512],
                            psM[0:32, :])
                    g.op("v", pev, reads=[("psM",)], writes=[("un", u, sc)],
                         label=f"pev{sc}{u}{fw}")
            for u in range(4):
                for fw in range(2):
                    def mm(e, u=u, fw=fw):
                        last = None
                        for et in range(16):
                            last = e.matmul(
                                psA[2 + fw][0:96, :],
                                wpass_sb[:, et * 384 + u * 96:
                                         et * 384 + (u + 1) * 96],
                                xnq[:, et * QKV_SC + fw * 512:
                                    et * QKV_SC + fw * 512 + 512],
                                start=(et == 0), stop=(et == 15))
                        return last
                    g.op("pe", mm, reads=[("xnq",), ("w", "w_pass")],
                         writes=[("psA", 2 + fw)], label=f"qmm{sc}{u}{fw}")

                    def qev(e, u=u, fw=fw, w0=w0):
                        return e.activation(
                            units[u][0:96, w0 + fw * 512:w0 + fw * 512 + 512],
                            psA[2 + fw][0:96, :], AF.Identity,
                            bias=bpass_sb[:, u:u + 1],
                            scale=SCALE if u in (0, 2) else 1.0)
                    g.op("a", qev, reads=[("psA", 2 + fw), ("w", "b_pass")],
                         writes=[("un", u, sc)], label=f"qev{sc}{u}{fw}")
            for vh in range(2):
                for fw in range(2):
                    def mm(e, vh=vh, fw=fw):
                        last = None
                        for et in range(16):
                            last = e.matmul(
                                psB[fw][:, :],
                                wv_sb[:, et * 256 + vh * 128:
                                      et * 256 + (vh + 1) * 128],
                                xnq[:, et * QKV_SC + fw * 512:
                                    et * QKV_SC + fw * 512 + 512],
                                start=(et == 0), stop=(et == 15))
                        return last
                    g.op("pe", mm, reads=[("xnq",), ("w", "w_v")],
                         writes=[("psB", fw)], label=f"vmm{sc}{vh}{fw}")

                    def vev(e, vh=vh, fw=fw):
                        return e.activation(
                            vfm[:, fw * 512:fw * 512 + 512], psB[fw][:, :],
                            AF.Identity, bias=bv_sb[:, vh:vh + 1])
                    g.op("a", vev, reads=[("psB", fw), ("w", "b_v")],
                         writes=[("vfm",)], label=f"vev{sc}{vh}{fw}")

                def vtp(e):
                    last = None
                    for i in range(8):
                        last = e.transpose(
                            psT[:, i * 128:(i + 1) * 128],
                            vfm[:, i * 128:(i + 1) * 128], c_idb[:, :])
                    return last
                g.op("pe", vtp, reads=[("vfm",), ("const", "idb")],
                     writes=[("psT",)], label=f"vtp{sc}{vh}")

                def vte(e, vh=vh, sc=sc):
                    return e.tensor_copy(
                        vtmv[:, sc * 8:(sc + 1) * 8, vh, :],
                        psT[:, :].rearrange("p (i d) -> p i d", i=8))
                g.op("v", vte, reads=[("psT",)], writes=[("vt", vh, sc)],
                     label=f"vte{sc}{vh}")
        stP1.close()

        stP2 = contextlib.ExitStack()
        p2rs = [sb(f"p2r{i}", [128, 2 * T], BF16, st=stP2) for i in range(2)]
        aosg = [sb(f"aosg{i}", [128, 256], BF16, st=stP2)
                for i in range(2)]
        ptb = [sb(f"ptb{i}", [128, 1024], BF16, st=stP2) for i in range(2)]
        avb = sb("avb", [128, 256], st=stP2)
        nmx = sb("nmx", [128, 64], st=stP2)
        ssum = sb("ssum", [128, 64], st=stP2)
        fbuf = sb("fbuf", [128, 64], st=stP2)
        wtm = sb("wtm", [128, 64], st=stP2)
        nmn = sb("nmn", [128, 4], st=stP2)
        rsum = sb("rsum", [128, 4], st=stP2)
        rrec = sb("rrec", [128, 4], st=stP2)

        un_reads = [("un", u, sc) for u in range(4) for sc in range(NSC)]
        vt_reads = [("vt", vh, sc) for vh in range(2) for sc in range(NSC)]

        SBK = [("psB", 0), ("psB", 1), ("psA", 0), ("psA", 1)]
        SBT = [psB[0], psB[1], psA[0], psA[1]]

        def S_thunks(hh, g2, qt, pp):
            uq, uk = units[2 * hh], units[2 * hh + 1]
            j = 2 * g2 + qt
            kvlen = 128 * j + 128
            kvspan = 256 * g2 + 256
            nw = (kvlen + 511) // 512
            p2 = p2rs[pp]
            base = pp * 32 + qt * 16
            th = []
            if qt == 0 and kvlen < kvspan:
                def t_ms():
                    g.op("v", lambda e: e.memset(p2[:, kvlen:kvspan], 0),
                         reads=[], writes=[("p2", pp, 0)],
                         label=f"mst{hh}_{j}")
                th.append(t_ms)
            for w0_ in range(0, nw, 2):
                ws = [w_ for w_ in (w0_, w0_ + 1) if w_ < nw]

                def t_batch(ws=ws, j=j, kvlen=kvlen, nw=nw, base=base,
                            uq=uq, uk=uk, p2=p2, pp=pp, qt=qt, hh=hh):
                    bkeys = [SBK[w_ % 4] for w_ in ws]

                    def smm(e):
                        last = None
                        for w_ in ws:
                            fwid = min(512, kvlen - 512 * w_)
                            isl = w_ == nw - 1
                            last = e.matmul(
                                SBT[w_ % 4][:, :fwid],
                                uq[:, j * 128:(j + 1) * 128],
                                uk[:, w_ * 512:w_ * 512 + fwid],
                                start=True, stop=not isl)
                            if isl:
                                last = e.matmul(
                                    SBT[w_ % 4][:, fwid - 128:fwid],
                                    c_idb[:, :], c_trm[:, :],
                                    start=False, stop=True)
                        return last
                    g.op("pe", smm,
                         reads=un_reads + [("const", "idb"),
                                           ("const", "trm")],
                         writes=bkeys, label=f"smm{hh}_{j}_{ws[0]}")

                    def cmx(e):
                        last = None
                        for w_ in ws:
                            fwid = min(512, kvlen - 512 * w_)
                            last = e.tensor_reduce(
                                nmx[:, base + w_:base + w_ + 1],
                                SBT[w_ % 4][:, :fwid], AX.X, ALU.max,
                                negate=True)
                        return last
                    g.op("v", cmx, reads=bkeys,
                         writes=[("nmx", pp, qt, ws[0])],
                         label=f"cmx{hh}_{j}_{ws[0]}")

                    def exc(e):
                        last = None
                        for w_ in ws:
                            fwid = min(512, kvlen - 512 * w_)
                            last = e.activation(
                                p2[:, qt * T + w_ * 512:
                                   qt * T + w_ * 512 + fwid],
                                SBT[w_ % 4][:, :fwid], AF.Exp,
                                bias=nmx[:, base + w_:base + w_ + 1],
                                scale=1.0,
                                accum_out=ssum[:, base + w_:base + w_ + 1])
                        return last
                    g.op("a", exc, reads=bkeys + [("nmx", pp, qt, ws[0])],
                         writes=[("p2", pp, qt), ("ssum", pp, qt, ws[0])],
                         label=f"exc{hh}_{j}_{ws[0]}")
                th.append(t_batch)

            fins = []

            def t_fin(j=j, nw=nw, base=base, pp=pp, qt=qt, hh=hh):
                rds = [("nmx", pp, qt, w0_) for w0_ in range(0, nw, 2)]
                srds = [("ssum", pp, qt, w0_) for w0_ in range(0, nw, 2)]
                g.op("v", lambda e: e.tensor_reduce(
                    nmn[:, 2 * pp + qt:2 * pp + qt + 1],
                    nmx[:, base:base + nw], AX.X, ALU.min),
                    reads=rds, writes=[("nmn", pp, qt)], label=f"nmn{hh}_{j}")
                g.op("a", lambda e: e.activation(
                    fbuf[:, base:base + nw], nmx[:, base:base + nw],
                    AF.Exp, bias=nmn[:, 2 * pp + qt:2 * pp + qt + 1],
                    scale=-1.0),
                    reads=rds + [("nmn", pp, qt)], writes=[("fct", pp, qt)],
                    label=f"fct{hh}_{j}")
                g.op("v", lambda e: e.tensor_tensor(
                    wtm[:, base:base + nw], fbuf[:, base:base + nw],
                    ssum[:, base:base + nw], ALU.mult),
                    reads=srds + [("fct", pp, qt)], writes=[("wtm", pp, qt)],
                    label=f"wtm{hh}_{j}")
                g.op("v", lambda e: e.tensor_reduce(
                    rsum[:, 2 * pp + qt:2 * pp + qt + 1],
                    wtm[:, base:base + nw], AX.X, ALU.add),
                    reads=[("wtm", pp, qt)], writes=[("rsm", pp, qt)],
                    label=f"wsm{hh}_{j}")
            fins.append(t_fin)
            for w0_ in range(0, nw, 2):
                ws = [w_ for w_ in (w0_, w0_ + 1) if w_ < nw]

                def t_rsc(ws=ws, base=base, pp=pp, qt=qt, hh=hh, j=j,
                          p2=p2, kvlen=kvlen):
                    def rsc(e, ws=ws):
                        last = None
                        for w_ in ws:
                            fwid = min(512, kvlen - 512 * w_)
                            sl = p2[:, qt * T + w_ * 512:
                                    qt * T + w_ * 512 + fwid]
                            last = e.tensor_scalar(
                                sl, sl, fbuf[:, base + w_:base + w_ + 1],
                                None, ALU.mult)
                        return last
                    g.op("v", rsc, reads=[("fct", pp, qt), ("p2", pp, qt)],
                         writes=[("p2", pp, qt)],
                         label=f"rsc{hh}_{j}_{ws[0]}")
                fins.append(t_rsc)
            return th, fins

        def B_thunks(hh, g2, pp):
            kvspan = 256 * g2 + 256
            nkc = kvspan // 128
            p2 = p2rs[pp]
            th = []

            def t_rec():
                g.op("v", lambda e: e.reciprocal(
                    rrec[:, 2 * pp:2 * pp + 2], rsum[:, 2 * pp:2 * pp + 2]),
                    reads=[("rsm", pp, 0), ("rsm", pp, 1)],
                    writes=[("rrec", pp)], label=f"rec{hh}_{g2}")
            th.append(t_rec)
            batches = [list(range(kb, min(kb + 2, nkc)))
                       for kb in range(0, nkc, 2)]
            nb = len(batches)

            psA2b = psA[2][:, :].bitcast(BF16)

            def mk_ptp(bi):
                kcs = batches[bi]
                ch = bi % 2
                tgt = psT if ch == 0 else psA2b
                tkey = ("psT",) if ch == 0 else ("psA", 2)

                def ptp(e, kcs=kcs, tgt=tgt):
                    last = None
                    for ii, kc in enumerate(kcs):
                        for qt in range(2):
                            last = e.transpose(
                                tgt[:, ii * 256 + qt * 128:
                                    ii * 256 + (qt + 1) * 128],
                                p2[:, qt * T + kc * 128:
                                   qt * T + kc * 128 + 128],
                                c_idb[:, :])
                    return last
                g.op("pe", ptp,
                     reads=[("p2", pp, 0), ("p2", pp, 1), ("const", "idb")],
                     writes=[tkey], label=f"ptp{hh}_{g2}_{bi}")

                peng = "v" if ch == 0 else "a"

                def pte(e, kcs=kcs, tgt=tgt, ch=ch, peng=peng):
                    if peng == "v":
                        return e.tensor_copy(
                            ptb[ch][:, :len(kcs) * 256],
                            tgt[:, :len(kcs) * 256])
                    return e.activation(
                        ptb[ch][:, :len(kcs) * 256],
                        tgt[:, :len(kcs) * 256], AF.Copy)
                g.op(peng, pte, reads=[tkey], writes=[("ptb", ch)],
                     label=f"pte{hh}_{g2}_{bi}")

            def mk_avm(bi):
                kcs = batches[bi]
                ch = bi % 2

                def avm(e, kcs=kcs, ch=ch, hh=hh, bi=bi, nb=nb):
                    last = None
                    for ii, kc in enumerate(kcs):
                        last = e.matmul(
                            psM[:, :256], vtmv[:, kc, hh, :],
                            ptb[ch][:, ii * 256:(ii + 1) * 256],
                            start=(bi == 0 and ii == 0),
                            stop=(bi == nb - 1 and ii == len(kcs) - 1))
                    return last
                g.op("pe", avm, reads=vt_reads + [("ptb", ch), ("psM",)],
                     writes=[("psM",)], label=f"avm{hh}_{g2}_{bi}")

            th.append(lambda: mk_ptp(0))
            for bi in range(nb):
                def t_step(bi=bi):
                    if bi + 1 < nb:
                        mk_ptp(bi + 1)
                    mk_avm(bi)
                th.append(t_step)

            def t_tail():
                g.op("v", lambda e: e.tensor_copy(avb[:, :], psM[:, :256]),
                     reads=[("psM",)], writes=[("avb",), ("psM",)],
                     label=f"ave{hh}_{g2}")
                for qt in range(2):
                    j = 2 * g2 + qt

                    def avt(e, qt=qt):
                        return e.transpose(
                            psM[:, :128], avb[:, qt * 128:(qt + 1) * 128],
                            c_idf[:, :])
                    g.op("pe", avt, reads=[("avb",), ("const", "idf")],
                         writes=[("psM",)], label=f"avt{hh}_{g2}_{qt}")

                    def aoe(e, qt=qt):
                        return e.activation(
                            aosg[pp][:, qt * 128:(qt + 1) * 128],
                            psM[:, :128], AF.Identity,
                            scale=rrec[:, 2 * pp + qt:2 * pp + qt + 1])
                    g.op("a", aoe, reads=[("psM",), ("rrec", pp)],
                         writes=[("aosg", pp, qt)], label=f"aoe{hh}_{g2}_{qt}")
                g.dma("sy", lambda e: e.dma_start(
                    out=ao_c.ap().rearrange("(jt p) hd -> p jt hd", p=128)
                    [:, 2 * g2:2 * g2 + 2, hh * 128:(hh + 1) * 128],
                    in_=aosg[pp][:, :].rearrange("p (jt d) -> p jt d", jt=2)),
                    reads=[("aosg", pp, 0), ("aosg", pp, 1)],
                    writes=[("dr", "aoc", hh, g2)], label=f"aost{hh}_{g2}")
            th.append(t_tail)
            return th

        def weave(main, extra):
            out = []
            nS, nB = len(main), len(extra)
            bi = 0
            for k, t in enumerate(main):
                out.append(t)
                want = ((k + 1) * nB) // max(nS, 1)
                while bi < want:
                    out.append(extra[bi])
                    bi += 1
            out.extend(extra[bi:])
            return out

        units_list = [(hh, g2) for hh in range(2) for g2 in range(32)]
        pending = []
        for i, (hh, g2) in enumerate(units_list):
            pp = i % 2
            b0, f0 = S_thunks(hh, g2, 0, pp)
            b1, f1 = S_thunks(hh, g2, 1, pp)
            stream = b0 + weave(b1, f0)
            for t in weave(stream, pending):
                t()
            pending = f1 + B_thunks(hh, g2, pp)
        for t in pending:
            t()
        g.cc(lambda e: e.collective_compute(
            "AllGather", ALU.bypass, replica_groups=[CORE_IDS],
            ins=[ao_c.ap()], outs=[ao_full.ap()]),
            reads=[("dr", "aoc", hh, g2)
                   for hh in range(2) for g2 in range(32)],
            writes=[("dr", "ao_full")], label="agao")
        if debug:
            g.dma("sy", lambda e: e.dma_start(
                out=dmp["d_ao"].ap(), in_=ao_full.ap()[0:T, :]),
                reads=[("dr", "ao_full")], label="dump_ao")
        stP2.close()
        stA.close()
        g.fence()

        stx2 = contextlib.ExitStack()
        x2buf = sb("x2buf", [128, NTT * D], st=stx2)
        stl = contextlib.ExitStack()
        aoin = sb("aoin", [128, NTT * 8 * 256], BF16, st=stl)
        xin2 = sb("xin2", [128, NTT * D], st=stl)
        g.dma("sy", lambda e: e.dma_start(
            out=xin2[:, :].rearrange("p (t d) -> p t d", t=NTT),
            in_=x_rows.ap().rearrange("(t p) d -> p t d", p=128)),
            writes=[("xin2", tt) for tt in range(NTT)], label="ldx2")

        def ld_ao(e):
            dsem = g.cur_dma_sem
            with e.register("r_ao") as r0:
                e.reg_load(r0, c_cid[0:1, 0:1])
                cid = e.snap(r0, min_val=0, max_val=7)
                base = cid * TPC
                last = None
                for r in range(NCORES):
                    last = e.dma_start(
                        out=aoin[:, :].rearrange(
                            "p (t r hd) -> p t r hd", t=NTT, r=8)[:, :, r, :],
                        in_=ao_full.ap()[
                            bass.ds(base + r * T, TPC), :].rearrange(
                            "(t p) hd -> p t hd", p=128))
                    if r < NCORES - 1:
                        last.then_inc(dsem, 16)
                return last
        g.dmam("g", ld_ao, NCORES,
               reads=[("dr", "ao_full"), ("const", "cid")],
               writes=[("aoin", tt) for tt in range(NTT)], label="ldao")
        for tt in range(NTT):
            def xadd(e, tt=tt):
                return e.tensor_tensor(
                    x2buf[:, tt * D:(tt + 1) * D],
                    xin2[:, tt * D:(tt + 1) * D],
                    aoin[:, tt * 2048:(tt + 1) * 2048], ALU.add)
            g.op("v", xadd, reads=[("xin2", tt), ("aoin", tt)],
                 writes=[("row", "x2", tt)], label=f"x2_{tt}")
        if debug:
            g.dma("sy", lambda e: e.dma_start(
                out=dmp["d_x2"].ap().rearrange("(t p) d -> p t d", p=128),
                in_=x2buf[:, :].rearrange("p (t d) -> p t d", t=NTT)),
                reads=[("row", "x2", tt) for tt in range(NTT)],
                label="dump_x2")
        stl.close()
        g.fence()

        def build_ffn(srcbuf, rtag_in, rtag_out, store_out):
            stf = contextlib.ExitStack()
            tg = rtag_out
            wb_sb = sb(f"wb_sb{tg}", [128, 16 * D], BF16, st=stf)
            rt_sb = sb(f"rt_sb{tg}", [128, 16 * 8], BF16, st=stf)
            at_sb = sb(f"at_sb{tg}", [128, 16 * 128], BF16, st=stf)
            bw_sb = sb(f"bw_sb{tg}", [128, D], BF16, st=stf)
            bb_sb = sb(f"bb_sb{tg}", [128, D], st=stf)
            xn2 = sb(f"xn2{tg}", [128, D], BF16, st=stf)
            xn2T = sb(f"xn2T{tg}", [128, 16 * 128], BF16, st=stf)
            lgt = sb(f"lgt{tg}", [128, 8], st=stf)
            lmx = sb(f"lmx{tg}", [128, 1], st=stf)
            msk = sb(f"msk{tg}", [128, 8], st=stf)
            hmb = sb(f"hmb{tg}", [128, 128], BF16, st=stf)
            hmT = sb(f"hmT{tg}", [128, 128], BF16, st=stf)
            fsc = sb(f"fsc{tg}", [128, D], st=stf)
            fob = sb(f"fob{tg}", [128, D], st=stf)

            for t_, s_, et_ in ((wbT, wb_sb, 16), (routerT, rt_sb, 16),
                                (aT, at_sb, 16)):
                g.dma("g", lambda e, t=t_, s=s_, et=et_: e.dma_start(
                    out=s[:, :].rearrange("p (et o) -> p et o", et=et),
                    in_=t.ap().rearrange("(et p) o -> p et o", p=128)),
                    writes=[("fw", t_.name, tg)], label=f"ldf_{t_.name}{tg}")
            g.dma("g", lambda e: e.dma_start(out=bw_sb[:, :], in_=bwT.ap()),
                  writes=[("fw", "bwT", tg)], label=f"ldf_bw{tg}")
            g.dma("sy", lambda e: e.dma_start(out=bb_sb[:, :],
                                              in_=bbase_rep.ap()),
                  writes=[("fw", "bb", tg)], label=f"ldf_bb{tg}")

            for tt in range(NTT):
                xap = srcbuf[:, tt * D:(tt + 1) * D]
                emit_ln(xap, ("row", rtag_in, tt), xn2[:, :], ("fxn", tg),
                        f"{tg}_{tt}")
                for half in range(2):
                    def tp(e, half=half):
                        last = None
                        for i in range(8):
                            et = half * 8 + i
                            last = e.transpose(
                                psT[:, i * 128:(i + 1) * 128],
                                xn2[:, et * 128:(et + 1) * 128], c_idb[:, :])
                        return last
                    g.op("pe", tp, reads=[("fxn", tg), ("const", "idb")],
                         writes=[("psT",)], label=f"ftp{tg}{tt}{half}")

                    def ev(e, half=half):
                        return e.tensor_copy(
                            xn2T[:, :].rearrange("p (et t) -> p et t", et=16)
                            [:, half * 8:(half + 1) * 8, :],
                            psT[:, :].rearrange("p (i t) -> p i t", i=8))
                    g.op("v", ev, reads=[("psT",)], writes=[("fxT", tg)],
                         label=f"fev{tg}{tt}{half}")

                def bigmm(e):
                    last = None
                    for et in range(16):
                        lhs = xn2T[:, et * 128:(et + 1) * 128]
                        for oc in range(4):
                            last = e.matmul(
                                psA[oc][:, :], lhs,
                                wb_sb[:, et * D + oc * 512:
                                      et * D + oc * 512 + 512],
                                start=(et == 0), stop=False)
                        last = e.matmul(
                            psB[0][:, :8], lhs,
                            rt_sb[:, et * 8:(et + 1) * 8],
                            start=(et == 0), stop=(et == 15))
                        last = e.matmul(
                            psB[1][:, :128], lhs,
                            at_sb[:, et * 128:(et + 1) * 128],
                            start=(et == 0), stop=(et == 15))
                    return last
                g.op("pe", bigmm,
                     reads=[("fxT", tg), ("fw", "wbT", tg),
                            ("fw", "routerT", tg), ("fw", "aT", tg)],
                     writes=[("psA", 0), ("psA", 1), ("psA", 2), ("psA", 3),
                             ("psB", 0), ("psB", 1)],
                     label=f"fmm{tg}{tt}")
                g.op("v", lambda e: e.tensor_copy(lgt[:, :], psB[0][:, :8]),
                     reads=[("psB", 0)], writes=[("flg", tg)],
                     label=f"flg{tg}{tt}")
                g.op("v", lambda e: e.tensor_reduce(
                    lmx[:, :], lgt[:, :], AX.X, ALU.max),
                    reads=[("flg", tg)], writes=[("fmx", tg)],
                    label=f"fmx{tg}{tt}")
                g.op("v", lambda e: e.tensor_tensor(
                    msk[:, :], lgt[:, :], lmx[:, 0:1].broadcast_to([128, 8]),
                    ALU.is_equal),
                    reads=[("flg", tg), ("fmx", tg)], writes=[("fms", tg)],
                    label=f"feq{tg}{tt}")
                g.op("v", lambda e: e.tensor_scalar(
                    msk[:, :], msk[:, :], 2.0, None, ALU.mult),
                    reads=[("fms", tg)], writes=[("fms", tg)],
                    label=f"fm2{tg}{tt}")
                def fhm(e):
                    return e.tensor_tensor(
                        hmb[:, :].rearrange("p (ee r) -> p ee r", ee=8),
                        psB[1][:, :128].rearrange("p (ee r) -> p ee r", ee=8),
                        msk[:, :].rearrange("p (ee o) -> p ee o", o=1)
                        .broadcast_to([128, 8, 16]),
                        ALU.mult)
                g.op("v", fhm, reads=[("psB", 1), ("fms", tg)],
                     writes=[("fhm", tg)], label=f"fhm{tg}{tt}")
                g.op("pe", lambda e: e.transpose(
                    psT[:, :128], hmb[:, :], c_idb[:, :]),
                    reads=[("fhm", tg), ("const", "idb")], writes=[("psT",)],
                    label=f"fht{tg}{tt}")
                g.op("v", lambda e: e.tensor_copy(hmT[:, :], psT[:, :128]),
                     reads=[("psT",)], writes=[("fhT", tg)],
                     label=f"fhe{tg}{tt}")

                def dmm(e):
                    last = None
                    for oc in range(4):
                        last = e.matmul(
                            psA[oc][:, :], hmT[:, :],
                            bw_sb[:, oc * 512:oc * 512 + 512],
                            start=False, stop=True)
                    return last
                g.op("pe", dmm, reads=[("fhT", tg), ("fw", "bwT", tg),
                                       ("psA", 0), ("psA", 1), ("psA", 2),
                                       ("psA", 3)],
                     writes=[("psA", 0), ("psA", 1), ("psA", 2), ("psA", 3)],
                     label=f"fdm{tg}{tt}")
                def e1(e):
                    last = None
                    for oc in range(4):
                        last = e.tensor_tensor(
                            fsc[:, oc * 512:(oc + 1) * 512], psA[oc][:, :],
                            bb_sb[:, oc * 512:oc * 512 + 512], ALU.add)
                    return last
                g.op("v", e1,
                     reads=[("psA", 0), ("psA", 1), ("psA", 2), ("psA", 3),
                            ("fw", "bb", tg)],
                     writes=[("fsc", tg)], label=f"fe1{tg}{tt}")
                if store_out is None:
                    def e2(e, tt=tt):
                        return e.tensor_tensor(
                            srcbuf[:, tt * D:(tt + 1) * D], fsc[:, :],
                            srcbuf[:, tt * D:(tt + 1) * D], ALU.add)
                    g.op("v", e2,
                         reads=[("fsc", tg), ("row", rtag_in, tt)],
                         writes=[("row", rtag_out, tt),
                                 ("row", rtag_in, tt)],
                         label=f"fe2{tg}{tt}")
                else:
                    def e2(e, tt=tt):
                        return e.tensor_tensor(
                            fob[:, :], fsc[:, :],
                            srcbuf[:, tt * D:(tt + 1) * D], ALU.add)
                    g.op("v", e2,
                         reads=[("fsc", tg), ("row", rtag_in, tt)],
                         writes=[("fob", tg)], label=f"fe2{tg}{tt}")
                    g.dma("sy", lambda e, tt=tt: e.dma_start(
                        out=store_out.ap()[tt * 128:(tt + 1) * 128, :],
                        in_=fob[:, :]),
                        reads=[("fob", tg)],
                        writes=[("outst", tt)], label=f"fst{tg}{tt}")
            stf.close()

        build_ffn(x2buf, "x2", "x3", None)
        g.fence()

        st4 = contextlib.ExitStack()
        xnsc4 = sb("xnsc4", [128, D], BF16, st=st4)
        xnts4 = sb("xnts4", [128, NTT * D], BF16, st=st4)
        emit_xnT_phase(x2buf, "x3", "x3", xnsc4, xnts4)
        g.dma("sy", lambda e: e.dma_start(
            out=x3_d.ap().rearrange("(t p) d -> p t d", p=128),
            in_=x2buf[:, :].rearrange("p (t d) -> p t d", t=NTT)),
            reads=[("row", "x3", tt) for tt in range(NTT)],
            writes=[("dr", "x3_d")], label="spx3")
        if debug:
            g.dma("sy", lambda e: e.dma_start(
                out=dmp["d_x3"].ap(), in_=x3_d.ap()),
                reads=[("dr", "x3_d")], label="dump_x3")
        st4.close()
        stx2.close()
        g.fence()

        st5 = contextlib.ExitStack()
        inw_sb = sb("inw_sb", [128, 16 * 1024], BF16, st=st5)
        inb_sb = sb("inb_sb", [128, 8], st=st5)
        cw_sb = sb("cw_sb", [128, 16], st=st5)
        cb_sb = sb("cb_sb", [128, 4], st=st5)
        xpj_sb = sb("xpj_sb", [128, 4 * 160], BF16, st=st5)
        xnq5 = sb("xnq5", [128, 16 * IP_SC], BF16, st=st5)
        hbuf = sb("hbuf", [128, 4 * (IP_SC + 3)], st=st5)
        hbv = hbuf[:, :].rearrange("p (db t) -> p db t", db=4)
        halo = sb("halo", [128, 12], st=st5)
        acc1 = sb("acc1", [128, IP_SC], st=st5)
        acc2 = sb("acc2", [128, IP_SC], st=st5)
        hsf = sb("hsf", [128, IP_SC], st=st5)
        hsb = sb("hsb", [128, 4 * IP_SC], BF16, st=st5)
        sgb = sb("sgb", [128, 4 * IP_SC], BF16, st=st5)
        tsacc = sb("tsacc", [128, T], st=st5)
        bcacc = sb("bcacc", [32, T], st=st5)

        g.dma("g", lambda e: e.dma_start(
            out=inw_sb[:, :].rearrange("p (et o) -> p et o", et=16),
            in_=inwT_sl.ap().rearrange("(et p) o -> p et o", p=128)),
            writes=[("w5", "inw")], label="ld_inw")
        g.dma("sy", lambda e: e.dma_start(
            out=inb_sb[:, :], in_=inb_sl.ap()),
            writes=[("w5", "inb")], label="ld_inb")
        g.dma("sy", lambda e: e.dma_start(
            out=cw_sb[:, :], in_=convw_sl.ap()),
            writes=[("w5", "cw")], label="ld_cw")
        g.dma("sy", lambda e: e.dma_start(
            out=cb_sb[:, :], in_=convb_sl.ap()),
            writes=[("w5", "cb")], label="ld_cb")
        g.dma("g", lambda e: e.dma_start(
            out=xpj_sb[:, :].rearrange("p (db o) -> p db o", db=4),
            in_=xprojT_sl.ap().rearrange("(db p) o -> p db o", p=128)),
            writes=[("w5", "xpj")], label="ld_xpj")
        g.op("v", lambda e: e.memset(halo[:, :], 0), writes=[("halo",)],
             label="halo0")

        for sc in range(NSC):
            w0 = sc * IP_SC
            g.dma("sy", lambda e, sc=sc: e.dma_start(
                out=xnq5[:, :].rearrange("p (et t) -> p et t", et=16),
                in_=xn_full.ap().rearrange("(r et p) t -> r p et t",
                                           p=128, et=16)[sc]),
                reads=[("dr", "xn_full")], writes=[("xnq5",)],
                label=f"ld5x{sc}")
            for db in range(4):
                def hcp(e, db=db):
                    return e.tensor_copy(hbv[:, db, 0:3],
                                         halo[:, db * 3:(db + 1) * 3])
                g.op("v", hcp, reads=[("halo",)], writes=[("hb", db)],
                     label=f"hcp{sc}{db}")
            for dt in range(8):
                for fw in range(2):
                    bank = 2 * (dt % 2) + fw
                    def mm(e, dt=dt, fw=fw, bank=bank):
                        last = None
                        for et in range(16):
                            last = e.matmul(
                                psA[bank][:, :],
                                inw_sb[:, et * 1024 + dt * 128:
                                       et * 1024 + (dt + 1) * 128],
                                xnq5[:, et * IP_SC + fw * 512:
                                     et * IP_SC + fw * 512 + 512],
                                start=(et == 0), stop=(et == 15))
                        return last
                    g.op("pe", mm, reads=[("xnq5",), ("w5", "inw")],
                         writes=[("psA", bank)], label=f"imm{sc}{dt}{fw}")
                    if dt < 4:
                        def hev(e, dt=dt, fw=fw, bank=bank):
                            return e.activation(
                                hbv[:, dt, 3 + fw * 512:3 + fw * 512 + 512],
                                psA[bank][:, :], AF.Identity,
                                bias=inb_sb[:, dt:dt + 1])
                        g.op("a", hev, reads=[("psA", bank), ("w5", "inb")],
                             writes=[("hb", dt)], label=f"hev{sc}{dt}{fw}")
                    else:
                        def gev(e, dt=dt, fw=fw, bank=bank):
                            return e.activation(
                                sgb[:, :].rearrange("p (db t) -> p db t",
                                                    db=4)
                                [:, dt - 4, fw * 512:fw * 512 + 512],
                                psA[bank][:, :], AF.Silu,
                                bias=inb_sb[:, dt:dt + 1])
                        g.op("a", gev, reads=[("psA", bank), ("w5", "inb")],
                             writes=[("sg", dt - 4)], label=f"gev{sc}{dt}{fw}")
            for db in range(4):
                g.op("v", lambda e, db=db: e.tensor_scalar(
                    acc1[:, :], hbv[:, db, 0:IP_SC],
                    cw_sb[:, db * 4:db * 4 + 1], None, ALU.mult),
                    reads=[("hb", db), ("w5", "cw")], writes=[("ac1",)],
                    label=f"cv0{sc}{db}")
                g.op("v", lambda e, db=db: e.scalar_tensor_tensor(
                    acc2[:, :], hbv[:, db, 1:1 + IP_SC],
                    cw_sb[:, db * 4 + 1:db * 4 + 2], acc1[:, :],
                    ALU.mult, ALU.add),
                    reads=[("hb", db), ("ac1",), ("w5", "cw")],
                    writes=[("ac2",)], label=f"cv1{sc}{db}")
                g.op("v", lambda e, db=db: e.scalar_tensor_tensor(
                    acc1[:, :], hbv[:, db, 2:2 + IP_SC],
                    cw_sb[:, db * 4 + 2:db * 4 + 3], acc2[:, :],
                    ALU.mult, ALU.add),
                    reads=[("hb", db), ("ac2",), ("w5", "cw")],
                    writes=[("ac1",)], label=f"cv2{sc}{db}")
                g.op("v", lambda e, db=db: e.scalar_tensor_tensor(
                    acc2[:, :], hbv[:, db, 3:3 + IP_SC],
                    cw_sb[:, db * 4 + 3:db * 4 + 4], acc1[:, :],
                    ALU.mult, ALU.add),
                    reads=[("hb", db), ("ac1",), ("w5", "cw")],
                    writes=[("ac2",)], label=f"cv3{sc}{db}")
                def hsv(e, db=db):
                    return e.tensor_copy(halo[:, db * 3:(db + 1) * 3],
                                         hbv[:, db, IP_SC:IP_SC + 3])
                g.op("v", hsv, reads=[("hb", db)], writes=[("halo",)],
                     label=f"hsv{sc}{db}")
                g.op("a", lambda e, db=db: e.activation(
                    hsf[:, :], acc2[:, :], AF.Silu,
                    bias=cb_sb[:, db:db + 1]),
                    reads=[("ac2",), ("w5", "cb")], writes=[("hsf",)],
                    label=f"cvs{sc}{db}")
                def hcast(e, db=db):
                    return e.tensor_copy(
                        hsb[:, db * IP_SC:(db + 1) * IP_SC], hsf[:, :])
                g.op("v", hcast, reads=[("hsf",)], writes=[("hsb", db)],
                     label=f"hct{sc}{db}")
            g.dma("sy", lambda e, w0=w0: e.dma_start(
                out=hsb_d.ap().rearrange("(db p) t -> p db t", p=128)
                [:, :, w0:w0 + IP_SC],
                in_=hsb[:, :].rearrange("p (db t) -> p db t", db=4)),
                reads=[("hsb", db) for db in range(4)],
                writes=[("dr", "hsb", sc)], label=f"sth{sc}")
            g.dma("sy", lambda e, w0=w0: e.dma_start(
                out=sgb_d.ap().rearrange("(db p) t -> p db t", p=128)
                [:, :, w0:w0 + IP_SC],
                in_=sgb[:, :].rearrange("p (db t) -> p db t", db=4)),
                reads=[("sg", db) for db in range(4)],
                writes=[("dr", "sgb", sc)], label=f"stg{sc}")
            for fw in range(2):
                def xmm(e, fw=fw):
                    last = None
                    for db in range(4):
                        last = e.matmul(
                            psB[fw][:, :],
                            xpj_sb[:, db * 160:db * 160 + 128],
                            hsb[:, db * IP_SC + fw * 512:
                                db * IP_SC + fw * 512 + 512],
                            start=(db == 0), stop=(db == 3))
                    return last
                g.op("pe", xmm,
                     reads=[("hsb", db) for db in range(4)] + [("w5", "xpj")],
                     writes=[("psB", fw)], label=f"xts{sc}{fw}")

                def xev(e, fw=fw, w0=w0):
                    return e.tensor_copy(
                        tsacc[:, w0 + fw * 512:w0 + fw * 512 + 512],
                        psB[fw][:, :])
                g.op("v", xev, reads=[("psB", fw)], writes=[("tsa", sc, fw)],
                     label=f"xte{sc}{fw}")

                def bmm(e, fw=fw):
                    last = None
                    for db in range(4):
                        last = e.matmul(
                            psA[fw][0:32, :],
                            xpj_sb[:, db * 160 + 128:db * 160 + 160],
                            hsb[:, db * IP_SC + fw * 512:
                                db * IP_SC + fw * 512 + 512],
                            start=(db == 0), stop=(db == 3))
                    return last
                g.op("pe", bmm,
                     reads=[("hsb", db) for db in range(4)] + [("w5", "xpj")],
                     writes=[("psA", fw)], label=f"xbc{sc}{fw}")

                def bev(e, fw=fw, w0=w0):
                    return e.tensor_copy(
                        bcacc[:, w0 + fw * 512:w0 + fw * 512 + 512],
                        psA[fw][0:32, :])
                g.op("v", bev, reads=[("psA", fw)], writes=[("bca", sc, fw)],
                     label=f"xbe{sc}{fw}")
        g.dma("sy", lambda e: e.dma_start(
            out=sp_c.ap()[0:128, :], in_=tsacc[:, :]),
            reads=[("tsa", sc, fw) for sc in range(NSC) for fw in range(2)],
            writes=[("dr", "spc1")], label="stts")
        g.dma("sy", lambda e: e.dma_start(
            out=sp_c.ap()[128:160, :], in_=bcacc[:, :]),
            reads=[("bca", sc, fw) for sc in range(NSC) for fw in range(2)],
            writes=[("dr", "spc2")], label="stbc")
        g.cc(lambda e: e.collective_compute(
            "AllReduce", ALU.add, replica_groups=[CORE_IDS],
            ins=[sp_c.ap()], outs=[sp_full.ap()]),
            reads=[("dr", "spc1"), ("dr", "spc2")],
            writes=[("dr", "sp_full")], label="arsp")
        if debug:
            g.dma("sy", lambda e: e.dma_start(
                out=dmp["d_sp"].ap(), in_=sp_full.ap()),
                reads=[("dr", "sp_full")], label="dump_sp")
            g.dma("sy", lambda e: e.dma_start(
                out=dmp["d_hs"].ap(), in_=hsb_d.ap()),
                reads=[("dr", "hsb", sc) for sc in range(NSC)],
                label="dump_hs")
            g.dma("sy", lambda e: e.dma_start(
                out=dmp["d_sg"].ap(), in_=sgb_d.ap()),
                reads=[("dr", "sgb", sc) for sc in range(NSC)],
                label="dump_sg")
        st5.close()
        g.fence()

        st6 = contextlib.ExitStack()
        dtw_sb = sb("dtw_sb", [128, DIPC], BF16, st=st6)
        dtb_sb = sb("dtb_sb", [128, 4], st=st6)
        al_sb = sb("al_sb", [128, 4 * NST], st=st6)
        a_sb = sb("a_sb", [128, 4 * NST], st=st6)
        dd_sb = sb("dd_sb", [128, 4], st=st6)
        ow_sb = sb("ow_sb", [128, 4 * D], BF16, st=st6)
        tsch = sb("tsch", [128, 2 * SCAN_TC], BF16, st=st6)
        bcrep = sb("bcrep", [128, 2 * 2 * NST * SCAN_TC], BF16, st=st6)
        dtch = sb("dtch", [128, 4 * SCAN_TC], st=st6)
        dtu = sb("dtu", [128, 2 * SCAN_TC], st=st6)
        spa = sb("spa", [128, 2 * SCAN_TC], st=st6)
        spb = sb("spb", [128, 2 * SCAN_TC], st=st6)
        ach = sb("ach", [128, 2 * NST * SCAN_TC], st=st6)
        bch = sb("bch", [128, 2 * NST * SCAN_TC], BF16, st=st6)
        sch = sb("sch", [128, 2 * NST * SCAN_TC], st=st6)
        cmch = sb("cmch", [128, NST * SCAN_TC], st=st6)
        state = sb("state", [128, 4 * NST], st=st6)
        hsch = sb("hsch", [128, 2 * 4 * SCAN_TC], BF16, st=st6)
        sgch = sb("sgch", [128, 2 * 4 * SCAN_TC], BF16, st=st6)
        ybuf = sb("ybuf", [128, 2 * SCAN_TC], st=st6)
        zsc = sb("zsc", [128, SCAN_TC], st=st6)
        zsc2 = sb("zsc2", [128, SCAN_TC], st=st6)
        zb = sb("zb", [128, 2 * 4 * SCAN_TC], BF16, st=st6)
        obuf = sb("obuf", [128, D], st=st6)

        g.dma("g", lambda e: e.dma_start(out=dtw_sb[:, :],
                                         in_=dtwT_sl.ap()),
              writes=[("w6", "dtw")], label="ld_dtw")
        g.dma("sy", lambda e: e.dma_start(
            out=dtb_sb[:, :], in_=dtb_sl.ap()),
            writes=[("w6", "dtb")], label="ld_dtb")
        g.dma("sy", lambda e: e.dma_start(
            out=al_sb[:, :], in_=alog_sl.ap()),
            writes=[("w6", "al")], label="ld_al")
        g.dma("sy", lambda e: e.dma_start(
            out=dd_sb[:, :], in_=d_sl.ap()),
            writes=[("w6", "dd")], label="ld_dd")
        g.dma("g", lambda e: e.dma_start(
            out=ow_sb[:, :].rearrange("p (db o) -> p db o", db=4),
            in_=outwT_sl.ap().rearrange("(db p) o -> p db o", p=128)),
            writes=[("w6", "ow")], label="ld_ow")
        g.op("a", lambda e: e.activation(a_sb[:, :], al_sb[:, :], AF.Exp),
             reads=[("w6", "al")], writes=[("w6", "ae")], label="aexp")
        g.op("v", lambda e: e.tensor_scalar(
            a_sb[:, :], a_sb[:, :], -1.0, None, ALU.mult),
            reads=[("w6", "ae")], writes=[("w6", "a")], label="aneg")
        g.op("v", lambda e: e.memset(state[:, :], 0), writes=[("st", 0),
             ("st", 1), ("st", 2), ("st", 3)], label="st0")

        def emit_ldts(tc):
            w0 = tc * SCAN_TC
            pp_tc = tc % 2
            TSO = pp_tc * SCAN_TC
            g.dma("g", lambda e, w0=w0, TSO=TSO: e.dma_start(
                out=tsch[:, TSO:TSO + SCAN_TC],
                in_=sp_full.ap()[0:128, w0:w0 + SCAN_TC]),
                reads=[("dr", "sp_full")], writes=[("tsch", pp_tc)],
                label=f"ldts{tc}")

        def emit_ldbig(tc):
            w0 = tc * SCAN_TC
            pp_tc = tc % 2
            HSO = pp_tc * 4 * SCAN_TC
            RPO = pp_tc * NST * SCAN_TC

            def ldbc(e, w0=w0, RPO=RPO):
                return e.dma_start(
                    out=bcrep[:, 2 * RPO:2 * RPO + 2 * NST * SCAN_TC]
                    .rearrange("p (n t) -> p n t", n=2 * NST),
                    in_=sp_full.ap().rearrange("(o r) t -> o r t", o=1)
                    [0:1, 128:160, w0:w0 + SCAN_TC]
                    .broadcast_to([128, 2 * NST, SCAN_TC]))
            g.dma("g", ldbc, reads=[("dr", "sp_full")],
                  writes=[("bcrep", pp_tc)], label=f"ldbc{tc}")
            g.dma("sy", lambda e, w0=w0, HSO=HSO: e.dma_start(
                out=hsch[:, HSO:HSO + 4 * SCAN_TC].rearrange(
                    "p (db t) -> p db t", db=4),
                in_=hsb_d.ap().rearrange("(db p) t -> p db t", p=128)
                [:, :, w0:w0 + SCAN_TC]),
                reads=[("dr", "hsb", w0 // IP_SC)], writes=[("hsch", pp_tc)],
                label=f"ldhs{tc}")
            g.dma("sy", lambda e, w0=w0, HSO=HSO: e.dma_start(
                out=sgch[:, HSO:HSO + 4 * SCAN_TC].rearrange(
                    "p (db t) -> p db t", db=4),
                in_=sgb_d.ap().rearrange("(db p) t -> p db t", p=128)
                [:, :, w0:w0 + SCAN_TC]),
                reads=[("dr", "sgb", w0 // IP_SC)], writes=[("sgch", pp_tc)],
                label=f"ldsg{tc}")

        def stA(u):
            tc, db = divmod(u, NDB)
            pp_tc = tc % 2
            TSO = pp_tc * SCAN_TC
            pp = db % 2
            dps = psB[1] if pp == 0 else psB[0]
            if db == 0 and tc + 1 < N_TC:
                emit_ldts(tc + 1)
            if db == 0 and tc >= 1:
                emit_ldbig(tc)

            def dmm(e, db=db, dps=dps, TSO=TSO):
                return e.matmul(
                    dps[:, :SCAN_TC], dtw_sb[:, db * 128:(db + 1) * 128],
                    tsch[:, TSO:TSO + SCAN_TC], start=True, stop=True)
            g.op("pe", dmm, reads=[("tsch", pp_tc), ("w6", "dtw")],
                 writes=[("dtps", pp)], label=f"dmm{tc}_{db}")
            g.op("a", lambda e, db=db, pp=pp, dps=dps: e.activation(
                spa[:, pp * SCAN_TC:(pp + 1) * SCAN_TC],
                dps[:, :SCAN_TC], AF.Abs, bias=dtb_sb[:, db:db + 1]),
                reads=[("dtps", pp), ("w6", "dtb")], writes=[("spa", pp)],
                label=f"sp1{tc}_{db}")
            g.op("a", lambda e, pp=pp: e.activation(
                spa[:, pp * SCAN_TC:(pp + 1) * SCAN_TC],
                spa[:, pp * SCAN_TC:(pp + 1) * SCAN_TC],
                AF.Exp, scale=-1.0),
                reads=[("spa", pp)], writes=[("spa", pp)],
                label=f"sp2{tc}_{db}")
            g.op("a", lambda e, pp=pp: e.activation(
                spa[:, pp * SCAN_TC:(pp + 1) * SCAN_TC],
                spa[:, pp * SCAN_TC:(pp + 1) * SCAN_TC],
                AF.Ln, bias=1.0),
                reads=[("spa", pp)], writes=[("spa", pp)],
                label=f"sp3{tc}_{db}")
            g.op("a", lambda e, db=db, pp=pp, dps=dps: e.activation(
                spb[:, pp * SCAN_TC:(pp + 1) * SCAN_TC],
                dps[:, :SCAN_TC], AF.Relu, bias=dtb_sb[:, db:db + 1]),
                reads=[("dtps", pp), ("w6", "dtb")], writes=[("spb", pp)],
                label=f"sp4{tc}_{db}")
        def stA2(u):
            tc, db = divmod(u, NDB)
            pp = db % 2
            g.op("v", lambda e, db=db, pp=pp: e.tensor_tensor(
                dtch[:, db * SCAN_TC:(db + 1) * SCAN_TC],
                spa[:, pp * SCAN_TC:(pp + 1) * SCAN_TC],
                spb[:, pp * SCAN_TC:(pp + 1) * SCAN_TC], ALU.add),
                reads=[("spa", pp), ("spb", pp)],
                writes=[("dtch", db), ("dtps", pp)],
                label=f"sp5{tc}_{db}")

        def stB(u):
            tc, db = divmod(u, NDB)
            pp_tc = tc % 2
            HSO = pp_tc * 4 * SCAN_TC
            RPO = pp_tc * NST * SCAN_TC
            pp = db % 2

            def aex(e, db=db, pp=pp):
                last = None
                for n in range(NST):
                    last = e.activation(
                        ach[:, (pp * NST + n) * SCAN_TC:
                            (pp * NST + n + 1) * SCAN_TC],
                        dtch[:, db * SCAN_TC:(db + 1) * SCAN_TC],
                        AF.Exp,
                        scale=a_sb[:, db * NST + n:db * NST + n + 1])
                return last
            g.op("a", aex, reads=[("dtch", db), ("w6", "a")],
                 writes=[("ach", pp)], label=f"aex{tc}_{db}")

            par = u % 2

            def du(e, db=db, HSO=HSO, par=par):
                return e.tensor_tensor(
                    dtu[:, par * SCAN_TC:(par + 1) * SCAN_TC],
                    dtch[:, db * SCAN_TC:(db + 1) * SCAN_TC],
                    hsch[:, HSO + db * SCAN_TC:
                         HSO + (db + 1) * SCAN_TC], ALU.mult)
            g.op("v", du, reads=[("dtch", db), ("hsch", pp_tc)],
                 writes=[("dtu", par)], label=f"dtu{tc}_{db}")

            def bop(e, db=db, pp=pp, RPO=RPO, par=par):
                return e.tensor_tensor(
                    bch[:, pp * NST * SCAN_TC:(pp + 1) * NST * SCAN_TC]
                    .rearrange("p (n t) -> p n t", n=NST),
                    dtu[:, par * SCAN_TC:(par + 1) * SCAN_TC]
                    .rearrange("p (o t) -> p o t", o=1)
                    .broadcast_to([128, NST, SCAN_TC]),
                    bcrep[:, 2 * RPO:2 * RPO + NST * SCAN_TC]
                    .rearrange("p (n t) -> p n t", n=NST),
                    ALU.mult)
            g.op("v", bop, reads=[("dtu", par), ("bcrep", pp_tc)],
                 writes=[("bch", pp)], label=f"bop{tc}_{db}")

        def stC_scn(u):
            tc, db = divmod(u, NDB)
            pp = db % 2
            par = u % 2
            SCO = par * NST * SCAN_TC

            def scn(e, db=db, pp=pp, SCO=SCO):
                last = None
                for n in range(NST):
                    last = e.tensor_tensor_scan(
                        sch[:, SCO + n * SCAN_TC:SCO + (n + 1) * SCAN_TC],
                        ach[:, (pp * NST + n) * SCAN_TC:
                            (pp * NST + n + 1) * SCAN_TC],
                        bch[:, (pp * NST + n) * SCAN_TC:
                            (pp * NST + n + 1) * SCAN_TC],
                        state[:, db * NST + n:db * NST + n + 1],
                        ALU.mult, ALU.add)
                return last
            g.op("v", scn,
                 reads=[("ach", pp), ("bch", pp), ("st", db)],
                 writes=[("sch", par), ("ach", pp), ("bch", pp)],
                 label=f"scn{tc}_{db}")

        def stC_cmu(u):
            tc, db = divmod(u, NDB)
            par = u % 2
            SCO = par * NST * SCAN_TC
            pp_tc = tc % 2
            RPO = pp_tc * NST * SCAN_TC

            def cmu(e, RPO=RPO, SCO=SCO):
                return e.tensor_tensor(
                    cmch[:, :],
                    sch[:, SCO:SCO + NST * SCAN_TC],
                    bcrep[:, 2 * RPO + NST * SCAN_TC:
                          2 * RPO + 2 * NST * SCAN_TC],
                    ALU.mult)
            g.op("v", cmu, reads=[("sch", par), ("bcrep", pp_tc)],
                 writes=[("cmch",)], label=f"cmu{tc}_{db}")

        def stC2(u):
            tc, db = divmod(u, NDB)
            par = u % 2
            SCO = par * NST * SCAN_TC

            def stu(e, db=db, SCO=SCO):
                return e.activation(
                    state[:, db * NST:(db + 1) * NST],
                    sch[:, SCO:SCO + NST * SCAN_TC]
                    .rearrange("p (n t) -> p n t", n=NST)
                    [:, :, SCAN_TC - 1:SCAN_TC].rearrange(
                        "p n o -> p (n o)"), AF.Copy)
            g.op("a", stu, reads=[("sch", par)],
                 writes=[("st", db)], label=f"stu{tc}_{db}")

        def stD1(u):
            tc, db = divmod(u, NDB)
            par = u % 2

            def yrd(e, par=par):
                h = NST * SCAN_TC
                while h > 2 * SCAN_TC:
                    h //= 2
                    e.tensor_tensor(cmch[:, :h], cmch[:, :h],
                                    cmch[:, h:2 * h], ALU.add)
                return e.tensor_tensor(
                    ybuf[:, par * SCAN_TC:(par + 1) * SCAN_TC],
                    cmch[:, :SCAN_TC], cmch[:, SCAN_TC:2 * SCAN_TC],
                    ALU.add)
            g.op("v", yrd, reads=[("cmch",)],
                 writes=[("yb", par), ("cmch",)], label=f"yrd{tc}_{db}")

        def stD2(u):
            tc, db = divmod(u, NDB)
            pp_tc = tc % 2
            HSO = pp_tc * 4 * SCAN_TC
            par = u % 2
            ZO = pp_tc * 4 * SCAN_TC

            def z1a(e, db=db, HSO=HSO):
                return e.tensor_tensor(
                    zsc[:, :],
                    hsch[:, HSO + db * SCAN_TC:HSO + (db + 1) * SCAN_TC],
                    dd_sb[:, db:db + 1].broadcast_to([128, SCAN_TC]),
                    ALU.mult)
            g.op("v", z1a, reads=[("hsch", pp_tc), ("w6", "dd")],
                 writes=[("zsc",)], label=f"z1a{tc}_{db}")

            def z1b(e, par=par):
                return e.tensor_tensor(
                    zsc2[:, :], zsc[:, :],
                    ybuf[:, par * SCAN_TC:(par + 1) * SCAN_TC], ALU.add)
            g.op("v", z1b, reads=[("zsc",), ("yb", par)],
                 writes=[("zsc2",)], label=f"z1b{tc}_{db}")

            def z2(e, db=db, HSO=HSO, ZO=ZO):
                return e.tensor_tensor(
                    zb[:, ZO + db * SCAN_TC:ZO + (db + 1) * SCAN_TC],
                    zsc2[:, :],
                    sgch[:, HSO + db * SCAN_TC:HSO + (db + 1) * SCAN_TC],
                    ALU.mult)
            g.op("v", z2, reads=[("zsc2",), ("sgch", pp_tc)],
                 writes=[("zb", pp_tc, db)], label=f"z2_{tc}_{db}")
            if db == NDB - 1:
                emit_oproj(tc)

        def emit_oproj(tc):
            pp_tc = tc % 2
            ZO = pp_tc * 4 * SCAN_TC
            for t1 in range(SCAN_TC // 128):
                def omm(e, t1=t1, ZO=ZO):
                    last = None
                    for db in range(4):
                        for oc in range(4):
                            last = e.matmul(
                                psA[oc][:, :],
                                zb[:, ZO + db * SCAN_TC + t1 * 128:
                                   ZO + db * SCAN_TC + (t1 + 1) * 128],
                                ow_sb[:, db * D + oc * 512:
                                      db * D + oc * 512 + 512],
                                start=(db == 0), stop=(db == 3))
                    return last
                g.op("pe", omm,
                     reads=[("zb", pp_tc, db) for db in range(4)]
                     + [("w6", "ow")],
                     writes=[("psA", 0), ("psA", 1), ("psA", 2), ("psA", 3)],
                     label=f"omm{tc}_{t1}")
                for oc in range(4):
                    def oev(e, oc=oc):
                        return e.activation(
                            obuf[:, oc * 512:(oc + 1) * 512], psA[oc][:, :],
                            AF.Copy)
                    g.op("a", oev, reads=[("psA", oc)], writes=[("obuf", oc)],
                         label=f"oev{tc}_{t1}_{oc}")
                g.dma("sy", lambda e, tc=tc, t1=t1: e.dma_start(
                    out=rs_in.ap()[tc * SCAN_TC + t1 * 128:
                                   tc * SCAN_TC + (t1 + 1) * 128, :],
                    in_=obuf[:, :]),
                    reads=[("obuf", oc) for oc in range(4)],
                    writes=[("rsw", tc, t1)], label=f"ost{tc}_{t1}")

        emit_ldts(0)
        emit_ldbig(0)
        U = N_TC * NDB
        for r in range(U + 5):
            if 0 <= r - 2 < U:
                stB(r - 2)
            if 0 <= r - 4 < U:
                stC_cmu(r - 4)
            if 0 <= r - 4 < U:
                stD1(r - 4)
            if r < U:
                stA(r)
            if 0 <= r - 3 < U:
                stC_scn(r - 3)
            if 0 <= r - 3 < U:
                stC2(r - 3)
            if 0 <= r - 4 < U:
                stD2(r - 4)
            if 0 <= r - 1 < U:
                stA2(r - 1)
        g.cc(lambda e: e.collective_compute(
            "ReduceScatter", ALU.add, replica_groups=[CORE_IDS],
            ins=[rs_in.ap()], outs=[rs_out.ap()]),
            reads=[("rsw", tc, t1) for tc in range(N_TC)
                   for t1 in range(SCAN_TC // 128)],
            writes=[("dr", "rs_out")], label="rsop")
        if debug:
            g.dma("sy", lambda e: e.dma_start(
                out=dmp["d_rs"].ap(), in_=rs_out.ap()),
                reads=[("dr", "rs_out")], label="dump_rs")
        st6.close()
        g.fence()

        st7 = contextlib.ExitStack()
        x4buf = sb("x4buf", [128, NTT * D], st=st7)
        ob_sb = sb("ob_sb", [128, D], st=st7)
        x3t = sb("x3t", [128, D], st=st7)
        rst = sb("rst", [128, D], st=st7)
        g.dma("sy", lambda e: e.dma_start(out=ob_sb[:, :], in_=outb_rep.ap()),
              writes=[("ob",)], label="ldob")
        for tt in range(NTT):
            g.dma("sy", lambda e, tt=tt: e.dma_start(
                out=x3t[:, :],
                in_=x3_d.ap()[tt * 128:(tt + 1) * 128, :]),
                reads=[("dr", "x3_d")], writes=[("x3t",)],
                label=f"ldx3_{tt}")
            g.dma("sy", lambda e, tt=tt: e.dma_start(
                out=rst[:, :],
                in_=rs_out.ap()[tt * 128:(tt + 1) * 128, :]),
                reads=[("dr", "rs_out")], writes=[("rstt",)],
                label=f"ldrs_{tt}")

            def a1(e, tt=tt):
                return e.tensor_tensor(
                    x4buf[:, tt * D:(tt + 1) * D],
                    x3t[:, :], rst[:, :], ALU.add)
            g.op("v", a1, reads=[("x3t",), ("rstt",)],
                 writes=[("x4a", tt)], label=f"x4a{tt}")

            def a2(e, tt=tt):
                return e.tensor_tensor(
                    x4buf[:, tt * D:(tt + 1) * D],
                    x4buf[:, tt * D:(tt + 1) * D], ob_sb[:, :], ALU.add)
            g.op("v", a2, reads=[("x4a", tt), ("ob",)],
                 writes=[("row", "x4", tt)], label=f"x4b{tt}")
        build_ffn(x4buf, "x4", "x5", out_rows)
        st7.close()

        g.op("sy", lambda e: e.nop(),
             reads=[("outst", tt) for tt in range(NTT)], label="final")

        g.plan()
        build_program._last_g = g
        g.emit(nc, semmap)

    return nc


def _prep_inputs(inputs):
    x = np.asarray(inputs["L0_attn_a_x"])
    wqkv = np.asarray(inputs["L0_attn_a_Wqkv_w"])
    bqkv = np.asarray(inputs["L0_attn_a_Wqkv_b"])
    in_w = np.asarray(inputs["L0_attn_b_in_w"])
    in_b = np.asarray(inputs["L0_attn_b_in_b"])
    conv_w = np.asarray(inputs["L0_attn_b_conv_w"])
    conv_b = np.asarray(inputs["L0_attn_b_conv_b"])
    xproj_w = np.asarray(inputs["L0_attn_b_xproj_w"])
    dt_w = np.asarray(inputs["L0_attn_b_dt_w"])
    dt_b = np.asarray(inputs["L0_attn_b_dt_b"])
    a_log = np.asarray(inputs["L0_attn_b_A_log"])
    d_ssm = np.asarray(inputs["L0_attn_b_D"])
    out_w = np.asarray(inputs["L0_attn_b_out_w"])
    out_b = np.asarray(inputs["L0_attn_b_out_b"])
    w_base = np.asarray(inputs["L0_ffn_W_base"])
    b_base = np.asarray(inputs["L0_ffn_b_base"])
    router_w = np.asarray(inputs["L0_ffn_router_W"])
    a_lora = np.asarray(inputs["L0_ffn_A"])
    bw_lora = np.asarray(inputs["L0_ffn_Bw"])

    C = np.ascontiguousarray

    inv = (1.0 / 10000.0 ** (np.arange(0, 32, 2, dtype=np.float32) / 32.0))
    t = np.arange(T, dtype=np.float32)
    fr = np.outer(t, inv)
    cosg = np.tile(np.cos(fr).T.astype(np.float32), (8, 1))
    sing = np.tile(np.sin(fr).T.astype(np.float32), (8, 1))
    for base in (0, 64):
        for u in (0, 2):
            cosg[base + u * 16: base + (u + 1) * 16] *= np.float32(SCALE)
            sing[base + u * 16: base + (u + 1) * 16] *= np.float32(SCALE)
    idb = np.eye(128, dtype=np.float32).astype(BF16NP)
    idf = np.eye(128, dtype=np.float32)
    trm = np.triu(np.full((128, 128), -10000.0, np.float32), 1).astype(BF16NP)
    ones1 = np.ones((1, 128), np.float32)
    wbT = C(w_base.T)
    bbase_rep = np.tile(b_base[None, :], (128, 1)).astype(np.float32)
    routerT = C(router_w.T)
    aT = C(a_lora.reshape(128, D).T)
    bwT = C(np.transpose(bw_lora, (0, 2, 1)).reshape(128, D))
    outb_rep = np.tile(out_b[None, :], (128, 1)).astype(np.float32)

    maps = []
    for c in range(NCORES):
        m = {}
        m["x_rows"] = C(x[0, c * TPC:(c + 1) * TPC, :])
        m["cid_in"] = np.array([[c]], np.int32)
        h0, h1 = 2 * c, 2 * c + 1
        unit_list = [("q", h0), ("k", h0), ("q", h1), ("k", h1)]

        def urows(kind, hh):
            base = (0 if kind == "q" else D) + hh * 128
            return base
        rot_idx = []
        for part in range(2):
            for kind, hh in unit_list:
                b0 = urows(kind, hh) + part * 16
                rot_idx.extend(range(b0, b0 + 16))
        m["w_rot"] = C(wqkv[rot_idx, :].T)
        m["b_rot"] = C(bqkv[rot_idx].reshape(128, 1))
        wp = []
        bp = []
        for kind, hh in unit_list:
            b0 = urows(kind, hh)
            wp.append(wqkv[b0 + 32:b0 + 128, :])
            bp.append(bqkv[b0 + 32:b0 + 128])
        m["w_pass"] = C(np.concatenate(wp, axis=0).T)
        bps = np.stack(bp, axis=1)
        bps[:, 0] *= np.float32(SCALE)
        bps[:, 2] *= np.float32(SCALE)
        m["b_pass"] = C(bps)
        vrows = list(range(2 * D + h0 * 128, 2 * D + h0 * 128 + 128)) + \
            list(range(2 * D + h1 * 128, 2 * D + h1 * 128 + 128))
        m["w_v"] = C(wqkv[vrows, :].T)
        m["b_v"] = C(bqkv[vrows].reshape(2, 128).T)
        perm = np.zeros((128, 128), np.float32)
        for u in range(4):
            for j_ in range(16):
                perm[u * 16 + j_, u * 32 + j_] = 1.0
                perm[64 + u * 16 + j_, u * 32 + 16 + j_] = 1.0
        m["perm_all"] = perm.astype(BF16NP)
        m["cosg"] = cosg
        m["sing"] = sing
        m["wbT"] = wbT
        m["bbase_rep"] = bbase_rep
        m["routerT"] = routerT
        m["aT"] = aT
        m["bwT"] = bwT
        s = slice(c * DIPC, (c + 1) * DIPC)
        m["inwT_sl"] = C(np.concatenate(
            [in_w[c * DIPC:(c + 1) * DIPC, :],
             in_w[DI + c * DIPC:DI + (c + 1) * DIPC, :]], axis=0).T)
        m["inb_sl"] = C(np.concatenate(
            [in_b[c * DIPC:(c + 1) * DIPC],
             in_b[DI + c * DIPC:DI + (c + 1) * DIPC]]).reshape(8, 128).T)
        m["convw_sl"] = C(conv_w[s, 0, :].reshape(4, 128, 4).transpose(1, 0, 2).reshape(128, 16))
        m["convb_sl"] = C(conv_b[s].reshape(4, 128).T)
        m["xprojT_sl"] = C(xproj_w[:, s].T)
        m["dtwT_sl"] = C(dt_w[s, :].T)
        m["dtb_sl"] = C(dt_b[s].reshape(4, 128).T)
        m["alog_sl"] = C(a_log[s, :].reshape(4, 128, NST).transpose(1, 0, 2).reshape(128, 4 * NST))
        m["d_sl"] = C(d_ssm[s].reshape(4, 128).T)
        m["outwT_sl"] = C(out_w[:, s].T)
        m["outb_rep"] = outb_rep
        m["idb"] = idb
        m["idf"] = idf
        m["trm"] = trm
        m["ones1"] = ones1
        maps.append(m)
    return maps


_NC_CACHE = {}


def kernel(**inputs):
    import os
    debug = bool(int(os.environ.get("MK_DEBUG", "0")))
    maps = _prep_inputs(inputs)
    key = ("nc", debug)
    if key not in _NC_CACHE:
        _NC_CACHE[key] = build_program(debug=debug)
    nc = _NC_CACHE[key]
    res = run_bass_kernel_spmd(nc, maps, core_ids=CORE_IDS)
    if debug:
        kernel._dbg = res.results
    out = np.concatenate([res.results[c]["out_rows"] for c in range(NCORES)],
                         axis=0)
    return out.reshape(1, T, D).astype(np.float32)
